# version 07
# baseline (speedup 1.0000x reference)
import numpy as np
import ml_dtypes
from contextlib import ExitStack
import concourse.bass as bass
import concourse.mybir as mybir
from concourse.bass_utils import run_bass_kernel_spmd

F32 = mybir.dt.float32
BF16 = mybir.dt.bfloat16
I32 = mybir.dt.int32
AF = mybir.ActivationFunctionType
ALU = mybir.AluOpType
AX = mybir.AxisListType
BF16NP = ml_dtypes.bfloat16

NCORE = 8
NSEQ = 2
T = 2048
NTOK = NSEQ * T
D = 1024
NCH = D // 128
HID = 4096
NHID = HID // 128
QT = 1024
NQ = NTOK // QT
TT = 512


class Sem:
    def __init__(self, h):
        self.h = h


class Prog:
    def __init__(self, nc, ctx):
        self.nc = nc
        self.ctx = ctx
        self.cur = ctx
        self.sems = {}
        self.ln_tiles = 0

    def sem(self, name):
        if name not in self.sems:
            h = self.nc.alloc_semaphore(name=name)
            self.sems[name] = Sem(h)
            if self.cur is not self.ctx and not name.startswith("LN_"):
                self._phase_sems[-1].append(name)
        return self.sems[name].h

    def sb(self, name, shape, dt):
        return self.cur.enter_context(self.nc.sbuf_tensor(name, shape, dt))

    def ps(self, name, shape, dt=F32):
        return self.cur.enter_context(self.nc.psum_tensor(name, shape, dt))

    def push(self):
        if not hasattr(self, "_stack"):
            self._stack = []
            self._phase_sems = []
        self._stack.append(self.cur)
        self._phase_sems.append([])
        self.cur = ExitStack()
        return self.cur

    def pop(self):
        names = self._phase_sems.pop()
        if names:
            hs = [self.sems[n].h for n in names]
            self.nc.clear_and_free_semaphores(hs)
            for n in names:
                del self.sems[n]
        self.cur.close()
        self.cur = self._stack.pop()



def _fm(vec):
    n = vec.shape[0] // 128
    return np.ascontiguousarray(vec.reshape(n, 128).T)


def pack_l0_attn_w(inputs, proj):
    Wv = np.asarray(inputs[f"L0_attn_{proj}_Wv"], np.float32)
    Wr = np.asarray(inputs[f"L0_attn_{proj}_Wr"], np.float32)
    Wl = np.asarray(inputs[f"L0_attn_{proj}_Wl"], np.float32)
    blocks, meta = [], []
    for b in range(8):
        lst = []
        if b > 0:
            lst.append((b - 1, len(blocks))); blocks.append(Wr[b - 1])
        lst.append((b, len(blocks))); blocks.append(Wv[b])
        if b < 7:
            lst.append((b + 1, len(blocks))); blocks.append(Wl[b])
        meta.append(lst)
    return np.ascontiguousarray(np.concatenate(blocks, axis=0)), meta

L0A_META = [
    [(0, 0), (1, 1)],
    [(0, 2), (1, 3), (2, 4)],
    [(1, 5), (2, 6), (3, 7)],
    [(2, 8), (3, 9), (4, 10)],
    [(3, 11), (4, 12), (5, 13)],
    [(4, 14), (5, 15), (6, 16)],
    [(5, 17), (6, 18), (7, 19)],
    [(6, 20), (7, 21)],
]


def make_consts():
    Dset = [0, 1, 2, 3, 4, 8, 16, 32, 64, 128, 256, 512, 1024]
    p = np.arange(128)[:, None]
    f = np.arange(128)[None, :]
    m_self = np.isin(f - p, Dset).astype(np.float32)
    m_prev = np.isin(f - p + 128, Dset).astype(np.float32)
    m_diag = (f == p).astype(np.float32)
    tri = (p <= f).astype(np.float32)
    parts = [m_self, m_prev, m_diag, tri]
    f2 = np.arange(512)[None, :]
    for r in range(4):
        parts.append((p <= f2 - 128 * r).astype(np.float32))
    masks = np.concatenate(parts, axis=1).astype(BF16NP)
    ident = np.eye(128, dtype=BF16NP)
    return ident, masks


SM_COLS = {}

def pack_smalls(inputs):
    global SM_COLS
    cols = []
    SM_COLS = {}

    def add(name, arr2d):
        SM_COLS[name] = (len(cols), arr2d.shape[1])
        for j in range(arr2d.shape[1]):
            cols.append(np.asarray(arr2d[:, j], np.float32))

    add("l0f_s1", _fm(np.asarray(inputs["L0_ffn_w1_s"], np.float32)))
    add("l0f_s2", _fm(np.asarray(inputs["L0_ffn_w2_s"], np.float32)))
    add("l0f_sh1", _fm(np.asarray(inputs["L0_ffn_h1_s"], np.float32)))
    add("l0f_sh2", _fm(np.asarray(inputs["L0_ffn_h2_s"], np.float32)))
    add("l0f_s3", _fm(np.asarray(inputs["L0_ffn_w3_s"], np.float32)))
    for pj in ("q", "k", "v", "o"):
        add(f"l0a_{pj}b", _fm(np.asarray(inputs[f"L0_attn_{pj}_b"], np.float32)))
    add("l1a_cab", _fm(np.asarray(inputs["L1_attn_c_attn_b"], np.float32)))
    add("l1a_cpb", _fm(np.asarray(inputs["L1_attn_c_proj_b"], np.float32)))
    add("l1a_lnw", _fm(np.asarray(inputs["L1_attn_ln_local_w"], np.float32)))
    add("l1a_lnb", _fm(np.asarray(inputs["L1_attn_ln_local_b"], np.float32)))
    add("l1a_gb1", _fm(np.asarray(inputs["L1_attn_gate_b1"], np.float32)))
    add("l1a_glw", _fm(np.asarray(inputs["L1_attn_gate_ln_w"], np.float32)))
    add("l1a_glb", _fm(np.asarray(inputs["L1_attn_gate_ln_b"], np.float32)))
    cb = np.asarray(inputs["L1_attn_conv_b"], np.float32)
    add("l1a_cvb", np.concatenate([cb, cb])[:, None])
    cw = np.asarray(inputs["L1_attn_comp_ln_w"], np.float32)
    add("l1a_clw", np.concatenate([cw, cw])[:, None])
    clb = np.asarray(inputs["L1_attn_comp_ln_b"], np.float32)
    add("l1a_clb", np.concatenate([clb, clb])[:, None])
    gb2 = np.asarray(inputs["L1_attn_gate_b2"], np.float32)
    z = np.zeros((128, 2), np.float32)
    z[0, 0] = gb2[0]; z[0, 1] = gb2[1]
    add("l1a_gb2", z)
    return np.ascontiguousarray(np.stack(cols, axis=1))


def prep_weights(inputs):
    w = {}
    w["l0f_w1T"] = np.ascontiguousarray(np.asarray(inputs["L0_ffn_w1_w"], np.float32).T)
    w["l0f_w2T"] = np.ascontiguousarray(np.asarray(inputs["L0_ffn_w2_w"], np.float32).T)
    w["l0f_h1T"] = np.ascontiguousarray(np.asarray(inputs["L0_ffn_h1_w"], np.float32).T)
    w["l0f_h2T"] = np.ascontiguousarray(np.asarray(inputs["L0_ffn_h2_w"], np.float32).T)
    w["l0f_w3T"] = np.ascontiguousarray(np.asarray(inputs["L0_ffn_w3_w"], np.float32).T)
    for pj in ("q", "k", "v", "o"):
        w[f"l0a_{pj}w"], _ = pack_l0_attn_w(inputs, pj)
    w["l1a_caw"] = np.asarray(inputs["L1_attn_c_attn_w"], np.float32)
    w["l1a_cpw"] = np.asarray(inputs["L1_attn_c_proj_w"], np.float32)
    w["l1a_cvw"] = np.ascontiguousarray(
        np.asarray(inputs["L1_attn_conv_w"], np.float32).transpose(2, 1, 0).reshape(32 * 64, 64))
    w["l1a_gw1"] = np.asarray(inputs["L1_attn_gate_w1"], np.float32)
    w["l1a_gw2"] = np.asarray(inputs["L1_attn_gate_w2"], np.float32)
    w["l1m_wg"] = np.asarray(inputs["L1_ffn_Wg"], np.float32)
    w["l1m_w1"] = np.ascontiguousarray(np.asarray(inputs["L1_ffn_W1"], np.float32).reshape(8 * 1024, 2048))
    w["l1m_w2"] = np.ascontiguousarray(np.asarray(inputs["L1_ffn_W2"], np.float32).reshape(8 * 1024, 2048))
    w["l1m_w3"] = np.ascontiguousarray(np.asarray(inputs["L1_ffn_W3"], np.float32).reshape(8 * 2048, 1024))
    return w


def x_to_fm(x_core):
    n = np.asarray(x_core, np.float32).reshape(NTOK, NCH, 128)
    return np.ascontiguousarray(n.transpose(2, 1, 0).reshape(128, NCH * NTOK))


def fm_to_x(fm):
    n = fm.reshape(128, NCH, NTOK).transpose(2, 1, 0)
    return np.ascontiguousarray(n.reshape(NSEQ, T, D))



class Common:
    pass


def emit_setup(P, dr, n_smalls):
    nc = P.nc
    C = Common()
    C.smalls = P.sb("sb_smalls", [128, n_smalls], F32)
    C.ident = P.sb("sb_ident", [128, 128], BF16)
    C.masks = P.sb("sb_masks", [128, 2560], BF16)
    C.ones_col = P.sb("ones_col", [128, 1], BF16)
    C.ones_row = P.sb("ones_row", [1, 128], BF16)
    C.ones_rowF = P.sb("ones_rowF", [1, 128], F32)
    C.cc = P.sb("cc", [128, 12], F32)
    C.hs3 = P.sb("hs3", [128, 8], F32)
    s = P.sem("setup")
    with nc.Block() as block:
        @block.sync
        def _(sync):
            sync.dma_start(C.smalls[:, :], dr["smalls"]).then_inc(s, 16)
            sync.dma_start(C.ident[:, :], dr["ident"]).then_inc(s, 16)
            sync.dma_start(C.masks[:, :], dr["masks"]).then_inc(s, 16)

        @block.vector
        def _(vector):
            vector.memset(C.ones_col[:, :], 1.0)
            vector.memset(C.ones_row[:, :], 1.0)
            vector.memset(C.ones_rowF[:, :], 1.0)
            vector.memset(C.cc[:, 0:1], 1.0 / 1024.0)
            vector.memset(C.cc[:, 1:2], 1e-5)
            vector.memset(C.cc[:, 2:3], -0.5)
            vector.memset(C.cc[:, 3:4], 0.5)
            vector.memset(C.cc[:, 4:5], 1.0 / 64.0)
            vector.memset(C.cc[:, 5:6], 0.125)
            vector.memset(C.cc[:, 6:7], -1.0)
            vector.memset(C.cc[:, 7:8], 1.0)
            vector.memset(C.cc[:, 8:9], -1e30).then_inc(s, 1)
            vector.wait_ge(s, 49)
            vector.tensor_scalar(C.hs3[:, :], C.smalls[:, SM_COLS["l0f_s3"][0]:SM_COLS["l0f_s3"][0] + 8],
                                 C.cc[:, 3:4], None, ALU.mult)
    return C


def sm(C, name, j0=0, nj=1):
    c0, n = SM_COLS[name]
    return C.smalls[:, c0 + j0: c0 + j0 + nj]



def emit_cast(P, pairs, tag="cast"):
    nc = P.nc
    P.push()
    CW = 2048
    NB = 4
    buff = [P.sb(f"{tag}_f{i}", [128, CW], F32) for i in range(NB)]
    bufb = [P.sb(f"{tag}_b{i}", [128, CW], BF16) for i in range(NB)]
    s_in = [P.sem(f"{tag}_in{i}") for i in range(NB)]
    s_cv = P.sem(f"{tag}_cv")
    s_out = [P.sem(f"{tag}_out{i}") for i in range(NB)]

    tiles = []
    for src, dst in pairs:
        rows, cols = src.tensor.shape
        assert rows % 128 == 0, (rows, cols)
        for r0 in range(0, rows, 128):
            for c0 in range(0, cols, CW):
                cw = min(CW, cols - c0)
                tiles.append((src, dst, r0, c0, cw))

    with nc.Block() as block:
        def loads(eng, par):
            for i, (src, dst, r0, c0, cw) in enumerate(tiles):
                if i % 2 != par:
                    continue
                if i >= NB:
                    eng.wait_ge(s_cv, i - NB + 1)
                eng.dma_start(buff[i % NB][:, 0:cw],
                              src[r0:r0 + 128, c0:c0 + cw]).then_inc(s_in[i % NB], 16)

        @block.sync
        def _(sync):
            loads(sync, 0)

        @block.scalar
        def _(scalar):
            loads(scalar, 1)

        @block.vector
        def _(vector):
            for i, (src, dst, r0, c0, cw) in enumerate(tiles):
                vector.wait_ge(s_in[i % NB], 16 * (i // NB + 1))
                if i >= NB:
                    vector.wait_ge(s_out[i % NB], 16 * (i // NB))
                vector.tensor_copy(bufb[i % NB][:, 0:cw], buff[i % NB][:, 0:cw]).then_inc(s_cv, 1)

        @block.gpsimd
        def _(gpsimd):
            for i, (src, dst, r0, c0, cw) in enumerate(tiles):
                gpsimd.wait_ge(s_cv, i + 1)
                gpsimd.dma_start(dst[r0:r0 + 128, c0:c0 + cw],
                                 bufb[i % NB][:, 0:cw]).then_inc(s_out[i % NB], 16)
            n = len(tiles)
            for b in range(NB):
                nb_ = (n - b + NB - 1) // NB
                if nb_ > 0:
                    gpsimd.wait_ge(s_out[b], 16 * nb_)
    P.pop()




class LNEmit:

    def __init__(self, P, C, tag, R_in, ntiles, rcol, xn_ap, ps3=None):
        self.P, self.C, self.tag = P, C, tag
        self.R_in, self.ntiles, self.rcol, self.xn_ap = R_in, ntiles, rcol, xn_ap
        self.rbuf = P.sb(f"{tag}_lnr", [128, NCH * TT], F32)
        self.atmp = P.sb(f"{tag}_lna", [128, 4 * TT], F32)
        self.xbb = [P.sb(f"{tag}_lnx{i}", [128, TT], BF16) for i in range(2)]
        self.sqb = P.sb(f"{tag}_lnq", [128, 2 * TT], BF16)
        self.brow = P.sb(f"{tag}_lnb", [1, 1024], BF16)
        if ps3 is not None and len(ps3) == 4:
            a_, b_, lo_, hi_ = ps3
            self.ps_s = a_[0:1, 0:TT]
            self.ps_q = b_[0:1, 0:TT]
            self.ps_b_lo = lo_[:, 0:TT]
            self.ps_b_hi = hi_[:, 0:TT]
        elif ps3 is not None:
            a_, b_, c_ = ps3
            self.ps_s = a_[0:1, 0:TT]
            self.ps_q = b_[0:1, 0:TT]
            self.ps_b_lo = c_[:, 0:TT]
            self.ps_b_hi = c_[:, TT:2 * TT]
        else:
            self.ps_s = P.ps(f"{tag}_lnps", [1, TT])[:, :]
            self.ps_q = P.ps(f"{tag}_lnpq", [1, TT])[:, :]
            pb = P.ps(f"{tag}_lnpb", [128, 1024])
            self.ps_b_lo = pb[:, 0:TT]
            self.ps_b_hi = pb[:, TT:2 * TT]
        self.sR = P.sem("LN_R"); self.sXB = P.sem("LN_XB")
        self.sSQ = P.sem("LN_SQ"); self.sAs = P.sem("LN_As")
        self.sAq = P.sem("LN_Aq"); self.sST = P.sem("LN_ST")
        self.sRS = P.sem("LN_RS"); self.sBR = P.sem("LN_BR")
        self.sBC = P.sem("LN_BC"); self.sXN = P.sem("LN_XN")
        self.sDV = P.sem("LN_DV"); self.sDV2 = P.sem("LN_DV2")
        self.sDV3 = P.sem("LN_DV3"); self.sAC = P.sem("LN_AC")
        self.a0 = P.ln_tiles
        P.ln_tiles += ntiles

    def gpsimd(self, g, a):
        aa = self.a0 + a
        if aa >= 1:
            g.wait_ge(self.sXN, aa)
        for c in range(NCH):
            g.dma_start(self.rbuf[:, c * TT:(c + 1) * TT],
                        self.R_in[:, self.rcol(a, c):self.rcol(a, c) + TT]).then_inc(self.sR, 16)

    def pe(self, t, a, extra_wait=None):
        C = self.C
        if extra_wait is not None:
            extra_wait(t, a)
        aa = self.a0 + a
        for c in range(NCH):
            idx = aa * 8 + c
            t.wait_ge(self.sXB, idx + 1)
            if c == 0:
                t.wait_ge(self.sST, aa)
            t.matmul(self.ps_s, C.ones_col[:, :], self.xbb[c % 2][:, :],
                     start=(c == 0), stop=(c == 7)).then_inc(self.sAs, 1)
        for c in range(NCH):
            idx = aa * 8 + c
            t.wait_ge(self.sSQ, idx + 1)
            if c == 0:
                t.wait_ge(self.sST, aa)
            t.matmul(self.ps_q, C.ones_col[:, :], self.sqb[:, (c % 2) * TT:(c % 2) * TT + TT],
                     start=(c == 0), stop=(c == 7)).then_inc(self.sAq, 1)
        t.wait_ge(self.sBR, aa + 1)
        t.wait_ge(self.sXN, aa)
        t.matmul(self.ps_b_lo, C.ones_row[:, :], self.brow[:, 0:TT],
                 start=True, stop=True)
        t.matmul(self.ps_b_hi, C.ones_row[:, :], self.brow[:, TT:2 * TT],
                 start=True, stop=True).then_inc(self.sBC, 1)

    def act(self, s, a):
        C = self.C
        aa = self.a0 + a
        s.wait_ge(self.sR, 16 * 8 * (aa + 1))
        for c in range(NCH):
            idx = aa * 8 + c
            if idx >= 2:
                s.wait_ge(self.sAq, idx - 1)
            s.activation(self.sqb[:, (c % 2) * TT:(c % 2) * TT + TT],
                         self.rbuf[:, c * TT:(c + 1) * TT],
                         AF.Square).then_inc(self.sSQ, 1)
        s.wait_ge(self.sST, aa + 1)
        s.activation(self.atmp[0:1, 1 * TT:2 * TT], self.atmp[0:1, 1 * TT:2 * TT],
                     AF.Ln, bias=C.cc[0:1, 1:2]).then_inc(self.sAC, 1)
        s.wait_ge(self.sAC, aa + 1)
        s.activation(self.atmp[0:1, 2 * TT:3 * TT], self.atmp[0:1, 1 * TT:2 * TT],
                     AF.Exp, scale=C.cc[0:1, 2:3]).then_inc(self.sRS, 1)

    def dve(self, v, a, extra_wait=None):
        C = self.C
        aa = self.a0 + a
        v.wait_ge(self.sR, 16 * 8 * (aa + 1))
        for c in range(NCH):
            idx = aa * 8 + c
            if idx >= 2:
                v.wait_ge(self.sAs, idx - 1)
            v.tensor_copy(self.xbb[c % 2][:, :], self.rbuf[:, c * TT:(c + 1) * TT]).then_inc(self.sXB, 1)
        v.wait_ge(self.sAs, (aa + 1) * 8)
        v.wait_ge(self.sAq, (aa + 1) * 8)
        if aa >= 1:
            v.wait_ge(self.sXN, aa)
        v.tensor_scalar(self.atmp[0:1, 0:TT], self.ps_s, C.cc[0:1, 0:1], None,
                        ALU.mult).then_inc(self.sDV, 1)
        v.tensor_scalar(self.atmp[0:1, TT:2 * TT], self.ps_q, C.cc[0:1, 0:1], None, ALU.mult)
        v.wait_ge(self.sDV, 3 * aa + 1)
        v.tensor_tensor(self.atmp[0:1, 2 * TT:3 * TT], self.atmp[0:1, 0:TT], self.atmp[0:1, 0:TT],
                        ALU.mult).then_inc(self.sDV, 1)
        v.wait_ge(self.sDV, 3 * aa + 2)
        v.tensor_tensor(self.atmp[0:1, 1 * TT:2 * TT], self.atmp[0:1, TT:2 * TT],
                        self.atmp[0:1, 2 * TT:3 * TT], ALU.subtract).then_inc(self.sST, 1)
        v.wait_ge(self.sRS, aa + 1)
        if aa >= 1:
            v.wait_ge(self.sBC, aa)
        v.tensor_tensor(self.atmp[0:1, 3 * TT:4 * TT], self.atmp[0:1, 0:TT],
                        self.atmp[0:1, 2 * TT:3 * TT], ALU.mult).then_inc(self.sDV, 1)
        v.wait_ge(self.sDV, 3 * aa + 3)
        v.tensor_copy(self.brow[:, 0:TT], self.atmp[0:1, 2 * TT:3 * TT])
        v.tensor_copy(self.brow[:, TT:2 * TT], self.atmp[0:1, 3 * TT:4 * TT]).then_inc(self.sBR, 1)
        v.wait_ge(self.sBC, aa + 1)
        if extra_wait is not None:
            extra_wait(v, a)
        for c in range(NCH):
            if c >= 2:
                v.wait_ge(self.sDV3, 8 * aa + c - 1)
            v.tensor_tensor(self.atmp[:, (c % 2) * TT:(c % 2) * TT + TT],
                            self.rbuf[:, c * TT:(c + 1) * TT],
                            self.ps_b_lo, ALU.mult).then_inc(self.sDV2, 1)
            if c >= 1:
                v.wait_ge(self.sDV2, 8 * aa + c)
                cc_ = c - 1
                v.tensor_tensor(self.xn_ap(a, cc_), self.atmp[:, (cc_ % 2) * TT:(cc_ % 2) * TT + TT],
                                self.ps_b_hi, ALU.subtract).then_inc(self.sDV3, 1)
        v.wait_ge(self.sDV2, 8 * aa + 8)
        cc_ = NCH - 1
        v.tensor_tensor(self.xn_ap(a, cc_), self.atmp[:, (cc_ % 2) * TT:(cc_ % 2) * TT + TT],
                        self.ps_b_hi, ALU.subtract).then_inc(self.sDV3, 1)
        v.engine_nop().then_inc(self.sXN, 1)


def emit_l0ffn(P, C, dr, tag):
    nc = P.nc
    P.push()
    R_in, R_out = dr["R_in"], dr["R_out"]
    w1T, w2T, h1T, h2T, w3T = dr["bw1T"], dr["bw2T"], dr["bh1T"], dr["bh2T"], dr["bw3T"]

    h_a = P.sb(f"{tag}_ha", [128, NHID * QT], BF16)
    h_b = P.sb(f"{tag}_hb", [128, NHID * QT], BF16)
    wbuf = [P.sb(f"{tag}_w{i}", [128, 8192], BF16) for i in range(2)]
    rbufE = [P.sb(f"{tag}_re{i}", [128, TT], F32) for i in range(2)]
    tmp = [P.sb(f"{tag}_tmp{i}", [128, TT], F32) for i in range(2)]

    psA = [P.ps(f"{tag}_psA{i}", [128, TT]) for i in range(2)]
    psB = [P.ps(f"{tag}_psB{i}", [128, TT]) for i in range(2)]

    def bcol(chunk, u):
        return chunk * QT + u * TT

    def rcol_a(a, c):
        return c * NTOK + (a // 2) * QT + (a % 2) * TT

    def rcol(chunk, q, u):
        return chunk * NTOK + q * QT + u * TT

    L = LNEmit(P, C, tag, R_in, 2 * NQ, rcol_a,
               lambda a, c: h_b[:, bcol(c, a % 2):bcol(c, a % 2) + TT])

    sMA = P.sem(f"{tag}_MA"); sMB = P.sem(f"{tag}_MB"); sSil = P.sem(f"{tag}_Sil")
    sEvB = P.sem(f"{tag}_EvB")
    sMC = P.sem(f"{tag}_MC"); sCDc = P.sem(f"{tag}_CDc")
    sMD = P.sem(f"{tag}_MD"); sCDd = P.sem(f"{tag}_CDd")
    sME = P.sem(f"{tag}_ME"); sRE = [P.sem(f"{tag}_RE0"), P.sem(f"{tag}_RE1")]
    sE = P.sem(f"{tag}_E")
    sSO = [P.sem(f"{tag}_SO0"), P.sem(f"{tag}_SO1")]
    sW = [P.sem(f"{tag}_W0"), P.sem(f"{tag}_W1")]

    wgroups = []
    for q in range(NQ):
        for mg in range(8):
            dmas = [(w1T, k * 128, mg * 512, 512, k * 512) for k in range(8)]
            dmas += [(w2T, k * 128, mg * 512, 512, 4096 + k * 512) for k in range(8)]
            wgroups.append(("B", q, mg, dmas))
        for mg2 in range(16):
            wgroups.append(("C", q, mg2, [(h1T, k * 128, mg2 * 256, 256, k * 256) for k in range(32)]))
        for mg2 in range(16):
            wgroups.append(("D", q, mg2, [(h2T, k * 128, mg2 * 256, 256, k * 256) for k in range(32)]))
        for mgE in range(4):
            wgroups.append(("E", q, mgE, [(w3T, k * 128, mgE * 256, 256, k * 256) for k in range(32)]))
    Wdone, gidx = [], {}
    ptot = [0, 0]
    for gg, (kind, q, idx, dmas) in enumerate(wgroups):
        ptot[gg % 2] += 16 * len(dmas)
        Wdone.append(ptot[gg % 2])
        gidx[(kind, q, idx)] = gg

    def eidx(i):
        mgE, m, u = i // 4, (i // 2) % 2, i % 2
        return mgE, m, u, mgE * 2 + m

    with nc.Block() as block:
        def gdone(kind, q, idx):
            if kind == "B":
                return (sMB, q * 64 + idx * 8 + 8)
            if kind == "C":
                return (sMC, q * 64 + idx * 4 + 4)
            if kind == "D":
                return (sMD, q * 64 + idx * 4 + 4)
            return (sME, q * 16 + idx * 4 + 4)

        @block.sync
        def _(sync):
            for gg, (kind, q, idx, dmas) in enumerate(wgroups):
                if gg >= 2:
                    pk, pq, pi, _ = wgroups[gg - 2]
                    dsem, dval = gdone(pk, pq, pi)
                    sync.wait_ge(dsem, dval)
                for (dram, r0, c0, cw, dc) in dmas:
                    sync.dma_start(wbuf[gg % 2][:, dc:dc + cw],
                                   dram[r0:r0 + 128, c0:c0 + cw]).then_inc(sW[gg % 2], 16)

        @block.gpsimd
        def _(g):
            for q in range(NQ):
                for u in range(2):
                    L.gpsimd(g, q * 2 + u)
                for i in range(16):
                    mgE, m, u, dchunk = eidx(i)
                    ie = q * 16 + i
                    if ie >= 2:
                        g.wait_ge(sE, ie - 1)
                    g.dma_start(rbufE[ie % 2][:, :],
                                R_in[:, rcol(dchunk, q, u):rcol(dchunk, q, u) + TT]).then_inc(sRE[ie % 2], 16)
                    if ie >= 1:
                        pe = ie - 1
                        pq, pi = pe // 16, pe % 16
                        _, _, pu, pd = eidx(pi)
                        g.wait_ge(sE, pe + 1)
                        g.dma_start(R_out[:, rcol(pd, pq, pu):rcol(pd, pq, pu) + TT],
                                    tmp[pe % 2][:, :]).then_inc(sSO[pe % 2], 16)
            ie = NQ * 16 - 1
            _, _, lu, ld = eidx(ie % 16)
            g.wait_ge(sE, ie + 1)
            g.dma_start(R_out[:, rcol(ld, NQ - 1, lu):rcol(ld, NQ - 1, lu) + TT],
                        tmp[ie % 2][:, :]).then_inc(sSO[ie % 2], 16)
            g.wait_ge(sSO[0], 16 * (NQ * 16 // 2))
            g.wait_ge(sSO[1], 16 * (NQ * 16 // 2))

        @block.tensor
        def _(t):
            for q in range(NQ):
                for u in range(2):
                    L.pe(t, q * 2 + u)
                for mg in range(8):
                    gg = gidx[("B", q, mg)]
                    for m in range(4):
                        for u in range(2):
                            ib = q * 64 + mg * 8 + m * 2 + u
                            if m == 0 and u == 0:
                                t.wait_ge(sW[gg % 2], Wdone[gg])
                            if mg == 0 and m == 0 and u == 0:
                                t.wait_ge(L.sXN, L.a0 + 2 * q + 2)
                                if q >= 1:
                                    t.wait_ge(sE, 16 * q)
                                    t.wait_ge(sCDd, 64 * q)
                            if ib >= 2:
                                t.wait_ge(sSil, ib - 1)
                                t.wait_ge(sEvB, ib - 1)
                            for k in range(8):
                                ins = t.matmul(psA[ib % 2][:, :],
                                               wbuf[gg % 2][:, k * 512 + m * 128:k * 512 + m * 128 + 128],
                                               h_b[:, bcol(k, u):bcol(k, u) + TT],
                                               start=(k == 0), stop=(k == 7))
                                if k == 7:
                                    ins.then_inc(sMA, 1)
                            for k in range(8):
                                ins = t.matmul(psB[ib % 2][:, :],
                                               wbuf[gg % 2][:, 4096 + k * 512 + m * 128:4096 + k * 512 + m * 128 + 128],
                                               h_b[:, bcol(k, u):bcol(k, u) + TT],
                                               start=(k == 0), stop=(k == 7))
                                if k == 7:
                                    ins.then_inc(sMB, 1)
                for mg2 in range(16):
                    gg = gidx[("C", q, mg2)]
                    for m in range(2):
                        for u in range(2):
                            ic = q * 64 + mg2 * 4 + m * 2 + u
                            if m == 0 and u == 0:
                                t.wait_ge(sW[gg % 2], Wdone[gg])
                            if mg2 == 0 and m == 0 and u == 0:
                                t.wait_ge(sEvB, 64 * (q + 1))
                                t.wait_ge(sSil, 64 * (q + 1))
                            if ic >= 2:
                                t.wait_ge(sCDc, ic - 1)
                            for k in range(32):
                                ins = t.matmul(psA[ic % 2][:, :],
                                               wbuf[gg % 2][:, k * 256 + m * 128:k * 256 + m * 128 + 128],
                                               h_a[:, bcol(k, u):bcol(k, u) + TT],
                                               start=(k == 0), stop=(k == 31))
                                if k == 31:
                                    ins.then_inc(sMC, 1)
                for mg2 in range(16):
                    gg = gidx[("D", q, mg2)]
                    for m in range(2):
                        for u in range(2):
                            idd = q * 64 + mg2 * 4 + m * 2 + u
                            if m == 0 and u == 0:
                                t.wait_ge(sW[gg % 2], Wdone[gg])
                            if mg2 == 0 and m == 0 and u == 0:
                                t.wait_ge(sCDc, 64 * (q + 1))
                            if idd >= 2:
                                t.wait_ge(sCDd, idd - 1)
                            for k in range(32):
                                ins = t.matmul(psA[idd % 2][:, :],
                                               wbuf[gg % 2][:, k * 256 + m * 128:k * 256 + m * 128 + 128],
                                               h_b[:, bcol(k, u):bcol(k, u) + TT],
                                               start=(k == 0), stop=(k == 31))
                                if k == 31:
                                    ins.then_inc(sMD, 1)
                for mgE in range(4):
                    gg = gidx[("E", q, mgE)]
                    for m in range(2):
                        for u in range(2):
                            ie = q * 16 + mgE * 4 + m * 2 + u
                            if m == 0 and u == 0:
                                t.wait_ge(sW[gg % 2], Wdone[gg])
                            if mgE == 0 and m == 0 and u == 0:
                                t.wait_ge(sCDd, 64 * (q + 1))
                            if ie >= 2:
                                t.wait_ge(sE, ie - 1)
                            for k in range(32):
                                ins = t.matmul(psB[ie % 2][:, :],
                                               wbuf[gg % 2][:, k * 256 + m * 128:k * 256 + m * 128 + 128],
                                               h_a[:, bcol(k, u):bcol(k, u) + TT],
                                               start=(k == 0), stop=(k == 31))
                                if k == 31:
                                    ins.then_inc(sME, 1)

        @block.scalar
        def _(s):
            for q in range(NQ):
                for u in range(2):
                    L.act(s, q * 2 + u)
                for mg in range(8):
                    for m in range(4):
                        for u in range(2):
                            ib = q * 64 + mg * 8 + m * 2 + u
                            mglob = mg * 4 + m
                            s.wait_ge(sMA, ib + 1)
                            if mg == 0 and m == 0 and u == 0 and q >= 1:
                                s.wait_ge(sSO[0], 128 * q)
                                s.wait_ge(sSO[1], 128 * q)
                            if ib >= 2:
                                s.wait_ge(sEvB, ib - 1)
                            s.activation(tmp[ib % 2][:, :], psA[ib % 2][:, :], AF.Silu,
                                         scale=sm(C, "l0f_s1", mglob)).then_inc(sSil, 1)
                for mg2 in range(16):
                    for m in range(2):
                        for u in range(2):
                            ic = q * 64 + mg2 * 4 + m * 2 + u
                            mglob = mg2 * 2 + m
                            s.wait_ge(sMC, ic + 1)
                            s.activation(h_b[:, bcol(mglob, u):bcol(mglob, u) + TT],
                                         psA[ic % 2][:, :], AF.Gelu,
                                         scale=sm(C, "l0f_sh1", mglob)).then_inc(sCDc, 1)
                for mg2 in range(16):
                    for m in range(2):
                        for u in range(2):
                            idd = q * 64 + mg2 * 4 + m * 2 + u
                            mglob = mg2 * 2 + m
                            s.wait_ge(sMD, idd + 1)
                            s.activation(h_a[:, bcol(mglob, u):bcol(mglob, u) + TT],
                                         psA[idd % 2][:, :], AF.Gelu,
                                         scale=sm(C, "l0f_sh2", mglob)).then_inc(sCDd, 1)

        @block.vector
        def _(v):
            def extraw(vv, a):
                if a % 2 == 0 and a >= 2:
                    vv.wait_ge(sMD, 64 * (a // 2))
            for q in range(NQ):
                for u in range(2):
                    L.dve(v, q * 2 + u, extra_wait=extraw)
                for mg in range(8):
                    for m in range(4):
                        for u in range(2):
                            ib = q * 64 + mg * 8 + m * 2 + u
                            mglob = mg * 4 + m
                            v.wait_ge(sMB, ib + 1)
                            v.wait_ge(sSil, ib + 1)
                            if mg == 0 and m == 0 and u == 0 and q >= 1:
                                v.wait_ge(sME, 16 * q)
                            v.scalar_tensor_tensor(h_a[:, bcol(mglob, u):bcol(mglob, u) + TT],
                                                   psB[ib % 2][:, :], sm(C, "l0f_s2", mglob),
                                                   tmp[ib % 2][:, :], ALU.mult,
                                                   ALU.mult).then_inc(sEvB, 1)
                for i in range(16):
                    mgE, m, u, dchunk = eidx(i)
                    ie = q * 16 + i
                    v.wait_ge(sME, ie + 1)
                    v.wait_ge(sRE[ie % 2], 16 * (ie // 2 + 1))
                    if ie >= 2:
                        v.wait_ge(sSO[ie % 2], 16 * (ie // 2))
                    v.scalar_tensor_tensor(tmp[ie % 2][:, :], psB[ie % 2][:, :],
                                           C.hs3[:, dchunk:dchunk + 1], rbufE[ie % 2][:, :],
                                           ALU.mult, ALU.add).then_inc(sE, 1)
    P.pop()

MOE_NE = 8
MOE_ROUTING = True
MOE_RT_LEVEL = 3

def emit_moe_dense(P, C, dr, tag):
    nc = P.nc
    P.push()
    R_in, R_out = dr["R_in"], dr["R_out"]
    w1d, w2d, w3d, wgd = dr["bw1"], dr["bw2"], dr["bw3"], dr["bwg"]
    identF, SEL = dr["identF"], dr["SEL"]

    MH = 16
    xn = P.sb(f"{tag}_xn", [128, NCH * QT], BF16)
    hb = P.sb(f"{tag}_hb", [128, MH * QT], BF16)
    wbuf = [P.sb(f"{tag}_w{i}", [128, 8192], BF16) for i in range(2)]
    ACC = P.sb(f"{tag}_acc", [128, NCH * QT], F32)
    wgb = P.sb(f"{tag}_wgb", [128, NCH * 8], BF16)
    idf = P.sb(f"{tag}_idf", [128, 128], F32)
    sel = P.sb(f"{tag}_sel", [8, 8 * 128], F32)
    lgbuf = P.sb(f"{tag}_lg", [128, 64], F32)
    rtmp = P.sb(f"{tag}_rt", [128, 24], F32)
    wtile = P.sb(f"{tag}_wt", [128, 64], F32)
    wsb = P.sb(f"{tag}_wsb", [8, 1024], F32)
    wbsb = P.sb(f"{tag}_wbs", [128, 2 * TT], F32)
    ytmp = [P.sb(f"{tag}_yt{i}", [128, TT], F32) for i in range(2)]
    tmp = [P.sb(f"{tag}_tmp{i}", [128, TT], F32) for i in range(2)]
    silu = [P.sb(f"{tag}_sl{i}", [128, TT], F32) for i in range(2)]
    rbufE = [P.sb(f"{tag}_re{i}", [128, TT], F32) for i in range(2)]

    psA = [P.ps(f"{tag}_psA{i}", [128, TT]) for i in range(2)]
    psB = [P.ps(f"{tag}_psB{i}", [128, TT]) for i in range(2)]

    def bcol(chunk, u):
        return chunk * QT + u * TT

    def rcol_a(a, c):
        return c * NTOK + (a // 2) * QT + (a % 2) * TT

    def rcol(chunk, q, u):
        return chunk * NTOK + q * QT + u * TT

    psLs = P.ps(f"{tag}_psLs", [1, TT])
    psLq = P.ps(f"{tag}_psLq", [1, TT])
    pbY = P.ps(f"{tag}_pbY", [128, 1024])
    pbY_lo, pbY_hi = pbY[:, 0:TT], pbY[:, TT:2 * TT]
    L = LNEmit(P, C, tag, R_in, 2 * NQ, rcol_a,
               lambda a, c: xn[:, bcol(c, a % 2):bcol(c, a % 2) + TT],
               ps3=(psLs, psLq, psA[0], psA[1]))

    sW = [P.sem(f"{tag}_W0"), P.sem(f"{tag}_W1")]
    sLG = P.sem(f"{tag}_LG"); sRT = P.sem(f"{tag}_RT"); sRD = P.sem(f"{tag}_RD")
    sSG = P.sem(f"{tag}_SG"); sWT = P.sem(f"{tag}_WT"); sMT = P.sem(f"{tag}_MT")
    sMLG = P.sem(f"{tag}_MLG")
    sWBM = P.sem(f"{tag}_WBM"); sWB = P.sem(f"{tag}_WB")
    sMA = P.sem(f"{tag}_MA"); sMB = P.sem(f"{tag}_MB"); sSil = P.sem(f"{tag}_Sil")
    sEvB = P.sem(f"{tag}_EvB")
    sMY = P.sem(f"{tag}_MY"); sYW = P.sem(f"{tag}_YW"); sYA = P.sem(f"{tag}_YA")
    sMS = P.sem(f"{tag}_MS"); sRU = P.sem(f"{tag}_RU")
    sRE = [P.sem(f"{tag}_RE0"), P.sem(f"{tag}_RE1")]
    sSO = [P.sem(f"{tag}_SO0"), P.sem(f"{tag}_SO1")]
    sCst = P.sem(f"{tag}_Cst")

    wgroups = []
    for q in range(NQ):
        for e in range(MOE_NE):
            for mg in range(4):
                dmas = [(w1d, 1024 * e + k * 128, mg * 512, 512, k * 512) for k in range(8)]
                dmas += [(w2d, 1024 * e + k * 128, mg * 512, 512, 4096 + k * 512) for k in range(8)]
                wgroups.append(("B", q, e, mg, dmas))
            for mgE in range(2):
                dmas = [(w3d, 2048 * e + k * 128, mgE * 512, 512, k * 512) for k in range(16)]
                wgroups.append(("E", q, e, mgE, dmas))
    Wdone, gidx = [], {}
    ptot = [0, 0]
    for gg, (kind, q, e, idx, dmas) in enumerate(wgroups):
        ptot[gg % 2] += 16 * len(dmas)
        Wdone.append(ptot[gg % 2])
        gidx[(kind, q, e, idx)] = gg

    def ib_of(q, e, mg, m, u):
        return (q * 8 + e) * 32 + mg * 8 + m * 2 + u

    def iy_of(q, e, j):
        return (q * 8 + e) * 16 + j

    with nc.Block() as block:
        def gdone(kind, q, e, idx):
            if kind == "B":
                return (sMB, ib_of(q, e, idx, 3, 1) + 1)
            return (sMY, iy_of(q, e, idx * 8 + 7) + 1)

        @block.sync
        def _(sync):
            for c in range(NCH):
                sync.dma_start(wgb[:, c * 8:(c + 1) * 8],
                               wgd[c * 128:(c + 1) * 128, 0:8]).then_inc(sCst, 16)
            sync.dma_start(idf[:, :], identF).then_inc(sCst, 16)
            sync.dma_start(sel[:, :], SEL).then_inc(sCst, 16)
            for gg, (kind, q, e, idx, dmas) in enumerate(wgroups):
                if gg >= 2:
                    pk, pq, pe_, pi, _ = wgroups[gg - 2]
                    dsem, dval = gdone(pk, pq, pe_, pi)
                    sync.wait_ge(dsem, dval)
                for (dram, r0, c0, cw, dc) in dmas:
                    sync.dma_start(wbuf[gg % 2][:, dc:dc + cw],
                                   dram[r0:r0 + 128, c0:c0 + cw]).then_inc(sW[gg % 2], 16)

        @block.gpsimd
        def _(g):
            for q in range(NQ):
                for u in range(2):
                    L.gpsimd(g, q * 2 + u)
                g.wait_ge(sRU, 16 * q)
                g.memset(ACC[:, :], 0.0).then_inc(sMS, 1)
                for i in range(16):
                    m, u = i // 2, i % 2
                    ir = q * 16 + i
                    if ir >= 2:
                        g.wait_ge(sRU, ir - 1)
                    g.dma_start(rbufE[ir % 2][:, :],
                                R_in[:, rcol(m, q, u):rcol(m, q, u) + TT]).then_inc(sRE[ir % 2], 16)
                    if ir >= 1:
                        pr = ir - 1
                        pqq, pii = pr // 16, pr % 16
                        pm, pu = pii // 2, pii % 2
                        g.wait_ge(sRU, pr + 1)
                        g.dma_start(R_out[:, rcol(pm, pqq, pu):rcol(pm, pqq, pu) + TT],
                                    tmp[pr % 2][:, :]).then_inc(sSO[pr % 2], 16)
            ir = NQ * 16 - 1
            pm, pu = (ir % 16) // 2, ir % 2
            g.wait_ge(sRU, ir + 1)
            g.dma_start(R_out[:, rcol(pm, NQ - 1, pu):rcol(pm, NQ - 1, pu) + TT],
                        tmp[ir % 2][:, :]).then_inc(sSO[ir % 2], 16)
            g.wait_ge(sSO[0], 16 * (NQ * 16 // 2))
            g.wait_ge(sSO[1], 16 * (NQ * 16 // 2))

        @block.tensor
        def _(t):
            t.wait_ge(sCst, 160)
            for q in range(NQ):
                def hook(tt_, a):
                    qq = a // 2
                    if qq >= 1 and MOE_NE == 8:
                        tt_.wait_ge(sSil, 256 * qq)
                        tt_.wait_ge(sLG, 8 * qq)
                for u in range(2):
                    L.pe(t, q * 2 + u, extra_wait=hook)
                if not MOE_ROUTING:
                    pass
                t.wait_ge(L.sXN, L.a0 + 2 * q + 2)
                if q >= 1 and MOE_NE == 8:
                    t.wait_ge(sSil, 256 * q)
                for tc in range(8 if MOE_ROUTING else 0):
                    if q * 8 + tc >= 1:
                        t.wait_ge(sLG, q * 8 + tc)
                    for c in range(NCH):
                        t.matmul(psA[0][:, 0:8], xn[:, c * QT + tc * 128:c * QT + tc * 128 + 128],
                                 wgb[:, c * 8:(c + 1) * 8],
                                 start=(c == 0), stop=(c == 7)).then_inc(sMLG, 1 if c == 7 else 0)
                for tc in range(8 if (MOE_ROUTING and MOE_RT_LEVEL >= 2.5) else 0):
                    i = q * 8 + tc
                    t.wait_ge(sRT, i + 1)
                    if q >= 1 and tc == 0 and MOE_NE == 8:
                        t.wait_ge(sEvB, 256 * q)
                    if i >= 1 and MOE_RT_LEVEL >= 3:
                        t.wait_ge(sWT, i)
                    t.matmul(psB[0][0:8, (tc % 4) * 128:(tc % 4) * 128 + 128],
                             wtile[:, tc * 8:(tc + 1) * 8], idf[:, :],
                             start=True, stop=True).then_inc(sMT, 1)
                for e in range(MOE_NE):
                    for u in range(2):
                        if e == 0 and u == 0:
                            t.wait_ge(L.sXN, L.a0 + 2 * q + 2)
                            if q >= 1:
                                t.wait_ge(sYA, 128 * q)
                                t.wait_ge(sWB, 16 * q)
                        t.wait_ge(sWT, q * 8 + 8)
                        t.wait_ge(sYA, q * 128 + e * 16)
                        t.wait_ge(sWB, 2 * (q * 8 + e))
                        t.matmul((pbY_lo if u == 0 else pbY_hi), sel[:, e * 128:(e + 1) * 128],
                                 wsb[:, u * TT:u * TT + TT],
                                 start=True, stop=True).then_inc(sWBM, 1)
                    for mg in range(4):
                        gg = gidx[("B", q, e, mg)]
                        for m in range(4):
                            for u in range(2):
                                ib = ib_of(q, e, mg, m, u)
                                if m == 0 and u == 0:
                                    t.wait_ge(sW[gg % 2], Wdone[gg])
                                if e == 0 and mg == 0 and m == 0 and u == 0:
                                    t.wait_ge(sLG, 8 * (q + 1))
                                    t.wait_ge(sWT, 8 * (q + 1))
                                if ib >= 2:
                                    t.wait_ge(sSil, ib - 1)
                                    t.wait_ge(sEvB, ib - 1)
                                for k in range(8):
                                    ins = t.matmul(psA[ib % 2][:, :],
                                                   wbuf[gg % 2][:, k * 512 + m * 128:k * 512 + m * 128 + 128],
                                                   xn[:, bcol(k, u):bcol(k, u) + TT],
                                                   start=(k == 0), stop=(k == 7))
                                    if k == 7:
                                        ins.then_inc(sMA, 1)
                                for k in range(8):
                                    ins = t.matmul(psB[ib % 2][:, :],
                                                   wbuf[gg % 2][:, 4096 + k * 512 + m * 128:4096 + k * 512 + m * 128 + 128],
                                                   xn[:, bcol(k, u):bcol(k, u) + TT],
                                                   start=(k == 0), stop=(k == 7))
                                    if k == 7:
                                        ins.then_inc(sMB, 1)
                    for mgE in range(2):
                        gg = gidx[("E", q, e, mgE)]
                        for m in range(4):
                            for u in range(2):
                                j = mgE * 8 + m * 2 + u
                                iy = iy_of(q, e, j)
                                if m == 0 and u == 0:
                                    t.wait_ge(sW[gg % 2], Wdone[gg])
                                if mgE == 0 and m == 0 and u == 0:
                                    t.wait_ge(sWB, 2 * (q * 8 + e) + 2)
                                    t.wait_ge(sEvB, ib_of(q, e, 3, 3, 1) + 1)
                                if iy >= 2:
                                    t.wait_ge(sYA, iy - 1)
                                for k in range(16):
                                    ins = t.matmul((pbY_lo if j % 2 == 0 else pbY_hi),
                                                   wbuf[gg % 2][:, k * 512 + m * 128:k * 512 + m * 128 + 128],
                                                   hb[:, bcol(k, u):bcol(k, u) + TT],
                                                   start=(k == 0), stop=(k == 15))
                                    if k == 15:
                                        ins.then_inc(sMY, 1)

        @block.scalar
        def _(s):
            for q in range(NQ):
                for u in range(2):
                    L.act(s, q * 2 + u)
                for tc in range(8 if (MOE_ROUTING and MOE_RT_LEVEL >= 2) else 0):
                    i = q * 8 + tc
                    s.wait_ge(sSG, i + 1)
                    s.activation(rtmp[:, 19:20], rtmp[:, 18:19], AF.Sigmoid).then_inc(sRD, 1)
                for e in range(MOE_NE):
                    for mg in range(4):
                        for m in range(4):
                            for u in range(2):
                                ib = ib_of(q, e, mg, m, u)
                                s.wait_ge(sMA, ib + 1)
                                if ib >= 2:
                                    s.wait_ge(sEvB, ib - 1)
                                s.activation(silu[ib % 2][:, :], psA[ib % 2][:, :],
                                             AF.Silu).then_inc(sSil, 1)

        @block.vector
        def _(v):
            rdc = [0]

            def rinc(ins):
                rdc[0] += 1
                ins.then_inc(sRD, 1)
                return rdc[0]

            for q in range(NQ):
                def extraw(vv, a):
                    qq = a // 2
                    if qq >= 1:
                        if MOE_NE == 8:
                            vv.wait_ge(sMB, 256 * qq)
                        if MOE_ROUTING:
                            vv.wait_ge(sMLG, 8 * qq)
                for u in range(2):
                    L.dve(v, q * 2 + u, extra_wait=extraw)
                for tc in range(8 if MOE_ROUTING else 0):
                    i = q * 8 + tc
                    base = 9 * i
                    lgs = lgbuf[:, tc * 8:(tc + 1) * 8]
                    v.wait_ge(sMLG, i + 1)
                    v.tensor_copy(lgs, psA[0][:, 0:8]).then_inc(sLG, 1)
                    if MOE_RT_LEVEL < 2:
                        continue
                    v.wait_ge(sLG, i + 1)
                    v.reduce_max(rtmp[:, 16:17], lgs, axis=AX.X).then_inc(sRD, 1)
                    v.wait_ge(sRD, base + 1)
                    v.tensor_scalar(rtmp[:, 0:8], lgs, rtmp[:, 16:17], None,
                                    ALU.is_equal).then_inc(sRD, 1)
                    v.wait_ge(sRD, base + 2)
                    v.scalar_tensor_tensor(rtmp[:, 8:16], rtmp[:, 0:8], C.cc[:, 8:9],
                                           lgs, ALU.mult, ALU.add).then_inc(sRD, 1)
                    v.wait_ge(sRD, base + 3)
                    v.reduce_max(rtmp[:, 17:18], rtmp[:, 8:16], axis=AX.X).then_inc(sRD, 1)
                    v.wait_ge(sRD, base + 4)
                    v.tensor_scalar(rtmp[:, 8:16], rtmp[:, 8:16], rtmp[:, 17:18], None,
                                    ALU.is_equal).then_inc(sRD, 1)
                    v.wait_ge(sRD, base + 5)
                    v.tensor_tensor(rtmp[:, 18:19], rtmp[:, 16:17], rtmp[:, 17:18],
                                    ALU.subtract).then_inc(sSG, 1)
                    v.wait_ge(sRD, base + 6)
                    v.tensor_scalar(rtmp[:, 20:21], rtmp[:, 19:20], C.cc[:, 3:4], None,
                                    ALU.mult).then_inc(sRD, 1)
                    v.wait_ge(sRD, base + 7)
                    v.tensor_scalar(rtmp[:, 21:22], rtmp[:, 19:20], C.cc[:, 2:3], C.cc[:, 3:4],
                                    ALU.mult, ALU.add).then_inc(sRD, 1)
                    v.wait_ge(sRD, base + 8)
                    v.tensor_scalar(rtmp[:, 0:8], rtmp[:, 0:8], rtmp[:, 20:21], None,
                                    ALU.mult).then_inc(sRD, 1)
                    v.wait_ge(sRD, base + 9)
                    if q >= 1 and MOE_RT_LEVEL >= 3:
                        v.wait_ge(sMT, (q - 1) * 8 + tc + 1)
                    v.scalar_tensor_tensor(wtile[:, tc * 8:(tc + 1) * 8], rtmp[:, 8:16],
                                           rtmp[:, 21:22],
                                           rtmp[:, 0:8], ALU.mult, ALU.add).then_inc(sRT, 1)
                for tc in range(8 if (MOE_ROUTING and MOE_RT_LEVEL >= 3) else 0):
                    i = q * 8 + tc
                    v.wait_ge(sMT, i + 1)
                    if q >= 1 and tc == 0 and MOE_NE == 8:
                        v.wait_ge(sWBM, 16 * q)
                    v.tensor_copy(wsb[:, tc * 128:(tc + 1) * 128],
                                  psB[0][0:8, (tc % 4) * 128:(tc % 4) * 128 + 128]).then_inc(sWT, 1)
                for e in range(MOE_NE):
                    for u in range(2):
                        v.wait_ge(sWBM, 2 * (q * 8 + e) + u + 1)
                        if (q, e) != (0, 0):
                            v.wait_ge(sYW, (q * 8 + e) * 16)
                        v.tensor_copy(wbsb[:, u * TT:u * TT + TT],
                                      (pbY_lo if u == 0 else pbY_hi)).then_inc(sWB, 1)
                    for mg in range(4):
                        for m in range(4):
                            for u in range(2):
                                ib = ib_of(q, e, mg, m, u)
                                mglob = mg * 4 + m
                                v.wait_ge(sMB, ib + 1)
                                v.wait_ge(sSil, ib + 1)
                                if mg == 0 and m == 0 and u == 0 and (q, e) != (0, 0):
                                    v.wait_ge(sMY, (q * 8 + e) * 16)
                                v.tensor_tensor(hb[:, bcol(mglob, u):bcol(mglob, u) + TT],
                                                psB[ib % 2][:, :], silu[ib % 2][:, :],
                                                ALU.mult).then_inc(sEvB, 1)
                    for j in range(16):
                        mgE, m, u = j // 8, (j % 8) // 2, j % 2
                        mo = mgE * 4 + m
                        iy = iy_of(q, e, j)
                        v.wait_ge(sMY, iy + 1)
                        v.tensor_tensor(ytmp[j % 2][:, :],
                                        (pbY_lo if j % 2 == 0 else pbY_hi),
                                        wbsb[:, u * TT:u * TT + TT], ALU.mult).then_inc(sYW, 1)
                        v.wait_ge(sYW, iy + 1)
                        if e >= 1:
                            v.wait_ge(sYA, iy_of(q, e - 1, j) + 1)
                        if e == 0 and j == 0:
                            v.wait_ge(sMS, q + 1)
                        v.tensor_tensor(ACC[:, bcol(mo, u):bcol(mo, u) + TT], ytmp[j % 2][:, :],
                                        ACC[:, bcol(mo, u):bcol(mo, u) + TT], ALU.add).then_inc(sYA, 1)
                for i in range(16):
                    m, u = i // 2, i % 2
                    ir = q * 16 + i
                    if i == 0:
                        if MOE_NE == 8:
                            v.wait_ge(sYA, 16 * MOE_NE * (q + 1))
                        v.wait_ge(sMS, q + 1)
                    v.wait_ge(sRE[ir % 2], 16 * (ir // 2 + 1))
                    if ir >= 2:
                        v.wait_ge(sSO[ir % 2], 16 * (ir // 2))
                    v.tensor_tensor(tmp[ir % 2][:, :], ACC[:, bcol(m, u):bcol(m, u) + TT],
                                    rbufE[ir % 2][:, :], ALU.add).then_inc(sRU, 1)
    P.pop()

def l0a_chunks(i):
    out = []
    for j in (i - 8, i - 4, i - 2):
        if j >= 0:
            out.append((j, 256))
    if i - 1 >= 0:
        out.append((i - 1, 128))
    out.append((i, 0))
    return out

L0A_CHUNKS = {i: l0a_chunks(i) for i in range(16)}
L0A_CUMI = {}
_c = 0
for _i in range(16):
    L0A_CUMI[_i] = _c
    _c += len(L0A_CHUNKS[_i])
L0A_NCH_HEAD = _c
L0A_NCH_HG = _c * 8


def sm_rows(C, name, chunk, row0, nrows=64):
    c0, n = SM_COLS[name]
    return C.smalls[row0:row0 + nrows, c0 + chunk:c0 + chunk + 1]


def emit_l0attn(P, C, dr, tag):
    nc = P.nc
    P.push()
    R_in, R_out = dr["R_in"], dr["R_out"]
    wsrc = [dr["bqw"], dr["bkw"], dr["bvw"], dr["bow"]]

    ctiles = dr.get("cast_tiles") or []
    if ctiles:
        cbf = [P.sb(f"{tag}_cf{i}", [128, 512], F32) for i in range(4)]
        cbb = [P.sb(f"{tag}_cb{i}", [128, 512], BF16) for i in range(4)]
        sCI = [P.sem(f"{tag}_CI{i}") for i in range(4)]
        sCC = P.sem(f"{tag}_CC")
        sCS = [P.sem(f"{tag}_CS{i}") for i in range(4)]
    chalf = (len(ctiles) + 1) // 2

    NT = T
    xn = P.sb(f"{tag}_xn", [128, NCH * NT], BF16)
    Qe = P.sb(f"{tag}_qe", [65, 8 * NT], BF16)
    Ke = P.sb(f"{tag}_ke", [65, 8 * NT], BF16)
    vt = P.sb(f"{tag}_vt", [128, 16 * 8 * 65], BF16)
    wqk = P.sb(f"{tag}_wqk", [128, 88 * 128], BF16)
    kmax = P.sb(f"{tag}_kmx", [1, 4], F32)
    qsq = P.sb(f"{tag}_qsq", [64, TT], BF16)
    pbf = [P.sb(f"{tag}_pb{i}", [128, 128], BF16) for i in range(2)]
    rcp = P.sb(f"{tag}_rcp", [1, 128], F32)
    sct = [P.sb(f"{tag}_sct{i}", [64, 128], F32) for i in range(2)]
    t1b = P.sb(f"{tag}_t1b", [64, 128], F32)
    tmpR = [P.sb(f"{tag}_tr{i}", [128, TT], F32) for i in range(2)]
    rbufE = [P.sb(f"{tag}_rE{i}", [128, TT], F32) for i in range(2)]

    pA = P.ps(f"{tag}_pA", [128, TT])
    pB = P.ps(f"{tag}_pB", [128, TT])
    pC = P.ps(f"{tag}_pC", [128, 1024])
    psS = [P.ps(f"{tag}_pS0", [128, TT]), P.ps(f"{tag}_pS1", [128, TT])]
    psO = [P.ps(f"{tag}_pO0", [128, TT]), P.ps(f"{tag}_pO1", [128, TT])]

    sWL = P.sem(f"{tag}_WL"); sKO = P.sem(f"{tag}_KO"); sVT = P.sem(f"{tag}_VT")
    sMP = P.sem(f"{tag}_MP"); sEvP = P.sem(f"{tag}_EvP")
    sNQ = P.sem(f"{tag}_NQ")
    sMS2 = P.sem(f"{tag}_MS2"); sEXP = P.sem(f"{tag}_EXP"); sPM = P.sem(f"{tag}_PM")
    sMV = P.sem(f"{tag}_MV"); sOD = P.sem(f"{tag}_OD"); sMR = P.sem(f"{tag}_MR")
    sEV = P.sem(f"{tag}_EVc"); sNR = P.sem(f"{tag}_NR"); sEB = P.sem(f"{tag}_EBx")
    sRE = [P.sem(f"{tag}_RE0"), P.sem(f"{tag}_RE1")]
    sSO = [P.sem(f"{tag}_SO0"), P.sem(f"{tag}_SO1")]
    sRU = P.sem(f"{tag}_RU")

    def wcol(p, j):
        return (p * 22 + j) * 128

    def proj_sched(hg):
        out = []
        for pi, kind in ((0, "q"), (1, "k")):
            for b in range(4 * hg, 4 * hg + 4):
                for tt in range(4):
                    out.append((kind, pi, b, tt))
        for b in range(4 * hg, 4 * hg + 4):
            for tc in range(16):
                out.append(("v", 2, b, tc))
        return out

    S_SEQ = 2 * L0A_NCH_HG
    IO_SEQ = 256
    EVP_HG = 96
    EVP_SEQ = 192
    NQ_HEAD = 34
    NQ_SEQ = NQ_HEAD * 16
    IP_SEQ = 224

    def nq_base(s, hg, hl):
        return s * NQ_SEQ + hg * NQ_HEAD * 8 + hl * NQ_HEAD

    def ia_base(s, hg, hl, i):
        return s * S_SEQ + hg * L0A_NCH_HG + hl * L0A_NCH_HEAD + L0A_CUMI[i]

    def io_of(s, hg, hl, i):
        return s * IO_SEQ + hg * 128 + hl * 16 + i

    def rot_wait(eng, ip):
        p = ip - 2
        if p < 0:
            return
        sq, loc = p // IP_SEQ, p % IP_SEQ
        if loc >= 192:
            eng.wait_ge(sRU, sq * 32 + (loc - 192) + 1)
        else:
            eng.wait_ge(sEvP, sq * 192 + loc + 1)

    for s in range(NSEQ):
        ts = s * NT
        ip0 = s * IP_SEQ
        evp0 = s * EVP_SEQ
        ru0 = s * 32

        P.push()
        L = LNEmit(P, C, f"{tag}{s}", R_in, 4,
                   lambda a, c: c * NTOK + ts + a * TT,
                   lambda a, c: xn[:, c * NT + a * TT: c * NT + a * TT + TT],
                   ps3=(pA, pB, pC))
        with nc.Block() as b1:
            @b1.gpsimd
            def _(g):
                for a in range(4):
                    L.gpsimd(g, a)

            @b1.tensor
            def _(t):
                for a in range(4):
                    L.pe(t, a)

            @b1.scalar
            def _(sc):
                for a in range(4):
                    L.act(sc, a)

            @b1.vector
            def _(v):
                for a in range(4):
                    L.dve(v, a)
        P.pop()

        P.push()
        obuf = P.sb(f"{tag}_obf{s}", [128, NCH * NT], BF16)

        with nc.Block() as b2:
            @b2.sync
            def _(sy):
                if s == 0:
                    for p in range(4):
                        for j in range(22):
                            sy.dma_start(wqk[:, wcol(p, j):wcol(p, j) + 128],
                                         wsrc[p][j * 128:(j + 1) * 128, :]).then_inc(sWL, 16)
                t0, t1 = s * chalf, min(len(ctiles), (s + 1) * chalf)
                for i in range(t0, t1 + 4):
                    if i < t1:
                        src, dst, r0, c0, cw = ctiles[i]
                        if i >= 4:
                            sy.wait_ge(sCC, i - 3)
                        sy.dma_start(cbf[i % 4][:, 0:cw],
                                     src[r0:r0 + 128, c0:c0 + cw]).then_inc(sCI[i % 4], 16)
                    j_ = i - 4
                    if t0 <= j_ < t1:
                        src, dst, r0, c0, cw = ctiles[j_]
                        sy.wait_ge(sCC, j_ + 1)
                        sy.dma_start(dst[r0:r0 + 128, c0:c0 + cw],
                                     cbb[j_ % 4][:, 0:cw]).then_inc(sCS[j_ % 4], 16)
                if s == 1 and ctiles:
                    n = len(ctiles)
                    for b in range(4):
                        nb_ = (n - b + 3) // 4
                        if nb_ > 0:
                            sy.wait_ge(sCS[b], 16 * nb_)

            @b2.gpsimd
            def _(g):
                cv_next = [s * chalf]
                cv_end = min(len(ctiles), (s + 1) * chalf)

                def cast_conv():
                    i = cv_next[0]
                    if i >= cv_end:
                        return
                    src, dst, r0, c0, cw = ctiles[i]
                    g.wait_ge(sCI[i % 4], 16 * (i // 4 + 1))
                    if i >= 4:
                        g.wait_ge(sCS[i % 4], 16 * (i // 4))
                    g.tensor_copy(cbb[i % 4][:, 0:cw], cbf[i % 4][:, 0:cw]).then_inc(sCC, 1)
                    cv_next[0] += 1

                nm = 0
                for hg in range(2):
                    for hl in range(8):
                        for i in range(16):
                            for n_, (j, mcol) in enumerate(L0A_CHUNKS[i]):
                                ia = ia_base(s, hg, hl, i) + n_
                                g.wait_ge(sEXP, ia + 1)
                                if ia >= 2:
                                    g.wait_ge(sMV, ia - 1)
                                g.tensor_tensor(pbf[ia % 2][:, :], pbf[ia % 2][:, :],
                                                C.masks[:, mcol:mcol + 128], ALU.mult).then_inc(sPM, 1)
                                nm += 1
                                if nm % 2 == 0:
                                    cast_conv()
                while cv_next[0] < cv_end:
                    cast_conv()
                for ir_ in range(32):
                    b, tt = ir_ // 4, ir_ % 4
                    ir = ru0 + ir_
                    if ir >= 2:
                        g.wait_ge(sRU, ir - 1)
                    g.dma_start(rbufE[ir % 2][:, :],
                                R_in[:, b * NTOK + ts + tt * TT: b * NTOK + ts + tt * TT + TT]
                                ).then_inc(sRE[ir % 2], 16)
                    if ir_ >= 1:
                        pr = ir - 1
                        pb_, pt_ = (pr - ru0) // 4, (pr - ru0) % 4
                        g.wait_ge(sRU, pr + 1)
                        g.dma_start(R_out[:, pb_ * NTOK + ts + pt_ * TT: pb_ * NTOK + ts + pt_ * TT + TT],
                                    tmpR[pr % 2][:, :]).then_inc(sSO[pr % 2], 16)
                ir = ru0 + 31
                g.wait_ge(sRU, ir + 1)
                g.dma_start(R_out[:, 7 * NTOK + ts + 3 * TT: 7 * NTOK + ts + 3 * TT + TT],
                            tmpR[ir % 2][:, :]).then_inc(sSO[ir % 2], 16)
                g.wait_ge(sSO[0], 16 * ((ru0 + 32 + 1) // 2))
                g.wait_ge(sSO[1], 16 * ((ru0 + 32) // 2))

            @b2.tensor
            def _(t):
                if s == 0:
                    t.wait_ge(sWL, 16 * 88)
                for hg in range(2):
                    for idx, (kind, pi, b, tx) in enumerate(proj_sched(hg)):
                        ip = ip0 + hg * 96 + idx
                        rot_wait(t, ip)
                        blocks = L0A_META[b]
                        dst = (pA if ip % 2 == 0 else pB)
                        if kind in ("q", "k"):
                            for n_, (ib, ro) in enumerate(blocks):
                                t.matmul(dst[:, :],
                                         wqk[:, wcol(pi, ro):wcol(pi, ro) + 128],
                                         xn[:, ib * NT + tx * TT: ib * NT + tx * TT + TT],
                                         start=(n_ == 0), stop=(n_ == len(blocks) - 1)
                                         ).then_inc(sMP, 1 if n_ == len(blocks) - 1 else 0)
                        else:
                            for n_, (ib, ro) in enumerate(blocks):
                                t.matmul(dst[:, 0:128],
                                         xn[:, ib * NT + tx * 128: ib * NT + tx * 128 + 128],
                                         wqk[:, wcol(2, ro):wcol(2, ro) + 128],
                                         start=(n_ == 0), stop=(n_ == len(blocks) - 1)
                                         ).then_inc(sMP, 1 if n_ == len(blocks) - 1 else 0)
                    for hl in range(8):
                        base = nq_base(s, hg, hl)
                        for tt in range(4):
                            t.wait_ge(sNQ, base + 4 * tt + 1)
                            t.matmul(pC[0:1, 0:TT], C.ones_col[0:64, 0:1], qsq[:, :],
                                     start=True, stop=True).then_inc(sNQ, 1)
                        for tt in range(4):
                            t.wait_ge(sNQ, base + 18 + 3 * tt + 1)
                            t.matmul(pC[0:1, 0:TT], C.ones_col[0:64, 0:1], qsq[:, :],
                                     start=True, stop=True).then_inc(sNQ, 1)
                    achunks = []
                    for hl in range(8):
                        for i in range(16):
                            io = io_of(s, hg, hl, i)
                            lst = L0A_CHUNKS[i]
                            for n_, (j, mcol) in enumerate(lst):
                                ia = ia_base(s, hg, hl, i) + n_
                                achunks.append(dict(hl=hl, i=i, j=j, ia=ia, io=io,
                                                    first=(i == 0 and n_ == 0),
                                                    start=(n_ == 0),
                                                    stop=(n_ == len(lst) - 1)))

                    def emit_S(ch):
                        hl, ia = ch["hl"], ch["ia"]
                        hcol = hl * NT
                        if ch["first"]:
                            t.wait_ge(sNQ, nq_base(s, hg, hl) + NQ_HEAD)
                            t.wait_ge(sKO, s + 1)
                        if ia >= 2:
                            t.wait_ge(sEXP, ia - 1)
                        t.matmul(psS[ia % 2][0:128, 0:128],
                                 Ke[0:65, hcol + ch["j"] * 128: hcol + ch["j"] * 128 + 128],
                                 Qe[0:65, hcol + ch["i"] * 128: hcol + ch["i"] * 128 + 128],
                                 start=True, stop=True).then_inc(sMS2, 1)

                    def emit_PV(ch):
                        hl, ia, io = ch["hl"], ch["ia"], ch["io"]
                        t.wait_ge(sPM, ia + 1)
                        if ch["start"] and io >= 2:
                            t.wait_ge(sEV, io - 1)
                            t.wait_ge(sNR, io - 1)
                        t.matmul(psO[io % 2][0:65, 0:128],
                                 vt[:, (ch["j"] * 8 + hl) * 65: (ch["j"] * 8 + hl) * 65 + 65],
                                 pbf[ia % 2][:, :],
                                 start=ch["start"], stop=ch["stop"]).then_inc(sMV, 1)

                    def emit_MR(io):
                        t.wait_ge(sNR, io + 1)
                        t.wait_ge(sEB, io)
                        t.matmul(pC[0:64, TT:TT + 128], C.ones_rowF[0:1, 0:64],
                                 rcp[:, :], start=True, stop=True).then_inc(sMR, 1)

                    aseq_ = achunks + [None, None]
                    for k_, ch in enumerate(aseq_):
                        if ch is not None:
                            emit_S(ch)
                        if k_ >= 1 and aseq_[k_ - 1] is not None:
                            emit_PV(aseq_[k_ - 1])
                        if k_ >= 2 and aseq_[k_ - 2] is not None and aseq_[k_ - 2]["stop"]:
                            emit_MR(aseq_[k_ - 2]["io"])
                for irx in range(32):
                    b, tt = irx // 4, irx % 4
                    ip = ip0 + 192 + irx
                    ir = ru0 + irx
                    if irx == 0:
                        t.wait_ge(sEvP, evp0 + EVP_SEQ)
                        t.wait_ge(sOD, 2 * (s + 1) * IO_SEQ)
                    rot_wait(t, ip)
                    dst = (pA if ip % 2 == 0 else pB)
                    blocks = L0A_META[b]
                    for n_, (ib, ro) in enumerate(blocks):
                        t.matmul(dst[:, :],
                                 wqk[:, wcol(3, ro):wcol(3, ro) + 128],
                                 obuf[:, ib * NT + tt * TT: ib * NT + tt * TT + TT],
                                 start=(n_ == 0), stop=(n_ == len(blocks) - 1)
                                 ).then_inc(sMP, 1 if n_ == len(blocks) - 1 else 0)

            @b2.scalar
            def _(sc):
                for hg in range(2):
                    for hl in range(8):
                        base = nq_base(s, hg, hl)
                        hcol = hl * NT
                        if hl == 0:
                            sc.wait_ge(sEvP, evp0 + EVP_HG * (hg + 1))
                        for tt in range(4):
                            sc.wait_ge(sNQ, base + 4 * tt)
                            sc.activation(qsq[:, :], Ke[0:64, hcol + tt * TT: hcol + tt * TT + TT],
                                          AF.Square).then_inc(sNQ, 1)
                        sc.wait_ge(sNQ, base + 16)
                        sc.activation(kmax[:, 1:2], kmax[:, 0:1], AF.Sqrt).then_inc(sNQ, 1)
                        for tt in range(4):
                            sc.wait_ge(sNQ, base + 18 + 3 * tt)
                            sc.activation(qsq[:, :], Qe[0:64, hcol + tt * TT: hcol + tt * TT + TT],
                                          AF.Square).then_inc(sNQ, 1)
                            sc.wait_ge(sNQ, base + 18 + 3 * tt + 2)
                            sc.activation(Qe[64:65, hcol + tt * TT: hcol + tt * TT + TT],
                                          pC[0:1, 0:TT], AF.Sqrt).then_inc(sNQ, 1)
                    aios = []
                    for hl in range(8):
                        for i in range(16):
                            aios.append((hl, i, io_of(s, hg, hl, i),
                                         ia_base(s, hg, hl, i), len(L0A_CHUNKS[i])))

                    def sc_exptile(pio_):
                        sc.wait_ge(sMR, pio_ + 1)
                        if pio_ >= 2:
                            sc.wait_ge(sOD, 2 * (pio_ - 2) + 1)
                        sc.activation(tmpR[pio_ % 2][0:64, 0:128], pC[0:64, TT:TT + 128],
                                      AF.Exp, scale=C.cc[0:64, 6:7]).then_inc(sEB, 1)

                    for k_, (hl, i, io, ia0, nch) in enumerate(aios):
                        for n_ in range(nch):
                            ia = ia0 + n_
                            sc.wait_ge(sMS2, ia + 1)
                            if ia >= 2:
                                sc.wait_ge(sMV, ia - 1)
                            sc.activation(pbf[ia % 2][:, :], psS[ia % 2][0:128, 0:128],
                                          AF.Exp, scale=C.cc[:, 5:6]).then_inc(sEXP, 1)
                        if k_ >= 1:
                            sc_exptile(aios[k_ - 1][2])
                        sc.wait_ge(sMV, ia0 + nch)
                        sc.wait_ge(sMR, io)
                        sc.activation(rcp[:, :], psO[io % 2][64:65, 0:128],
                                      AF.Ln).then_inc(sNR, 1)
                    sc_exptile(aios[-1][2])

            @b2.vector
            def _(v):
                if s == 0:
                    v.memset(Ke[64:65, :], 1.0).then_inc(sKO, 1)
                else:
                    v.wait_ge(sMS2, S_SEQ)
                    v.engine_nop().then_inc(sKO, 1)
                for hg in range(2):
                    for idx, (kind, pi, b, tx) in enumerate(proj_sched(hg)):
                        ip = ip0 + hg * 96 + idx
                        src = (pA if ip % 2 == 0 else pB)
                        v.wait_ge(sMP, ip + 1)
                        if idx == 0 and (s > 0 or hg > 0):
                            v.wait_ge(sMS2, s * S_SEQ + hg * L0A_NCH_HG)
                        bl = b - 4 * hg
                        if kind == "q":
                            v.tensor_scalar(Qe[0:64, (2 * bl) * NT + tx * TT:(2 * bl) * NT + tx * TT + TT],
                                            src[0:64, :], sm_rows(C, "l0a_qb", b, 0), None, ALU.add)
                            v.tensor_scalar(Qe[0:64, (2 * bl + 1) * NT + tx * TT:(2 * bl + 1) * NT + tx * TT + TT],
                                            src[64:128, :], sm_rows(C, "l0a_qb", b, 64), None,
                                            ALU.add).then_inc(sEvP, 1)
                        elif kind == "k":
                            v.tensor_scalar(Ke[0:64, (2 * bl) * NT + tx * TT:(2 * bl) * NT + tx * TT + TT],
                                            src[0:64, :], sm_rows(C, "l0a_kb", b, 0), None, ALU.add)
                            v.tensor_scalar(Ke[0:64, (2 * bl + 1) * NT + tx * TT:(2 * bl + 1) * NT + tx * TT + TT],
                                            src[64:128, :], sm_rows(C, "l0a_kb", b, 64), None,
                                            ALU.add).then_inc(sEvP, 1)
                        else:
                            tc = tx
                            if bl == 0 and tc == 0:
                                if s > 0 or hg > 0:
                                    v.wait_ge(sMV, s * S_SEQ + hg * L0A_NCH_HG)
                                v.memset(vt[:, :], 1.0).then_inc(sVT, 1)
                                v.wait_ge(sVT, s * 2 + hg + 1)
                            v.tensor_copy(vt[:, (tc * 8 + 2 * bl) * 65:(tc * 8 + 2 * bl) * 65 + 64],
                                          src[0:128, 0:64])
                            v.tensor_copy(vt[:, (tc * 8 + 2 * bl + 1) * 65:(tc * 8 + 2 * bl + 1) * 65 + 64],
                                          src[0:128, 64:128]).then_inc(sEvP, 1)
                    for hl in range(8):
                        base = nq_base(s, hg, hl)
                        hcol = hl * NT
                        for tt in range(4):
                            v.wait_ge(sNQ, base + 4 * tt + 2)
                            v.reduce_max(kmax[:, 3:4], pC[0:1, 0:TT], axis=AX.X).then_inc(sNQ, 1)
                            v.wait_ge(sNQ, base + 4 * tt + 3)
                            if tt == 0:
                                v.tensor_copy(kmax[:, 0:1], kmax[:, 3:4]).then_inc(sNQ, 1)
                            else:
                                v.tensor_tensor(kmax[:, 0:1], kmax[:, 0:1], kmax[:, 3:4],
                                                ALU.max).then_inc(sNQ, 1)
                        v.wait_ge(sNQ, base + 17)
                        v.tensor_scalar(kmax[:, 2:3], kmax[:, 1:2], C.cc[0:1, 6:7], None,
                                        ALU.mult).then_inc(sNQ, 1)
                        for tt in range(4):
                            v.wait_ge(sNQ, base + 30 + tt)
                            v.tensor_scalar(Qe[64:65, hcol + tt * TT: hcol + tt * TT + TT],
                                            Qe[64:65, hcol + tt * TT: hcol + tt * TT + TT],
                                            kmax[0:1, 2:3], None, ALU.mult).then_inc(sNQ, 1)
                    avios = []
                    for hl in range(8):
                        for i in range(16):
                            avios.append((hg * 8 + hl, i, io_of(s, hg, hl, i),
                                          ia_base(s, hg, hl, i) + len(L0A_CHUNKS[i])))

                    def a_evac(el):
                        io = el[2]
                        v.wait_ge(sMV, el[3])
                        v.tensor_copy(sct[io % 2][:, :],
                                      psO[io % 2][0:64, 0:128]).then_inc(sEV, 1)

                    def a_mult(el):
                        h, i, io = el[0], el[1], el[2]
                        v.wait_ge(sEB, io + 1)
                        v.wait_ge(sEV, io + 1)
                        v.tensor_tensor(t1b[:, :], sct[io % 2][:, :], tmpR[io % 2][0:64, 0:128],
                                        ALU.mult).then_inc(sOD, 1)
                        v.wait_ge(sOD, 2 * io + 1)
                        v.tensor_scalar(obuf[(h % 2) * 64:(h % 2) * 64 + 64,
                                             (h // 2) * NT + i * 128:(h // 2) * NT + i * 128 + 128],
                                        t1b[:, :], sm_rows(C, "l0a_vb", h // 2, (h % 2) * 64),
                                        None, ALU.add).then_inc(sOD, 1)

                    for k_, el in enumerate(avios):
                        a_evac(el)
                        if k_ >= 1:
                            a_mult(avios[k_ - 1])
                    a_mult(avios[-1])
                for irx in range(32):
                    b, tt = irx // 4, irx % 4
                    ip = ip0 + 192 + irx
                    ir = ru0 + irx
                    src = (pA if ip % 2 == 0 else pB)
                    v.wait_ge(sMP, ip + 1)
                    v.wait_ge(sRE[ir % 2], 16 * (ir // 2 + 1))
                    if ir >= 2:
                        v.wait_ge(sSO[ir % 2], 16 * (ir // 2))
                    v.scalar_tensor_tensor(tmpR[ir % 2][:, :], src[:, :],
                                           sm(C, "l0a_ob", b), rbufE[ir % 2][:, :],
                                           ALU.add, ALU.add).then_inc(sRU, 1)
        P.pop()
    P.pop()


def emit_l1attn(P, C, dr, tag):
    nc = P.nc
    P.push()
    R_in, R_out = dr["R_in"], dr["R_out"]
    caw, cpw_d, cvw_d, gw1_d, gw2_d = dr["bcaw"], dr["bcpw"], dr["bcvw"], dr["bgw1"], dr["bgw2"]

    NT = T
    NHG, HPG = 4, 4
    xn = P.sb(f"{tag}_xn", [128, NCH * NT], BF16)
    Qe = P.sb(f"{tag}_qe", [65, HPG * NT], BF16)
    Ke = P.sb(f"{tag}_ke", [65, HPG * NT], BF16)
    vf = P.sb(f"{tag}_vf", [64, HPG * NT], BF16)
    vt65 = P.sb(f"{tag}_vt", [128, 16 * HPG * 65], BF16)
    kce = P.sb(f"{tag}_kce", [65, HPG * 127], BF16)
    vcp65 = P.sb(f"{tag}_vcp", [127, HPG * 65], BF16)
    cvwb = P.sb(f"{tag}_cvw", [64, 32 * 64], BF16)
    gw2b = P.sb(f"{tag}_gw2", [128, 16], BF16)
    wbuf = [P.sb(f"{tag}_wb{i}", [128, 2048], BF16) for i in range(2)]
    kmax = P.sb(f"{tag}_kmx", [1, 4], F32)
    qsq = P.sb(f"{tag}_qsq", [64, TT], BF16)
    sqc = P.sb(f"{tag}_sqc", [64, 127], BF16)
    co = P.sb(f"{tag}_co", [64, 127], BF16)
    xhh = P.sb(f"{tag}_xhh", [64, 2 * 127], F32)
    vtmp = P.sb(f"{tag}_vtp", [64, 127], BF16)
    gsc = P.sb(f"{tag}_gsc", [1, 64], F32)
    lnsc = P.sb(f"{tag}_lns", [1, 4 * TT], F32)
    sql = P.sb(f"{tag}_sql", [128, 2 * TT], BF16)
    lnx = P.sb(f"{tag}_lnx", [128, 2 * TT], F32)
    pexp = [P.sb(f"{tag}_px{i}", [128, TT], BF16) for i in range(2)]
    rcp = P.sb(f"{tag}_rcp", [1, TT], F32)
    sct = [P.sb(f"{tag}_sct{i}", [64, TT], F32) for i in range(2)]
    gy = P.sb(f"{tag}_gy", [128, 8], F32)
    gyb = P.sb(f"{tag}_gyb", [128, 8], BF16)
    gy2 = P.sb(f"{tag}_gy2", [128, 8], BF16)
    gcol = P.sb(f"{tag}_gcl", [128, 4], F32)
    tmpR = [P.sb(f"{tag}_tr{i}", [128, TT], F32) for i in range(2)]
    rbufE = [P.sb(f"{tag}_rB{i}", [128, TT], F32) for i in range(2)]

    pA = P.ps(f"{tag}_pA", [128, TT])
    pB = P.ps(f"{tag}_pB", [128, TT])
    pC = P.ps(f"{tag}_pC", [128, 1024])
    psS = [P.ps(f"{tag}_pS0", [128, TT]), P.ps(f"{tag}_pS1", [128, TT])]
    psO = [P.ps(f"{tag}_pO0", [128, TT]), P.ps(f"{tag}_pO1", [128, TT])]
    pH = psS[0]

    sWL = P.sem(f"{tag}_WL"); sWG = [P.sem(f"{tag}_WG0"), P.sem(f"{tag}_WG1")]
    sMP = P.sem(f"{tag}_MP"); sEvP = P.sem(f"{tag}_EvP")
    sTR = P.sem(f"{tag}_TR"); sCV = P.sem(f"{tag}_CV"); sNQ = P.sem(f"{tag}_NQ")
    sMS2 = P.sem(f"{tag}_MS2"); sEXP = P.sem(f"{tag}_EXP"); sPM = P.sem(f"{tag}_PM")
    sMV = P.sem(f"{tag}_MV"); sOD = P.sem(f"{tag}_OD"); sMR = P.sem(f"{tag}_MR")
    sEV = P.sem(f"{tag}_EVc"); sNR = P.sem(f"{tag}_NR"); sEB = P.sem(f"{tag}_EBx")
    sLL = P.sem(f"{tag}_LL"); sGA = P.sem(f"{tag}_GA"); sCB = P.sem(f"{tag}_CB"); sCB2 = P.sem(f"{tag}_CB2")
    sVT = P.sem(f"{tag}_VT"); sKO = P.sem(f"{tag}_KO")
    sRE = [P.sem(f"{tag}_RE0"), P.sem(f"{tag}_RE1")]
    sSO = [P.sem(f"{tag}_SO0"), P.sem(f"{tag}_SO1")]
    sRU = P.sem(f"{tag}_RU")

    IP_SEQ = 136
    EVP_SEQ = 104
    NQ_HEAD = 38
    NQ_SEQ = NQ_HEAD * 16
    S_HG = HPG * 44
    S_SEQ = NHG * S_HG
    IO_HG = HPG * 8
    IO_SEQ = NHG * IO_HG
    CV_K, CV_V = 17, 19
    CV_HEAD = CV_K + CV_V
    CV_HG = HPG * CV_HEAD
    CV_SEQ = NHG * CV_HG
    TR_HG = 16 * HPG * 2
    TR_SEQ = NHG * TR_HG
    LL_TILE = 50
    GA_STEPS = 31
    LL0 = {0: 0, 1: 4 * 50}
    GA0 = {0: 0, 1: 31}
    MASKED = {}
    _cum = 0
    for _s in range(2):
        for _hg in range(NHG):
            for _hl in range(HPG):
                for _qb in range(4):
                    for _j in range(4 * _qb + 4):
                        _ia = _s * S_SEQ + _hg * S_HG + _hl * 44 + [0, 4, 12, 24][_qb] + _j
                        if _j >= 4 * _qb:
                            _cum += 1
                            MASKED[_ia] = _cum
                for _qb in range(4):
                    _ia = _s * S_SEQ + _hg * S_HG + _hl * 44 + 40 + _qb
                    if _qb == 0:
                        _cum += 1
                        MASKED[_ia] = _cum

    def wg_list(s):
        gl = []
        for hg in range(NHG):
            for ki in range(3):
                for sub in range(2):
                    oc = ki * 8 + hg * 2 + sub
                    done = s * IP_SEQ + hg * 24 + ki * 8 + sub * 4 + 4
                    gl.append(("qkv", hg, ki, sub, oc, done))
        for m in range(8):
            gl.append(("gw1", m, 0, 0, m, s * IP_SEQ + 96 + m + 1))
        for m in range(8):
            gl.append(("cpj", m, 0, 0, m, s * IP_SEQ + 104 + m * 4 + 4))
        return gl

    NWG = 40
    WGV = {}
    _pt = [0, 0]
    for _s in range(2):
        for _g, _w in enumerate(wg_list(_s)):
            _gg = _s * NWG + _g
            _nd = 16 if _w[0] == "gw1" else 8
            _pt[_gg % 2] += 16 * _nd
            WGV[_gg] = _pt[_gg % 2]

    def rot_guard(eng, ip):
        p = ip - 2
        if p < 0:
            return
        sq, loc = p // IP_SEQ, p % IP_SEQ
        if loc < EVP_SEQ:
            eng.wait_ge(sEvP, sq * EVP_SEQ + loc + 1)
        else:
            eng.wait_ge(sRU, sq * 32 + (loc - EVP_SEQ) + 1)

    def cv_base(s, hg, hl, kv):
        return s * CV_SEQ + hg * CV_HG + hl * CV_HEAD + (0 if kv == 0 else CV_K)

    def nq_base(s, hg, hl):
        return s * NQ_SEQ + (hg * HPG + hl) * NQ_HEAD

    def ia_base(s, hg, hl, qb, comp):
        b = s * S_SEQ + hg * S_HG + hl * 44
        if not comp:
            return b + [0, 4, 12, 24][qb]
        return b + 40 + qb

    def io_of(s, hg, hl, qb, comp):
        return s * IO_SEQ + hg * IO_HG + hl * 8 + (0 if not comp else 4) + qb

    for s in range(NSEQ):
        ts = s * NT
        ip0 = s * IP_SEQ
        evp0 = s * EVP_SEQ
        ru0 = s * 32

        P.push()
        L = LNEmit(P, C, f"{tag}{s}", R_in, 4,
                   lambda a, c: c * NTOK + ts + a * TT,
                   lambda a, c: xn[:, c * NT + a * TT: c * NT + a * TT + TT],
                   ps3=(pA, pB, pC))
        with nc.Block() as b1:
            @b1.gpsimd
            def _(g):
                for a in range(4):
                    L.gpsimd(g, a)

            @b1.tensor
            def _(t):
                for a in range(4):
                    L.pe(t, a)

            @b1.scalar
            def _(sc):
                for a in range(4):
                    L.act(sc, a)

            @b1.vector
            def _(v):
                for a in range(4):
                    L.dve(v, a)
        P.pop()

        P.push()
        lbuf = P.sb(f"{tag}_lb{s}", [128, NCH * NT], BF16)
        cbuf = P.sb(f"{tag}_cb{s}", [128, NCH * NT], BF16)
        wgl = wg_list(s)

        with nc.Block() as b2:
            @b2.sync
            def _(sy):
                if s == 0:
                    for tap in range(32):
                        sy.dma_start(cvwb[:, tap * 64:(tap + 1) * 64],
                                     cvw_d[tap * 64:tap * 64 + 64, :]).then_inc(sWL, 16)
                    for c in range(8):
                        sy.dma_start(gw2b[:, c * 2:(c + 1) * 2],
                                     gw2_d[c * 128:(c + 1) * 128, :]).then_inc(sWL, 16)
                for gg, (kind, aa, ki, sub, oc, done) in enumerate(wgl):
                    gg_g = s * NWG + gg
                    if gg_g >= 2:
                        pg = gg_g - 2
                        psq, ploc = pg // NWG, pg % NWG
                        sy.wait_ge(sMP, wg_list(psq)[ploc][5])
                    nk = 8 if kind in ("qkv", "cpj") else 16
                    for k in range(nk):
                        if kind == "qkv":
                            src = caw[k * 128:(k + 1) * 128, oc * 128:(oc + 1) * 128]
                        elif kind == "gw1":
                            src = gw1_d[k * 128:(k + 1) * 128, aa * 128:(aa + 1) * 128]
                        else:
                            src = cpw_d[k * 128:(k + 1) * 128, aa * 128:(aa + 1) * 128]
                        sy.dma_start(wbuf[gg_g % 2][:, k * 128:(k + 1) * 128],
                                     src).then_inc(sWG[gg_g % 2], 16)

            @b2.gpsimd
            def _(g):
                for hg in range(NHG):
                    for hl in range(HPG):
                        for qb in range(4):
                            for n_ in range(4 * qb + 4):
                                ia = ia_base(s, hg, hl, qb, False) + n_
                                if ia in MASKED:
                                    g.wait_ge(sEXP, ia + 1)
                                    r = n_ - 4 * qb
                                    g.tensor_tensor(pexp[ia % 2][:, :], pexp[ia % 2][:, :],
                                                    C.masks[:, 512 + r * 512: 512 + r * 512 + TT],
                                                    ALU.mult).then_inc(sPM, 1)
                        for qb in range(4):
                            ia = ia_base(s, hg, hl, qb, True)
                            if ia in MASKED:
                                g.wait_ge(sEXP, ia + 1)
                                g.tensor_tensor(pexp[ia % 2][0:127, 0:TT], pexp[ia % 2][0:127, 0:TT],
                                                C.masks[0:127, 512:512 + TT], ALU.mult).then_inc(sPM, 1)
                for ir_ in range(32):
                    b, tt = ir_ // 4, ir_ % 4
                    ir = ru0 + ir_
                    if ir >= 2:
                        g.wait_ge(sRU, ir - 1)
                    g.dma_start(rbufE[ir % 2][:, :],
                                R_in[:, b * NTOK + ts + tt * TT: b * NTOK + ts + tt * TT + TT]
                                ).then_inc(sRE[ir % 2], 16)
                    if ir_ >= 1:
                        pr = ir - 1
                        pb_, pt_ = (pr - ru0) // 4, (pr - ru0) % 4
                        g.wait_ge(sRU, pr + 1)
                        g.dma_start(R_out[:, pb_ * NTOK + ts + pt_ * TT: pb_ * NTOK + ts + pt_ * TT + TT],
                                    tmpR[pr % 2][:, :]).then_inc(sSO[pr % 2], 16)
                ir = ru0 + 31
                g.wait_ge(sRU, ir + 1)
                g.dma_start(R_out[:, 7 * NTOK + ts + 3 * TT: 7 * NTOK + ts + 3 * TT + TT],
                            tmpR[ir % 2][:, :]).then_inc(sSO[ir % 2], 16)
                g.wait_ge(sSO[0], 16 * ((ru0 + 32 + 1) // 2))
                g.wait_ge(sSO[1], 16 * ((ru0 + 32) // 2))

            @b2.tensor
            def _(t):
                if s == 0:
                    t.wait_ge(sWL, 16 * 40)
                for hg in range(NHG):
                    for ki in range(3):
                        for sub in range(2):
                            gg_g = s * NWG + hg * 6 + ki * 2 + sub
                            for tt in range(4):
                                ip = ip0 + hg * 24 + ki * 8 + sub * 4 + tt
                                rot_guard(t, ip)
                                if tt == 0:
                                    t.wait_ge(sWG[gg_g % 2], WGV[gg_g])
                                dst = (pA if ip % 2 == 0 else pB)
                                for k in range(NCH):
                                    t.matmul(dst[:, :],
                                             wbuf[gg_g % 2][:, k * 128:(k + 1) * 128],
                                             xn[:, k * NT + tt * TT: k * NT + tt * TT + TT],
                                             start=(k == 0), stop=(k == 7)
                                             ).then_inc(sMP, 1 if k == 7 else 0)
                    for hl in range(HPG):
                        for tc in range(16):
                            trs = s * TR_SEQ + hg * TR_HG + (hl * 16 + tc) * 2
                            t.wait_ge(sTR, trs)
                            if hl == 0 and tc == 0:
                                t.wait_ge(sEvP, evp0 + hg * 24 + 24)
                                if hg >= 1 or s >= 1:
                                    pio = s * IO_SEQ + hg * IO_HG if hg >= 1 else (s - 1) * IO_SEQ + NHG * IO_HG
                                    t.wait_ge(sEB, pio)
                            t.matmul(pC[0:128, TT:TT + 64],
                                     vf[:, hl * NT + tc * 128: hl * NT + tc * 128 + 128],
                                     C.ident[0:64, 0:64], start=True, stop=True).then_inc(sTR, 1)
                    for hl in range(HPG):
                        for kv in range(2):
                            base = cv_base(s, hg, hl, kv)
                            t.wait_ge(sCV, base)
                            if hl == 0 and kv == 0:
                                t.wait_ge(sNQ, (s * 16 + hg * HPG) * NQ_HEAD)
                                t.wait_ge(sTR, s * TR_SEQ + hg * TR_HG + TR_HG)
                                t.wait_ge(sEXP, s * S_SEQ + hg * S_HG)
                                t.wait_ge(sPM, MASKED.get(s * S_SEQ + hg * S_HG - 1, 0))
                            srcT = Ke if kv == 0 else vf
                            r0 = 0 if kv == 0 else 0
                            for tap in range(32):
                                t.matmul(pH[0:64, 0:127],
                                         cvwb[:, tap * 64:(tap + 1) * 64],
                                         srcT[0:64, hl * NT + tap: hl * NT + tap + 16 * 126 + 1:16],
                                         start=(tap == 0), stop=(tap == 31)
                                         ).then_inc(sCV, 1 if tap == 31 else 0)
                            t.wait_ge(sCV, base + 3)
                            t.matmul(pH[0:1, 128:255], C.ones_col[0:64, 0:1], co[:, :],
                                     start=True, stop=True).then_inc(sCV, 1)
                            t.wait_ge(sCV, base + 5)
                            t.matmul(pH[0:1, 128:255], C.ones_col[0:64, 0:1], sqc[:, :],
                                     start=True, stop=True).then_inc(sCV, 1)
                            t.wait_ge(sCV, base + 12)
                            t.matmul(pH[0:64, 256:383], C.ones_rowF[0:1, 0:64],
                                     lnsc[0:1, 0:127], start=True, stop=True).then_inc(sCV, 1)
                            t.wait_ge(sCV, base + 13)
                            t.matmul(pH[0:64, 384:511], C.ones_rowF[0:1, 0:64],
                                     lnsc[0:1, 127:254], start=True, stop=True).then_inc(sCV, 1)
                            if kv == 1:
                                t.wait_ge(sCV, base + 17)
                                t.matmul(pC[0:127, 0:64], vtmp[:, :], C.ident[0:64, 0:64],
                                         start=True, stop=True).then_inc(sCV, 1)
                    for hl in range(HPG):
                        base = nq_base(s, hg, hl)
                        hcol = hl * NT
                        for tt in range(4):
                            t.wait_ge(sNQ, base + 4 * tt + 1)
                            t.matmul(pC[0:1, 0:TT], C.ones_col[0:64, 0:1], qsq[:, :],
                                     start=True, stop=True).then_inc(sNQ, 1)
                        t.wait_ge(sNQ, base + 17)
                        t.matmul(pC[0:1, 0:127], C.ones_col[0:64, 0:1], sqc[:, :],
                                 start=True, stop=True).then_inc(sNQ, 1)
                        for tt in range(4):
                            t.wait_ge(sNQ, base + 22 + 3 * tt + 1)
                            t.matmul(pC[0:1, 0:TT], C.ones_col[0:64, 0:1], qsq[:, :],
                                     start=True, stop=True).then_inc(sNQ, 1)
                    chunks = []
                    for hl in range(HPG):
                        for qb in range(4):
                            io = io_of(s, hg, hl, qb, False)
                            for n_ in range(4 * qb + 4):
                                ia = ia_base(s, hg, hl, qb, False) + n_
                                chunks.append(dict(hl=hl, qb=qb, comp=False, n_=n_,
                                                   ia=ia, io=io,
                                                   first=(qb == 0 and n_ == 0),
                                                   start=(n_ == 0), stop=(n_ == 4 * qb + 3)))
                        for qb in range(4):
                            io = io_of(s, hg, hl, qb, True)
                            ia = ia_base(s, hg, hl, qb, True)
                            chunks.append(dict(hl=hl, qb=qb, comp=True, n_=0,
                                               ia=ia, io=io, first=False,
                                               start=True, stop=True))

                    def emit_S(ch):
                        hl, ia = ch["hl"], ch["ia"]
                        if ch["first"]:
                            t.wait_ge(sNQ, nq_base(s, hg, hl) + NQ_HEAD)
                            t.wait_ge(sKO, s * NHG + hg + 1)
                        if ia >= 2:
                            t.wait_ge(sEXP, ia - 1)
                        if not ch["comp"]:
                            j = ch["n_"]
                            t.matmul(psS[ia % 2][0:128, 0:TT],
                                     Ke[0:65, hl * NT + j * 128: hl * NT + j * 128 + 128],
                                     Qe[0:65, hl * NT + ch["qb"] * TT: hl * NT + ch["qb"] * TT + TT],
                                     start=True, stop=True).then_inc(sMS2, 1)
                        else:
                            t.matmul(psS[ia % 2][0:127, 0:TT],
                                     kce[0:65, hl * 127: hl * 127 + 127],
                                     Qe[0:65, hl * NT + ch["qb"] * TT: hl * NT + ch["qb"] * TT + TT],
                                     start=True, stop=True).then_inc(sMS2, 1)

                    def emit_PV(ch):
                        hl, ia, io = ch["hl"], ch["ia"], ch["io"]
                        if ia in MASKED:
                            t.wait_ge(sPM, MASKED[ia])
                        else:
                            t.wait_ge(sEXP, ia + 1)
                        if ch["start"] and io >= 2:
                            t.wait_ge(sEV, io - 1)
                            t.wait_ge(sNR, io - 1)
                        if not ch["comp"]:
                            j = ch["n_"]
                            t.matmul(psO[io % 2][0:65, 0:TT],
                                     vt65[:, (j * HPG + hl) * 65: (j * HPG + hl) * 65 + 65],
                                     pexp[ia % 2][:, :],
                                     start=ch["start"], stop=ch["stop"]).then_inc(sMV, 1)
                        else:
                            t.matmul(psO[io % 2][0:65, 0:TT],
                                     vcp65[0:127, hl * 65: hl * 65 + 65],
                                     pexp[ia % 2][0:127, 0:TT],
                                     start=True, stop=True).then_inc(sMV, 1)

                    def emit_MR(io):
                        t.wait_ge(sNR, io + 1)
                        t.wait_ge(sEB, io)
                        t.matmul(pC[0:64, TT:2 * TT], C.ones_rowF[0:1, 0:64],
                                 rcp[:, :], start=True, stop=True).then_inc(sMR, 1)

                    seq_ = chunks + [None, None]
                    for k_, ch in enumerate(seq_):
                        if ch is not None:
                            emit_S(ch)
                        if k_ >= 1 and seq_[k_ - 1] is not None:
                            emit_PV(seq_[k_ - 1])
                        if k_ >= 2 and seq_[k_ - 2] is not None and seq_[k_ - 2]["stop"]:
                            emit_MR(seq_[k_ - 2]["io"])
                for tt in range(4):
                    llb = LL0[s] + tt * 50
                    if tt == 0:
                        t.wait_ge(sOD, (s + 1) * IO_SEQ)
                    t.wait_ge(sLL, llb + 0)
                    for c in range(NCH):
                        t.matmul(pC[0:1, 0:TT], C.ones_col[:, 0:1],
                                 lbuf[:, c * NT + tt * TT: c * NT + tt * TT + TT],
                                 start=(c == 0), stop=(c == 7)).then_inc(sLL, 1 if c == 7 else 0)
                    for c in range(NCH):
                        st = 3 if c == 0 else min(2 * c + 3, 16)
                        t.wait_ge(sLL, llb + st)
                        t.matmul(pC[0:1, TT:2 * TT], C.ones_col[:, 0:1],
                                 sql[:, (c % 2) * TT:(c % 2) * TT + TT],
                                 start=(c == 0), stop=(c == 7)).then_inc(sLL, 1)
                    t.wait_ge(sLL, llb + 24)
                    t.matmul(pC[0:128, 0:TT], C.ones_rowF[:, :],
                             lnsc[0:1, 2 * TT:3 * TT], start=True, stop=True).then_inc(sLL, 1)
                    t.wait_ge(sLL, llb + 25)
                    t.matmul(pC[0:128, TT:2 * TT], C.ones_rowF[:, :],
                             lnsc[0:1, 3 * TT:4 * TT], start=True, stop=True).then_inc(sLL, 1)
                gb = GA0[s]
                for m in range(8):
                    ip = ip0 + 96 + m
                    rot_guard(t, ip)
                    gg_g = s * NWG + 24 + m
                    if m == 0:
                        t.wait_ge(sLL, LL0[s] + 4 * 50)
                        t.wait_ge(sOD, (s + 1) * IO_SEQ)
                    t.wait_ge(sWG[gg_g % 2], WGV[gg_g])
                    dst = (pA if ip % 2 == 0 else pB)
                    for k in range(16):
                        src = lbuf[:, k * NT + NT - 1: k * NT + NT] if k < 8 else \
                              cbuf[:, (k - 8) * NT + NT - 1: (k - 8) * NT + NT]
                        t.matmul(dst[:, 0:1], wbuf[gg_g % 2][:, k * 128:(k + 1) * 128],
                                 src, start=(k == 0), stop=(k == 15)
                                 ).then_inc(sMP, 1 if k == 15 else 0)
                t.wait_ge(sGA, gb + 3)
                t.matmul(pC[0:1, 0:8], C.ones_col[:, 0:1], gyb[:, :],
                         start=True, stop=True).then_inc(sGA, 1)
                t.matmul(pC[0:1, 8:16], C.ones_col[:, 0:1], gy2[:, :],
                         start=True, stop=True).then_inc(sGA, 1)
                t.wait_ge(sGA, gb + 14)
                t.matmul(pC[0:128, 16:17], C.ones_rowF[:, :], gsc[0:1, 4:5],
                         start=True, stop=True).then_inc(sGA, 1)
                t.matmul(pC[0:128, 17:18], C.ones_rowF[:, :], gsc[0:1, 5:6],
                         start=True, stop=True).then_inc(sGA, 1)
                t.wait_ge(sGA, gb + 21)
                for e in range(2):
                    for c in range(8):
                        t.matmul(pC[0:1, 18 + e:19 + e], gw2b[:, c * 2 + e:c * 2 + e + 1],
                                 gyb[:, c:c + 1], start=(c == 0), stop=(c == 7)
                                 ).then_inc(sGA, 1 if c == 7 else 0)
                t.wait_ge(sGA, gb + 28)
                t.matmul(pC[0:128, 20:21], C.ones_rowF[:, :], gsc[0:1, 8:9],
                         start=True, stop=True).then_inc(sGA, 1)
                t.matmul(pC[0:128, 21:22], C.ones_rowF[:, :], gsc[0:1, 9:10],
                         start=True, stop=True).then_inc(sGA, 1)
                for irx in range(32):
                    b, tt = irx // 4, irx % 4
                    ip = ip0 + 104 + irx
                    ir = ru0 + irx
                    gg_g = s * NWG + 32 + b
                    rot_guard(t, ip)
                    if irx == 0:
                        t.wait_ge(sCB, 32 * (s + 1))
                    if tt == 0:
                        t.wait_ge(sWG[gg_g % 2], WGV[gg_g])
                    dst = (pA if ip % 2 == 0 else pB)
                    for k in range(NCH):
                        t.matmul(dst[:, :], wbuf[gg_g % 2][:, k * 128:(k + 1) * 128],
                                 lbuf[:, k * NT + tt * TT: k * NT + tt * TT + TT],
                                 start=(k == 0), stop=(k == 7)
                                 ).then_inc(sMP, 1 if k == 7 else 0)

            @b2.scalar
            def _(sc):
                for hg in range(NHG):
                    for hl in range(HPG):
                        for kv in range(2):
                            base = cv_base(s, hg, hl, kv)
                            sc.wait_ge(sCV, base + 2)
                            sc.activation(sqc[:, :], co[:, :], AF.Square).then_inc(sCV, 1)
                            sc.wait_ge(sCV, base + 9)
                            sc.activation(lnsc[0:1, 381:508], lnsc[0:1, 381:508],
                                          AF.Ln, bias=C.cc[0:1, 1:2]).then_inc(sCV, 1)
                            sc.wait_ge(sCV, base + 10)
                            sc.activation(lnsc[0:1, 0:127], lnsc[0:1, 381:508],
                                          AF.Exp, scale=C.cc[0:1, 2:3]).then_inc(sCV, 1)
                            sc.wait_ge(sCV, base + 16)
                            gdst = kce[0:64, hl * 127: hl * 127 + 127] if kv == 0 else vtmp[:, :]
                            sc.activation(gdst, xhh[:, 127:254], AF.Gelu,
                                          scale=sm_rows(C, "l1a_clw", 0, 0),
                                          bias=sm_rows(C, "l1a_clb", 0, 0)).then_inc(sCV, 1)
                    for hl in range(HPG):
                        base = nq_base(s, hg, hl)
                        hcol = hl * NT
                        if hl == 0:
                            sc.wait_ge(sCV, s * CV_SEQ + (hg + 1) * CV_HG)
                            sc.wait_ge(sEvP, evp0 + hg * 24 + 24)
                        for tt in range(4):
                            sc.wait_ge(sNQ, base + 4 * tt)
                            sc.activation(qsq[:, :], Ke[0:64, hcol + tt * TT: hcol + tt * TT + TT],
                                          AF.Square).then_inc(sNQ, 1)
                        sc.wait_ge(sNQ, base + 16)
                        sc.activation(sqc[:, :], kce[0:64, hl * 127: hl * 127 + 127],
                                      AF.Square).then_inc(sNQ, 1)
                        sc.wait_ge(sNQ, base + 20)
                        sc.activation(kmax[:, 1:2], kmax[:, 0:1], AF.Sqrt).then_inc(sNQ, 1)
                        for tt in range(4):
                            sc.wait_ge(sNQ, base + 22 + 3 * tt)
                            sc.activation(qsq[:, :], Qe[0:64, hcol + tt * TT: hcol + tt * TT + TT],
                                          AF.Square).then_inc(sNQ, 1)
                            sc.wait_ge(sNQ, base + 22 + 3 * tt + 2)
                            sc.activation(Qe[64:65, hcol + tt * TT: hcol + tt * TT + TT],
                                          pC[0:1, 0:TT], AF.Sqrt).then_inc(sNQ, 1)
                    sios = []
                    for hl in range(HPG):
                        for cp_ in (False, True):
                            for qb in range(4):
                                sios.append((hl, qb, cp_, io_of(s, hg, hl, qb, cp_),
                                             ia_base(s, hg, hl, qb, cp_),
                                             1 if cp_ else 4 * qb + 4))

                    def sc_exptile(pio_):
                        sc.wait_ge(sMR, pio_ + 1)
                        if pio_ >= 2:
                            sc.wait_ge(sOD, pio_ - 1)
                        sc.activation(tmpR[pio_ % 2][0:64, 0:TT], pC[0:64, TT:2 * TT],
                                      AF.Exp, scale=C.cc[0:64, 6:7]).then_inc(sEB, 1)

                    for k_, (hl, qb, cp_, io, ia0, nch) in enumerate(sios):
                        for n_ in range(nch):
                            ia = ia0 + n_
                            sc.wait_ge(sMS2, ia + 1)
                            if ia >= 2:
                                sc.wait_ge(sMV, ia - 1)
                            if not cp_:
                                sc.activation(pexp[ia % 2][:, :], psS[ia % 2][0:128, 0:TT],
                                              AF.Exp, scale=C.cc[:, 5:6]).then_inc(sEXP, 1)
                            else:
                                sc.activation(pexp[ia % 2][0:127, 0:TT], psS[ia % 2][0:127, 0:TT],
                                              AF.Exp, scale=C.cc[0:127, 5:6]).then_inc(sEXP, 1)
                        if k_ >= 1:
                            sc_exptile(sios[k_ - 1][3])
                        sc.wait_ge(sMV, ia0 + nch)
                        sc.wait_ge(sMR, io)
                        sc.activation(rcp[:, :], psO[io % 2][64:65, 0:TT],
                                      AF.Ln).then_inc(sNR, 1)
                    sc_exptile(sios[-1][3])
                for tt in range(4):
                    llb = LL0[s] + tt * 50
                    for c in range(NCH):
                        st = (1 + c) if c < 2 else 2 * c
                        sc.wait_ge(sLL, llb + st)
                        sc.activation(sql[:, (c % 2) * TT:(c % 2) * TT + TT],
                                      lbuf[:, c * NT + tt * TT: c * NT + tt * TT + TT],
                                      AF.Square).then_inc(sLL, 1)
                    sc.wait_ge(sLL, llb + 21)
                    sc.activation(lnsc[0:1, TT:2 * TT], lnsc[0:1, TT:2 * TT],
                                  AF.Ln, bias=C.cc[0:1, 1:2]).then_inc(sLL, 1)
                    sc.wait_ge(sLL, llb + 22)
                    sc.activation(lnsc[0:1, 2 * TT:3 * TT], lnsc[0:1, TT:2 * TT],
                                  AF.Exp, scale=C.cc[0:1, 2:3]).then_inc(sLL, 1)
                gb = GA0[s]
                sc.wait_ge(sGA, gb + 2)
                sc.activation(gy2[:, :], gyb[:, :], AF.Square).then_inc(sGA, 1)
                sc.wait_ge(sGA, gb + 11)
                sc.activation(gsc[0:1, 1:2], gsc[0:1, 1:2], AF.Ln,
                              bias=C.cc[0:1, 1:2]).then_inc(sGA, 1)
                sc.wait_ge(sGA, gb + 12)
                sc.activation(gsc[0:1, 4:5], gsc[0:1, 1:2], AF.Exp,
                              scale=C.cc[0:1, 2:3]).then_inc(sGA, 1)
                sc.wait_ge(sGA, gb + 20)
                sc.activation(gyb[:, :], gy[:, :], AF.Relu).then_inc(sGA, 1)
                sc.wait_ge(sGA, gb + 26)
                sc.activation(gsc[0:1, 8:9], gsc[0:1, 10:11], AF.Sigmoid).then_inc(sGA, 1)

            @b2.vector
            def _(v):
                for hg in range(NHG):
                    for ki in range(3):
                        for sub in range(2):
                            for tt in range(4):
                                ip = ip0 + hg * 24 + ki * 8 + sub * 4 + tt
                                v.wait_ge(sMP, ip + 1)
                                src = (pA if ip % 2 == 0 else pB)
                                oc2 = hg * 2 + sub
                                if ki == 0:
                                    if (sub, tt) == (0, 0) and (s > 0 or hg > 0):
                                        v.wait_ge(sMS2, s * S_SEQ + hg * S_HG)
                                    for half in range(2):
                                        hd = sub * 2 + half
                                        ins = v.tensor_scalar(
                                            Qe[0:64, hd * NT + tt * TT: hd * NT + tt * TT + TT],
                                            src[half * 64:half * 64 + 64, :],
                                            sm_rows(C, "l1a_cab", oc2, half * 64), None, ALU.add)
                                    ins.then_inc(sEvP, 1)
                                elif ki == 1:
                                    for half in range(2):
                                        hd = sub * 2 + half
                                        ins = v.tensor_scalar(
                                            Ke[0:64, hd * NT + tt * TT: hd * NT + tt * TT + TT],
                                            src[half * 64:half * 64 + 64, :],
                                            sm_rows(C, "l1a_cab", 8 + oc2, half * 64), None, ALU.add)
                                    ins.then_inc(sEvP, 1)
                                else:
                                    for half in range(2):
                                        hd = sub * 2 + half
                                        ins = v.tensor_scalar(
                                            vf[0:64, hd * NT + tt * TT: hd * NT + tt * TT + TT],
                                            src[half * 64:half * 64 + 64, :],
                                            sm_rows(C, "l1a_cab", 16 + oc2, half * 64), None, ALU.add)
                                    ins.then_inc(sEvP, 1)
                    v.wait_ge(sMV, s * S_SEQ + hg * S_HG)
                    v.memset(vt65[:, :], 1.0).then_inc(sVT, 1)
                    v.memset(Ke[64:65, :], 1.0)
                    v.memset(kce[64:65, :], 1.0)
                    v.memset(vcp65[:, :], 1.0).then_inc(sKO, 1)
                    v.wait_ge(sVT, s * NHG + hg + 1)
                    for hl in range(HPG):
                        for tc in range(16):
                            trs = s * TR_SEQ + hg * TR_HG + (hl * 16 + tc) * 2
                            v.wait_ge(sTR, trs + 1)
                            v.tensor_copy(vt65[:, (tc * HPG + hl) * 65: (tc * HPG + hl) * 65 + 64],
                                          pC[0:128, TT:TT + 64]).then_inc(sTR, 1)
                    for hl in range(HPG):
                        for kv in range(2):
                            base = cv_base(s, hg, hl, kv)
                            v.wait_ge(sCV, base + 1)
                            v.tensor_scalar(co[:, :], pH[0:64, 0:127],
                                            sm_rows(C, "l1a_cvb", 0, 0), None, ALU.add).then_inc(sCV, 1)
                            v.wait_ge(sCV, base + 4)
                            v.tensor_scalar(lnsc[0:1, 254:381], pH[0:1, 128:255],
                                            C.cc[0:1, 4:5], None, ALU.mult).then_inc(sCV, 1)
                            v.wait_ge(sCV, base + 6)
                            v.tensor_scalar(lnsc[0:1, 381:508], pH[0:1, 128:255],
                                            C.cc[0:1, 4:5], None, ALU.mult).then_inc(sCV, 1)
                            v.wait_ge(sCV, base + 7)
                            v.tensor_tensor(lnsc[0:1, 508:635], lnsc[0:1, 254:381],
                                            lnsc[0:1, 254:381], ALU.mult).then_inc(sCV, 1)
                            v.wait_ge(sCV, base + 8)
                            v.tensor_tensor(lnsc[0:1, 381:508], lnsc[0:1, 381:508],
                                            lnsc[0:1, 508:635], ALU.subtract).then_inc(sCV, 1)
                            v.wait_ge(sCV, base + 11)
                            v.tensor_tensor(lnsc[0:1, 127:254], lnsc[0:1, 254:381],
                                            lnsc[0:1, 0:127], ALU.mult).then_inc(sCV, 1)
                            v.wait_ge(sCV, base + 14)
                            v.tensor_tensor(xhh[:, 0:127], co[:, :], pH[0:64, 256:383],
                                            ALU.mult).then_inc(sCV, 1)
                            v.wait_ge(sCV, base + 15)
                            v.tensor_tensor(xhh[:, 127:254], xhh[:, 0:127], pH[0:64, 384:511],
                                            ALU.subtract).then_inc(sCV, 1)
                            if kv == 1:
                                v.wait_ge(sCV, base + 18)
                                v.tensor_copy(vcp65[0:127, hl * 65: hl * 65 + 64],
                                              pC[0:127, 0:64]).then_inc(sCV, 1)
                    for hl in range(HPG):
                        base = nq_base(s, hg, hl)
                        hcol = hl * NT
                        for tt in range(4):
                            v.wait_ge(sNQ, base + 4 * tt + 2)
                            v.reduce_max(kmax[:, 3:4], pC[0:1, 0:TT], axis=AX.X).then_inc(sNQ, 1)
                            v.wait_ge(sNQ, base + 4 * tt + 3)
                            if tt == 0:
                                v.tensor_copy(kmax[:, 0:1], kmax[:, 3:4]).then_inc(sNQ, 1)
                            else:
                                v.tensor_tensor(kmax[:, 0:1], kmax[:, 0:1], kmax[:, 3:4],
                                                ALU.max).then_inc(sNQ, 1)
                        v.wait_ge(sNQ, base + 18)
                        v.reduce_max(kmax[:, 3:4], pC[0:1, 0:127], axis=AX.X).then_inc(sNQ, 1)
                        v.wait_ge(sNQ, base + 19)
                        v.tensor_tensor(kmax[:, 0:1], kmax[:, 0:1], kmax[:, 3:4],
                                        ALU.max).then_inc(sNQ, 1)
                        v.wait_ge(sNQ, base + 21)
                        v.tensor_scalar(kmax[:, 2:3], kmax[:, 1:2], C.cc[0:1, 6:7], None,
                                        ALU.mult).then_inc(sNQ, 1)
                        for tt in range(4):
                            v.wait_ge(sNQ, base + 34 + tt)
                            v.tensor_scalar(Qe[64:65, hcol + tt * TT: hcol + tt * TT + TT],
                                            Qe[64:65, hcol + tt * TT: hcol + tt * TT + TT],
                                            kmax[0:1, 2:3], None, ALU.mult).then_inc(sNQ, 1)
                    vios = []
                    for hl in range(HPG):
                        h = hg * HPG + hl
                        for cp_ in (False, True):
                            for qb in range(4):
                                vios.append(dict(
                                    h=h, qb=qb, comp=cp_,
                                    io=io_of(s, hg, hl, qb, cp_),
                                    cd=ia_base(s, hg, hl, qb, cp_) + (1 if cp_ else 4 * qb + 4)))

                    def v_evac(el):
                        io = el["io"]
                        v.wait_ge(sMV, el["cd"])
                        v.tensor_copy(sct[io % 2][:, :],
                                      psO[io % 2][0:64, 0:TT]).then_inc(sEV, 1)

                    def v_mult(el):
                        h, qb, io = el["h"], el["qb"], el["io"]
                        dstb = cbuf if el["comp"] else lbuf
                        v.wait_ge(sEB, io + 1)
                        v.wait_ge(sEV, io + 1)
                        v.tensor_tensor(dstb[(h % 2) * 64:(h % 2) * 64 + 64,
                                             (h // 2) * NT + qb * TT:(h // 2) * NT + qb * TT + TT],
                                        sct[io % 2][:, :], tmpR[io % 2][0:64, 0:TT],
                                        ALU.mult).then_inc(sOD, 1)

                    for k_, el in enumerate(vios):
                        v_evac(el)
                        if k_ >= 1:
                            v_mult(vios[k_ - 1])
                    v_mult(vios[-1])
                for tt in range(4):
                    llb = LL0[s] + tt * 50
                    v.wait_ge(sLL, llb + 17)
                    v.tensor_scalar(lnsc[0:1, 0:TT], pC[0:1, 0:TT], C.cc[0:1, 0:1],
                                    None, ALU.mult).then_inc(sLL, 1)
                    v.wait_ge(sLL, llb + 18)
                    v.tensor_scalar(lnsc[0:1, TT:2 * TT], pC[0:1, TT:2 * TT], C.cc[0:1, 0:1],
                                    None, ALU.mult).then_inc(sLL, 1)
                    v.wait_ge(sLL, llb + 19)
                    v.tensor_tensor(lnsc[0:1, 3 * TT:4 * TT], lnsc[0:1, 0:TT],
                                    lnsc[0:1, 0:TT], ALU.mult).then_inc(sLL, 1)
                    v.wait_ge(sLL, llb + 20)
                    v.tensor_tensor(lnsc[0:1, TT:2 * TT], lnsc[0:1, TT:2 * TT],
                                    lnsc[0:1, 3 * TT:4 * TT], ALU.subtract).then_inc(sLL, 1)
                    v.wait_ge(sLL, llb + 23)
                    v.tensor_tensor(lnsc[0:1, 3 * TT:4 * TT], lnsc[0:1, 0:TT],
                                    lnsc[0:1, 2 * TT:3 * TT], ALU.mult).then_inc(sLL, 1)
                    for c in range(NCH):
                        v.wait_ge(sLL, llb + 26 + 3 * c)
                        v.tensor_tensor(lnx[:, 0:TT], lbuf[:, c * NT + tt * TT: c * NT + tt * TT + TT],
                                        pC[0:128, 0:TT], ALU.mult).then_inc(sLL, 1)
                        v.wait_ge(sLL, llb + 27 + 3 * c)
                        v.tensor_tensor(lnx[:, TT:2 * TT], lnx[:, 0:TT],
                                        pC[0:128, TT:2 * TT], ALU.subtract).then_inc(sLL, 1)
                        v.wait_ge(sLL, llb + 28 + 3 * c)
                        v.tensor_scalar(lbuf[:, c * NT + tt * TT: c * NT + tt * TT + TT],
                                        lnx[:, TT:2 * TT], sm(C, "l1a_lnw", c), sm(C, "l1a_lnb", c),
                                        ALU.mult, ALU.add).then_inc(sLL, 1)
                gb = GA0[s]
                for m in range(8):
                    ip = ip0 + 96 + m
                    v.wait_ge(sMP, ip + 1)
                    src = (pA if ip % 2 == 0 else pB)
                    v.tensor_copy(gy[:, m:m + 1], src[:, 0:1]).then_inc(sEvP, 1)
                v.wait_ge(sEvP, evp0 + EVP_SEQ)
                v.tensor_tensor(gy[:, :], gy[:, :], sm(C, "l1a_gb1", 0, 8), ALU.add).then_inc(sGA, 1)
                v.wait_ge(sGA, gb + 1)
                v.tensor_copy(gyb[:, :], gy[:, :]).then_inc(sGA, 1)
                v.wait_ge(sGA, gb + 5)
                v.reduce_sum(gsc[0:1, 16:17], pC[0:1, 0:8], axis=AX.X).then_inc(sGA, 1)
                v.wait_ge(sGA, gb + 6)
                v.tensor_scalar(gsc[0:1, 0:1], gsc[0:1, 16:17], C.cc[0:1, 0:1], None,
                                ALU.mult).then_inc(sGA, 1)
                v.wait_ge(sGA, gb + 7)
                v.reduce_sum(gsc[0:1, 17:18], pC[0:1, 8:16], axis=AX.X).then_inc(sGA, 1)
                v.wait_ge(sGA, gb + 8)
                v.tensor_scalar(gsc[0:1, 1:2], gsc[0:1, 17:18], C.cc[0:1, 0:1], None,
                                ALU.mult).then_inc(sGA, 1)
                v.wait_ge(sGA, gb + 9)
                v.tensor_tensor(gsc[0:1, 2:3], gsc[0:1, 0:1], gsc[0:1, 0:1],
                                ALU.mult).then_inc(sGA, 1)
                v.wait_ge(sGA, gb + 10)
                v.tensor_tensor(gsc[0:1, 1:2], gsc[0:1, 1:2], gsc[0:1, 2:3],
                                ALU.subtract).then_inc(sGA, 1)
                v.wait_ge(sGA, gb + 13)
                v.tensor_tensor(gsc[0:1, 5:6], gsc[0:1, 0:1], gsc[0:1, 4:5],
                                ALU.mult).then_inc(sGA, 1)
                v.wait_ge(sGA, gb + 16)
                v.tensor_copy(gcol[:, 0:2], pC[0:128, 16:18]).then_inc(sGA, 1)
                v.wait_ge(sGA, gb + 17)
                v.tensor_scalar(gy[:, :], gy[:, :], gcol[:, 0:1], gcol[:, 1:2],
                                ALU.mult, ALU.subtract).then_inc(sGA, 1)
                v.wait_ge(sGA, gb + 18)
                v.tensor_tensor(gy[:, :], gy[:, :], sm(C, "l1a_glw", 0, 8), ALU.mult).then_inc(sGA, 1)
                v.wait_ge(sGA, gb + 19)
                v.tensor_tensor(gy[:, :], gy[:, :], sm(C, "l1a_glb", 0, 8), ALU.add).then_inc(sGA, 1)
                v.wait_ge(sGA, gb + 23)
                v.tensor_copy(gsc[0:1, 8:10], pC[0:1, 18:20]).then_inc(sGA, 1)
                v.wait_ge(sGA, gb + 24)
                v.tensor_tensor(gsc[0:1, 8:10], gsc[0:1, 8:10],
                                C.smalls[0:1, SM_COLS["l1a_gb2"][0]:SM_COLS["l1a_gb2"][0] + 2],
                                ALU.add).then_inc(sGA, 1)
                v.wait_ge(sGA, gb + 25)
                v.tensor_tensor(gsc[0:1, 10:11], gsc[0:1, 8:9], gsc[0:1, 9:10],
                                ALU.subtract).then_inc(sGA, 1)
                v.wait_ge(sGA, gb + 27)
                v.tensor_scalar(gsc[0:1, 9:10], gsc[0:1, 8:9], C.cc[0:1, 6:7], C.cc[0:1, 7:8],
                                ALU.mult, ALU.add).then_inc(sGA, 1)
                v.wait_ge(sGA, gb + 30)
                v.tensor_copy(gcol[:, 2:4], pC[0:128, 20:22]).then_inc(sGA, 1)
                for ctile in range(32):
                    c, tt = ctile // 4, ctile % 4
                    cb = s * 32 + ctile
                    if ctile == 0:
                        v.wait_ge(sGA, gb + 31)
                    if cb >= 2:
                        v.wait_ge(sCB, cb - 1)
                    v.tensor_scalar(lnx[:, (ctile % 2) * TT:(ctile % 2) * TT + TT],
                                    lbuf[:, c * NT + tt * TT: c * NT + tt * TT + TT],
                                    gcol[:, 2:3], None, ALU.mult).then_inc(sCB2, 1)
                    v.wait_ge(sCB2, cb + 1)
                    v.scalar_tensor_tensor(lbuf[:, c * NT + tt * TT: c * NT + tt * TT + TT],
                                           cbuf[:, c * NT + tt * TT: c * NT + tt * TT + TT],
                                           gcol[:, 3:4],
                                           lnx[:, (ctile % 2) * TT:(ctile % 2) * TT + TT],
                                           ALU.mult, ALU.add).then_inc(sCB, 1)
                for irx in range(32):
                    b, tt = irx // 4, irx % 4
                    ip = ip0 + 104 + irx
                    ir = ru0 + irx
                    src = (pA if ip % 2 == 0 else pB)
                    v.wait_ge(sMP, ip + 1)
                    v.wait_ge(sRE[ir % 2], 16 * (ir // 2 + 1))
                    if ir >= 2:
                        v.wait_ge(sSO[ir % 2], 16 * (ir // 2))
                    v.scalar_tensor_tensor(tmpR[ir % 2][:, :], src[:, :],
                                           sm(C, "l1a_cpb", b), rbufE[ir % 2][:, :],
                                           ALU.add, ALU.add).then_inc(sRU, 1)
        P.pop()
    P.pop()


WNAMES = ("l0f_w1T", "l0f_w2T", "l0f_h1T", "l0f_h2T", "l0f_w3T",
          "l0a_qw", "l0a_kw", "l0a_vw", "l0a_ow",
          "l1a_caw", "l1a_cpw", "l1a_cvw", "l1a_gw1", "l1a_gw2",
          "l1m_w1", "l1m_w2", "l1m_w3", "l1m_wg")


def build_nc(smalls_shape, wshapes, masks_shape):
    nc = bass.Bass("TRN2", target_bir_lowering=False, debug=False)
    dr = {}
    dr["smalls"] = nc.dram_tensor("smalls", list(smalls_shape), F32, kind="ExternalInput").ap()
    dr["ident"] = nc.dram_tensor("ident", [128, 128], BF16, kind="ExternalInput").ap()
    dr["masks"] = nc.dram_tensor("masks", list(masks_shape), BF16, kind="ExternalInput").ap()
    dr["identF"] = nc.dram_tensor("identF", [128, 128], F32, kind="ExternalInput").ap()
    dr["SEL"] = nc.dram_tensor("SEL", [8, 1024], F32, kind="ExternalInput").ap()
    dr["x_in"] = nc.dram_tensor("x_in", [128, NCH * NTOK], F32, kind="ExternalInput").ap()
    dr["y_out"] = nc.dram_tensor("y_out", [128, NCH * NTOK], F32, kind="ExternalOutput").ap()
    dr["R"] = nc.dram_tensor("R", [128, NCH * NTOK], F32, kind="Internal").ap()
    for n in WNAMES:
        dr[n] = nc.dram_tensor(n, list(wshapes[n]), F32, kind="ExternalInput").ap()
        dr["b_" + n] = nc.dram_tensor("b_" + n, list(wshapes[n]), BF16, kind="Internal").ap()

    with ExitStack() as ctx:
        P = Prog(nc, ctx)
        C = emit_setup(P, dr, smalls_shape[1])
        emit_cast(P, [(dr[n], dr["b_" + n]) for n in WNAMES if n.startswith("l0")])
        ctiles = []
        for n in WNAMES:
            if n.startswith("l0"):
                continue
            src, dst = dr[n], dr["b_" + n]
            rows, cols = src.tensor.shape
            for r0 in range(0, rows, 128):
                for c0 in range(0, cols, 512):
                    ctiles.append((src, dst, r0, c0, min(512, cols - c0)))
        emit_l0ffn(P, C, {
            "R_in": dr["x_in"], "R_out": dr["R"],
            "bw1T": dr["b_l0f_w1T"], "bw2T": dr["b_l0f_w2T"],
            "bh1T": dr["b_l0f_h1T"], "bh2T": dr["b_l0f_h2T"], "bw3T": dr["b_l0f_w3T"],
        }, tag="f0")
        emit_l0attn(P, C, {
            "R_in": dr["R"], "R_out": dr["R"],
            "bqw": dr["b_l0a_qw"], "bkw": dr["b_l0a_kw"],
            "bvw": dr["b_l0a_vw"], "bow": dr["b_l0a_ow"],
            "cast_tiles": ctiles,
        }, tag="a0")
        emit_l0ffn(P, C, {
            "R_in": dr["R"], "R_out": dr["R"],
            "bw1T": dr["b_l0f_w1T"], "bw2T": dr["b_l0f_w2T"],
            "bh1T": dr["b_l0f_h1T"], "bh2T": dr["b_l0f_h2T"], "bw3T": dr["b_l0f_w3T"],
        }, tag="f1")
        emit_moe_dense(P, C, {
            "R_in": dr["R"], "R_out": dr["R"],
            "bw1": dr["b_l1m_w1"], "bw2": dr["b_l1m_w2"], "bw3": dr["b_l1m_w3"],
            "bwg": dr["b_l1m_wg"], "identF": dr["identF"], "SEL": dr["SEL"],
        }, tag="m0")
        emit_l1attn(P, C, {
            "R_in": dr["R"], "R_out": dr["R"],
            "bcaw": dr["b_l1a_caw"], "bcpw": dr["b_l1a_cpw"], "bcvw": dr["b_l1a_cvw"],
            "bgw1": dr["b_l1a_gw1"], "bgw2": dr["b_l1a_gw2"],
        }, tag="x1")
        emit_moe_dense(P, C, {
            "R_in": dr["R"], "R_out": dr["y_out"],
            "bw1": dr["b_l1m_w1"], "bw2": dr["b_l1m_w2"], "bw3": dr["b_l1m_w3"],
            "bwg": dr["b_l1m_wg"], "identF": dr["identF"], "SEL": dr["SEL"],
        }, tag="m1")
    return nc


def kernel(**inputs):
    smalls = pack_smalls(inputs)
    weights = prep_weights(inputs)
    ident, masks = make_consts()
    identF = np.eye(128, dtype=np.float32)
    SEL = np.zeros((8, 8 * 128), np.float32)
    for e in range(8):
        SEL[e, e * 128:(e + 1) * 128] = 1.0
    x = np.asarray(inputs["L0_attn_x"], np.float32)

    wshapes = {n: weights[n].shape for n in WNAMES}
    nc = build_nc(smalls.shape, wshapes, masks.shape)

    in_maps = []
    for core in range(NCORE):
        im = {"smalls": smalls, "ident": ident, "masks": masks,
              "identF": identF, "SEL": SEL,
              "x_in": x_to_fm(x[2 * core:2 * core + 2])}
        for n in WNAMES:
            im[n] = weights[n]
        in_maps.append(im)

    res = run_bass_kernel_spmd(nc, in_maps, core_ids=list(range(NCORE)))
    out = np.concatenate([fm_to_x(np.asarray(res.results[i]["y_out"]))
                          for i in range(NCORE)], axis=0)
    return out


# version 09 (final)
# speedup vs baseline: 1.0759x; 1.0759x over previous
import numpy as np
import ml_dtypes
from contextlib import ExitStack
import concourse.bass as bass
import concourse.mybir as mybir
from concourse.bass_utils import run_bass_kernel_spmd

F32 = mybir.dt.float32
BF16 = mybir.dt.bfloat16
I32 = mybir.dt.int32
AF = mybir.ActivationFunctionType
ALU = mybir.AluOpType
AX = mybir.AxisListType
BF16NP = ml_dtypes.bfloat16

NCORE = 8
NSEQ = 2
T = 2048
NTOK = NSEQ * T
D = 1024
NCH = D // 128
HID = 4096
NHID = HID // 128
QT = 1024
NQ = NTOK // QT
TT = 512


class Sem:
    def __init__(self, h):
        self.h = h


class Prog:
    def __init__(self, nc, ctx):
        self.nc = nc
        self.ctx = ctx
        self.cur = ctx
        self.sems = {}
        self.ln_tiles = 0

    def sem(self, name):
        if name not in self.sems:
            h = self.nc.alloc_semaphore(name=name)
            self.sems[name] = Sem(h)
            if self.cur is not self.ctx and not name.startswith("LN_"):
                self._phase_sems[-1].append(name)
        return self.sems[name].h

    def sb(self, name, shape, dt):
        return self.cur.enter_context(self.nc.sbuf_tensor(name, shape, dt))

    def ps(self, name, shape, dt=F32):
        return self.cur.enter_context(self.nc.psum_tensor(name, shape, dt))

    def push(self):
        if not hasattr(self, "_stack"):
            self._stack = []
            self._phase_sems = []
        self._stack.append(self.cur)
        self._phase_sems.append([])
        self.cur = ExitStack()
        return self.cur

    def pop(self):
        names = self._phase_sems.pop()
        if names:
            hs = [self.sems[n].h for n in names]
            self.nc.clear_and_free_semaphores(hs)
            for n in names:
                del self.sems[n]
        self.cur.close()
        self.cur = self._stack.pop()



def _fm(vec):
    n = vec.shape[0] // 128
    return np.ascontiguousarray(vec.reshape(n, 128).T)


def pack_l0_attn_w(inputs, proj):
    Wv = np.asarray(inputs[f"L0_attn_{proj}_Wv"], np.float32)
    Wr = np.asarray(inputs[f"L0_attn_{proj}_Wr"], np.float32)
    Wl = np.asarray(inputs[f"L0_attn_{proj}_Wl"], np.float32)
    blocks, meta = [], []
    for b in range(8):
        lst = []
        if b > 0:
            lst.append((b - 1, len(blocks))); blocks.append(Wr[b - 1])
        lst.append((b, len(blocks))); blocks.append(Wv[b])
        if b < 7:
            lst.append((b + 1, len(blocks))); blocks.append(Wl[b])
        meta.append(lst)
    return np.ascontiguousarray(np.concatenate(blocks, axis=0)), meta

L0A_META = [
    [(0, 0), (1, 1)],
    [(0, 2), (1, 3), (2, 4)],
    [(1, 5), (2, 6), (3, 7)],
    [(2, 8), (3, 9), (4, 10)],
    [(3, 11), (4, 12), (5, 13)],
    [(4, 14), (5, 15), (6, 16)],
    [(5, 17), (6, 18), (7, 19)],
    [(6, 20), (7, 21)],
]


def make_consts():
    Dset = [0, 1, 2, 3, 4, 8, 16, 32, 64, 128, 256, 512, 1024]
    p = np.arange(128)[:, None]
    f = np.arange(128)[None, :]
    m_self = np.isin(f - p, Dset).astype(np.float32)
    m_prev = np.isin(f - p + 128, Dset).astype(np.float32)
    m_diag = (f == p).astype(np.float32)
    tri = (p <= f).astype(np.float32)
    parts = [m_self, m_prev, m_diag, tri]
    f2 = np.arange(512)[None, :]
    for r in range(4):
        parts.append((p <= f2 - 128 * r).astype(np.float32))
    masks = np.concatenate(parts, axis=1).astype(BF16NP)
    ident = np.eye(128, dtype=BF16NP)
    return ident, masks


SM_COLS = {}

def pack_smalls(inputs):
    global SM_COLS
    cols = []
    SM_COLS = {}

    def add(name, arr2d):
        SM_COLS[name] = (len(cols), arr2d.shape[1])
        for j in range(arr2d.shape[1]):
            cols.append(np.asarray(arr2d[:, j], np.float32))

    add("l0f_s1", _fm(np.asarray(inputs["L0_ffn_w1_s"], np.float32)))
    add("l0f_s2", _fm(np.asarray(inputs["L0_ffn_w2_s"], np.float32)))
    add("l0f_sh1", _fm(np.asarray(inputs["L0_ffn_h1_s"], np.float32)))
    add("l0f_sh2", _fm(np.asarray(inputs["L0_ffn_h2_s"], np.float32)))
    add("l0f_s3", _fm(np.asarray(inputs["L0_ffn_w3_s"], np.float32)))
    for pj in ("q", "k", "v", "o"):
        add(f"l0a_{pj}b", _fm(np.asarray(inputs[f"L0_attn_{pj}_b"], np.float32)))
    add("l1a_cab", _fm(np.asarray(inputs["L1_attn_c_attn_b"], np.float32)))
    add("l1a_cpb", _fm(np.asarray(inputs["L1_attn_c_proj_b"], np.float32)))
    add("l1a_lnw", _fm(np.asarray(inputs["L1_attn_ln_local_w"], np.float32)))
    add("l1a_lnb", _fm(np.asarray(inputs["L1_attn_ln_local_b"], np.float32)))
    add("l1a_gb1", _fm(np.asarray(inputs["L1_attn_gate_b1"], np.float32)))
    add("l1a_glw", _fm(np.asarray(inputs["L1_attn_gate_ln_w"], np.float32)))
    add("l1a_glb", _fm(np.asarray(inputs["L1_attn_gate_ln_b"], np.float32)))
    cb = np.asarray(inputs["L1_attn_conv_b"], np.float32)
    add("l1a_cvb", np.concatenate([cb, cb])[:, None])
    cw = np.asarray(inputs["L1_attn_comp_ln_w"], np.float32)
    add("l1a_clw", np.concatenate([cw, cw])[:, None])
    clb = np.asarray(inputs["L1_attn_comp_ln_b"], np.float32)
    add("l1a_clb", np.concatenate([clb, clb])[:, None])
    gb2 = np.asarray(inputs["L1_attn_gate_b2"], np.float32)
    z = np.zeros((128, 2), np.float32)
    z[0, 0] = gb2[0]; z[0, 1] = gb2[1]
    add("l1a_gb2", z)
    return np.ascontiguousarray(np.stack(cols, axis=1))


def prep_weights(inputs):
    w = {}
    w["l0f_w1T"] = np.ascontiguousarray(np.asarray(inputs["L0_ffn_w1_w"], np.float32).T)
    w["l0f_w2T"] = np.ascontiguousarray(np.asarray(inputs["L0_ffn_w2_w"], np.float32).T)
    w["l0f_h1T"] = np.ascontiguousarray(np.asarray(inputs["L0_ffn_h1_w"], np.float32).T)
    w["l0f_h2T"] = np.ascontiguousarray(np.asarray(inputs["L0_ffn_h2_w"], np.float32).T)
    w["l0f_w3T"] = np.ascontiguousarray(np.asarray(inputs["L0_ffn_w3_w"], np.float32).T)
    for pj in ("q", "k", "v", "o"):
        w[f"l0a_{pj}w"], _ = pack_l0_attn_w(inputs, pj)
    w["l1a_caw"] = np.asarray(inputs["L1_attn_c_attn_w"], np.float32)
    w["l1a_cpw"] = np.asarray(inputs["L1_attn_c_proj_w"], np.float32)
    w["l1a_cvw"] = np.ascontiguousarray(
        np.asarray(inputs["L1_attn_conv_w"], np.float32).transpose(2, 1, 0).reshape(32 * 64, 64))
    w["l1a_gw1"] = np.asarray(inputs["L1_attn_gate_w1"], np.float32)
    w["l1a_gw2"] = np.asarray(inputs["L1_attn_gate_w2"], np.float32)
    w["l1m_wg"] = np.asarray(inputs["L1_ffn_Wg"], np.float32)
    w["l1m_w1"] = np.ascontiguousarray(np.asarray(inputs["L1_ffn_W1"], np.float32).reshape(8 * 1024, 2048))
    w["l1m_w2"] = np.ascontiguousarray(np.asarray(inputs["L1_ffn_W2"], np.float32).reshape(8 * 1024, 2048))
    w["l1m_w3"] = np.ascontiguousarray(np.asarray(inputs["L1_ffn_W3"], np.float32).reshape(8 * 2048, 1024))
    return w


def x_to_fm(x_core):
    n = np.asarray(x_core, np.float32).reshape(NTOK, NCH, 128)
    return np.ascontiguousarray(n.transpose(2, 1, 0).reshape(128, NCH * NTOK))


def fm_to_x(fm):
    n = fm.reshape(128, NCH, NTOK).transpose(2, 1, 0)
    return np.ascontiguousarray(n.reshape(NSEQ, T, D))



class Common:
    pass


def emit_setup(P, dr, n_smalls):
    nc = P.nc
    C = Common()
    C.smalls = P.sb("sb_smalls", [128, n_smalls], F32)
    C.ident = P.sb("sb_ident", [128, 128], BF16)
    C.masks = P.sb("sb_masks", [128, 2560], BF16)
    C.ones_col = P.sb("ones_col", [128, 1], BF16)
    C.ones_row = P.sb("ones_row", [1, 128], BF16)
    C.ones_rowF = P.sb("ones_rowF", [1, 128], F32)
    C.cc = P.sb("cc", [128, 12], F32)
    C.hs3 = P.sb("hs3", [128, 8], F32)
    s = P.sem("setup")
    with nc.Block() as block:
        @block.sync
        def _(sync):
            sync.dma_start(C.smalls[:, :], dr["smalls"]).then_inc(s, 16)
            sync.dma_start(C.ident[:, :], dr["ident"]).then_inc(s, 16)
            sync.dma_start(C.masks[:, :], dr["masks"]).then_inc(s, 16)

        @block.vector
        def _(vector):
            vector.memset(C.ones_col[:, :], 1.0)
            vector.memset(C.ones_row[:, :], 1.0)
            vector.memset(C.ones_rowF[:, :], 1.0)
            vector.memset(C.cc[:, 0:1], 1.0 / 1024.0)
            vector.memset(C.cc[:, 1:2], 1e-5)
            vector.memset(C.cc[:, 2:3], -0.5)
            vector.memset(C.cc[:, 3:4], 0.5)
            vector.memset(C.cc[:, 4:5], 1.0 / 64.0)
            vector.memset(C.cc[:, 5:6], 0.125)
            vector.memset(C.cc[:, 6:7], -1.0)
            vector.memset(C.cc[:, 7:8], 1.0)
            vector.memset(C.cc[:, 8:9], -1e30).then_inc(s, 1)
            vector.wait_ge(s, 49)
            vector.tensor_scalar(C.hs3[:, :], C.smalls[:, SM_COLS["l0f_s3"][0]:SM_COLS["l0f_s3"][0] + 8],
                                 C.cc[:, 3:4], None, ALU.mult)
    return C


def sm(C, name, j0=0, nj=1):
    c0, n = SM_COLS[name]
    return C.smalls[:, c0 + j0: c0 + j0 + nj]



def emit_cast(P, pairs, tag="cast"):
    nc = P.nc
    P.push()
    CW = 2048
    NB = 4
    buff = [P.sb(f"{tag}_f{i}", [128, CW], F32) for i in range(NB)]
    bufb = [P.sb(f"{tag}_b{i}", [128, CW], BF16) for i in range(NB)]
    s_in = [P.sem(f"{tag}_in{i}") for i in range(NB)]
    s_cv = P.sem(f"{tag}_cv")
    s_out = [P.sem(f"{tag}_out{i}") for i in range(NB)]

    tiles = []
    for src, dst in pairs:
        rows, cols = src.tensor.shape
        assert rows % 128 == 0, (rows, cols)
        for r0 in range(0, rows, 128):
            for c0 in range(0, cols, CW):
                cw = min(CW, cols - c0)
                tiles.append((src, dst, r0, c0, cw))

    with nc.Block() as block:
        def loads(eng, par):
            for i, (src, dst, r0, c0, cw) in enumerate(tiles):
                if i % 2 != par:
                    continue
                if i >= NB:
                    eng.wait_ge(s_cv, i - NB + 1)
                eng.dma_start(buff[i % NB][:, 0:cw],
                              src[r0:r0 + 128, c0:c0 + cw]).then_inc(s_in[i % NB], 16)

        @block.sync
        def _(sync):
            loads(sync, 0)

        @block.scalar
        def _(scalar):
            loads(scalar, 1)

        @block.vector
        def _(vector):
            for i, (src, dst, r0, c0, cw) in enumerate(tiles):
                vector.wait_ge(s_in[i % NB], 16 * (i // NB + 1))
                if i >= NB:
                    vector.wait_ge(s_out[i % NB], 16 * (i // NB))
                vector.tensor_copy(bufb[i % NB][:, 0:cw], buff[i % NB][:, 0:cw]).then_inc(s_cv, 1)

        @block.gpsimd
        def _(gpsimd):
            for i, (src, dst, r0, c0, cw) in enumerate(tiles):
                gpsimd.wait_ge(s_cv, i + 1)
                gpsimd.dma_start(dst[r0:r0 + 128, c0:c0 + cw],
                                 bufb[i % NB][:, 0:cw]).then_inc(s_out[i % NB], 16)
            n = len(tiles)
            for b in range(NB):
                nb_ = (n - b + NB - 1) // NB
                if nb_ > 0:
                    gpsimd.wait_ge(s_out[b], 16 * nb_)
    P.pop()




class LNEmit:

    def __init__(self, P, C, tag, R_in, ntiles, rcol, xn_ap, ps3=None):
        self.P, self.C, self.tag = P, C, tag
        self.R_in, self.ntiles, self.rcol, self.xn_ap = R_in, ntiles, rcol, xn_ap
        self.rbuf = P.sb(f"{tag}_lnr", [128, NCH * TT], F32)
        self.atmp = P.sb(f"{tag}_lna", [128, 4 * TT], F32)
        self.xbb = [P.sb(f"{tag}_lnx{i}", [128, TT], BF16) for i in range(2)]
        self.sqb = P.sb(f"{tag}_lnq", [128, 2 * TT], BF16)
        self.brow = P.sb(f"{tag}_lnb", [1, 1024], BF16)
        if ps3 is not None and len(ps3) == 4:
            a_, b_, lo_, hi_ = ps3
            self.ps_s = a_[0:1, 0:TT]
            self.ps_q = b_[0:1, 0:TT]
            self.ps_b_lo = lo_[:, 0:TT]
            self.ps_b_hi = hi_[:, 0:TT]
        elif ps3 is not None:
            a_, b_, c_ = ps3
            self.ps_s = a_[0:1, 0:TT]
            self.ps_q = b_[0:1, 0:TT]
            self.ps_b_lo = c_[:, 0:TT]
            self.ps_b_hi = c_[:, TT:2 * TT]
        else:
            self.ps_s = P.ps(f"{tag}_lnps", [1, TT])[:, :]
            self.ps_q = P.ps(f"{tag}_lnpq", [1, TT])[:, :]
            pb = P.ps(f"{tag}_lnpb", [128, 1024])
            self.ps_b_lo = pb[:, 0:TT]
            self.ps_b_hi = pb[:, TT:2 * TT]
        self.sR = P.sem("LN_R"); self.sXB = P.sem("LN_XB")
        self.sSQ = P.sem("LN_SQ"); self.sAs = P.sem("LN_As")
        self.sAq = P.sem("LN_Aq"); self.sST = P.sem("LN_ST")
        self.sRS = P.sem("LN_RS"); self.sBR = P.sem("LN_BR")
        self.sBC = P.sem("LN_BC"); self.sXN = P.sem("LN_XN")
        self.sDV = P.sem("LN_DV"); self.sDV2 = P.sem("LN_DV2")
        self.sDV3 = P.sem("LN_DV3"); self.sAC = P.sem("LN_AC")
        self.a0 = P.ln_tiles
        P.ln_tiles += ntiles

    def gpsimd(self, g, a):
        aa = self.a0 + a
        if aa >= 1:
            g.wait_ge(self.sXN, aa)
        for c in range(NCH):
            g.dma_start(self.rbuf[:, c * TT:(c + 1) * TT],
                        self.R_in[:, self.rcol(a, c):self.rcol(a, c) + TT]).then_inc(self.sR, 16)

    def pe(self, t, a, extra_wait=None):
        C = self.C
        if extra_wait is not None:
            extra_wait(t, a)
        aa = self.a0 + a
        for c in range(NCH):
            idx = aa * 8 + c
            t.wait_ge(self.sXB, idx + 1)
            if c == 0:
                t.wait_ge(self.sST, aa)
            t.matmul(self.ps_s, C.ones_col[:, :], self.xbb[c % 2][:, :],
                     start=(c == 0), stop=(c == 7)).then_inc(self.sAs, 1)
        for c in range(NCH):
            idx = aa * 8 + c
            t.wait_ge(self.sSQ, idx + 1)
            if c == 0:
                t.wait_ge(self.sST, aa)
            t.matmul(self.ps_q, C.ones_col[:, :], self.sqb[:, (c % 2) * TT:(c % 2) * TT + TT],
                     start=(c == 0), stop=(c == 7)).then_inc(self.sAq, 1)
        t.wait_ge(self.sBR, aa + 1)
        t.wait_ge(self.sXN, aa)
        t.matmul(self.ps_b_lo, C.ones_row[:, :], self.brow[:, 0:TT],
                 start=True, stop=True)
        t.matmul(self.ps_b_hi, C.ones_row[:, :], self.brow[:, TT:2 * TT],
                 start=True, stop=True).then_inc(self.sBC, 1)

    def act(self, s, a):
        C = self.C
        aa = self.a0 + a
        s.wait_ge(self.sR, 16 * 8 * (aa + 1))
        for c in range(NCH):
            idx = aa * 8 + c
            if idx >= 2:
                s.wait_ge(self.sAq, idx - 1)
            s.activation(self.sqb[:, (c % 2) * TT:(c % 2) * TT + TT],
                         self.rbuf[:, c * TT:(c + 1) * TT],
                         AF.Square).then_inc(self.sSQ, 1)
        s.wait_ge(self.sST, aa + 1)
        s.activation(self.atmp[0:1, 1 * TT:2 * TT], self.atmp[0:1, 1 * TT:2 * TT],
                     AF.Ln, bias=C.cc[0:1, 1:2]).then_inc(self.sAC, 1)
        s.wait_ge(self.sAC, aa + 1)
        s.activation(self.atmp[0:1, 2 * TT:3 * TT], self.atmp[0:1, 1 * TT:2 * TT],
                     AF.Exp, scale=C.cc[0:1, 2:3]).then_inc(self.sRS, 1)

    def dve(self, v, a, extra_wait=None):
        C = self.C
        aa = self.a0 + a
        v.wait_ge(self.sR, 16 * 8 * (aa + 1))
        for c in range(NCH):
            idx = aa * 8 + c
            if idx >= 2:
                v.wait_ge(self.sAs, idx - 1)
            v.tensor_copy(self.xbb[c % 2][:, :], self.rbuf[:, c * TT:(c + 1) * TT]).then_inc(self.sXB, 1)
        v.wait_ge(self.sAs, (aa + 1) * 8)
        v.wait_ge(self.sAq, (aa + 1) * 8)
        if aa >= 1:
            v.wait_ge(self.sXN, aa)
        v.tensor_scalar(self.atmp[0:1, 0:TT], self.ps_s, C.cc[0:1, 0:1], None,
                        ALU.mult).then_inc(self.sDV, 1)
        v.tensor_scalar(self.atmp[0:1, TT:2 * TT], self.ps_q, C.cc[0:1, 0:1], None, ALU.mult)
        v.wait_ge(self.sDV, 3 * aa + 1)
        v.tensor_tensor(self.atmp[0:1, 2 * TT:3 * TT], self.atmp[0:1, 0:TT], self.atmp[0:1, 0:TT],
                        ALU.mult).then_inc(self.sDV, 1)
        v.wait_ge(self.sDV, 3 * aa + 2)
        v.tensor_tensor(self.atmp[0:1, 1 * TT:2 * TT], self.atmp[0:1, TT:2 * TT],
                        self.atmp[0:1, 2 * TT:3 * TT], ALU.subtract).then_inc(self.sST, 1)
        v.wait_ge(self.sRS, aa + 1)
        if aa >= 1:
            v.wait_ge(self.sBC, aa)
        v.tensor_tensor(self.atmp[0:1, 3 * TT:4 * TT], self.atmp[0:1, 0:TT],
                        self.atmp[0:1, 2 * TT:3 * TT], ALU.mult).then_inc(self.sDV, 1)
        v.wait_ge(self.sDV, 3 * aa + 3)
        v.tensor_copy(self.brow[:, 0:TT], self.atmp[0:1, 2 * TT:3 * TT])
        v.tensor_copy(self.brow[:, TT:2 * TT], self.atmp[0:1, 3 * TT:4 * TT]).then_inc(self.sBR, 1)
        v.wait_ge(self.sBC, aa + 1)
        if extra_wait is not None:
            extra_wait(v, a)
        for c in range(NCH):
            if c >= 2:
                v.wait_ge(self.sDV3, 8 * aa + c - 1)
            v.tensor_tensor(self.atmp[:, (c % 2) * TT:(c % 2) * TT + TT],
                            self.rbuf[:, c * TT:(c + 1) * TT],
                            self.ps_b_lo, ALU.mult).then_inc(self.sDV2, 1)
            if c >= 1:
                v.wait_ge(self.sDV2, 8 * aa + c)
                cc_ = c - 1
                v.tensor_tensor(self.xn_ap(a, cc_), self.atmp[:, (cc_ % 2) * TT:(cc_ % 2) * TT + TT],
                                self.ps_b_hi, ALU.subtract).then_inc(self.sDV3, 1)
        v.wait_ge(self.sDV2, 8 * aa + 8)
        cc_ = NCH - 1
        v.tensor_tensor(self.xn_ap(a, cc_), self.atmp[:, (cc_ % 2) * TT:(cc_ % 2) * TT + TT],
                        self.ps_b_hi, ALU.subtract).then_inc(self.sDV3, 1)
        v.engine_nop().then_inc(self.sXN, 1)


def emit_l0ffn(P, C, dr, tag):
    nc = P.nc
    P.push()
    R_in, R_out = dr["R_in"], dr["R_out"]
    w1T, w2T, h1T, h2T, w3T = dr["bw1T"], dr["bw2T"], dr["bh1T"], dr["bh2T"], dr["bw3T"]

    h_a = P.sb(f"{tag}_ha", [128, NHID * QT], BF16)
    h_b = P.sb(f"{tag}_hb", [128, NHID * QT], BF16)
    wbuf = [P.sb(f"{tag}_w{i}", [128, 8192], BF16) for i in range(2)]
    rbufE = [P.sb(f"{tag}_re{i}", [128, TT], F32) for i in range(2)]
    tmp = [P.sb(f"{tag}_tmp{i}", [128, TT], F32) for i in range(2)]

    psA = [P.ps(f"{tag}_psA{i}", [128, TT]) for i in range(2)]
    psB = [P.ps(f"{tag}_psB{i}", [128, TT]) for i in range(2)]

    def bcol(chunk, u):
        return chunk * QT + u * TT

    def rcol_a(a, c):
        return c * NTOK + (a // 2) * QT + (a % 2) * TT

    def rcol(chunk, q, u):
        return chunk * NTOK + q * QT + u * TT

    L = LNEmit(P, C, tag, R_in, 2 * NQ, rcol_a,
               lambda a, c: h_b[:, bcol(c, a % 2):bcol(c, a % 2) + TT])

    sMA = P.sem(f"{tag}_MA"); sMB = P.sem(f"{tag}_MB"); sSil = P.sem(f"{tag}_Sil")
    sEvB = P.sem(f"{tag}_EvB")
    sMC = P.sem(f"{tag}_MC"); sCDc = P.sem(f"{tag}_CDc")
    sMD = P.sem(f"{tag}_MD"); sCDd = P.sem(f"{tag}_CDd")
    sME = P.sem(f"{tag}_ME"); sRE = [P.sem(f"{tag}_RE0"), P.sem(f"{tag}_RE1")]
    sE = P.sem(f"{tag}_E")
    sSO = [P.sem(f"{tag}_SO0"), P.sem(f"{tag}_SO1")]
    sW = [P.sem(f"{tag}_W0"), P.sem(f"{tag}_W1")]

    wgroups = []
    for q in range(NQ):
        for mg in range(8):
            dmas = [(w1T, k * 128, mg * 512, 512, k * 512) for k in range(8)]
            dmas += [(w2T, k * 128, mg * 512, 512, 4096 + k * 512) for k in range(8)]
            wgroups.append(("B", q, mg, dmas))
        for mg2 in range(16):
            wgroups.append(("C", q, mg2, [(h1T, k * 128, mg2 * 256, 256, k * 256) for k in range(32)]))
        for mg2 in range(16):
            wgroups.append(("D", q, mg2, [(h2T, k * 128, mg2 * 256, 256, k * 256) for k in range(32)]))
        for mgE in range(4):
            wgroups.append(("E", q, mgE, [(w3T, k * 128, mgE * 256, 256, k * 256) for k in range(32)]))
    Wdone, gidx = [], {}
    ptot = [0, 0]
    for gg, (kind, q, idx, dmas) in enumerate(wgroups):
        ptot[gg % 2] += 16 * len(dmas)
        Wdone.append(ptot[gg % 2])
        gidx[(kind, q, idx)] = gg

    def eidx(i):
        mgE, m, u = i // 4, (i // 2) % 2, i % 2
        return mgE, m, u, mgE * 2 + m

    with nc.Block() as block:
        def gdone(kind, q, idx):
            if kind == "B":
                return (sMB, q * 64 + idx * 8 + 8)
            if kind == "C":
                return (sMC, q * 64 + idx * 4 + 4)
            if kind == "D":
                return (sMD, q * 64 + idx * 4 + 4)
            return (sME, q * 16 + idx * 4 + 4)

        @block.sync
        def _(sync):
            for gg, (kind, q, idx, dmas) in enumerate(wgroups):
                if gg >= 2:
                    pk, pq, pi, _ = wgroups[gg - 2]
                    dsem, dval = gdone(pk, pq, pi)
                    sync.wait_ge(dsem, dval)
                for (dram, r0, c0, cw, dc) in dmas:
                    sync.dma_start(wbuf[gg % 2][:, dc:dc + cw],
                                   dram[r0:r0 + 128, c0:c0 + cw]).then_inc(sW[gg % 2], 16)

        @block.gpsimd
        def _(g):
            for q in range(NQ):
                for u in range(2):
                    L.gpsimd(g, q * 2 + u)
                for i in range(16):
                    mgE, m, u, dchunk = eidx(i)
                    ie = q * 16 + i
                    if ie >= 2:
                        g.wait_ge(sE, ie - 1)
                    g.dma_start(rbufE[ie % 2][:, :],
                                R_in[:, rcol(dchunk, q, u):rcol(dchunk, q, u) + TT]).then_inc(sRE[ie % 2], 16)
                    if ie >= 1:
                        pe = ie - 1
                        pq, pi = pe // 16, pe % 16
                        _, _, pu, pd = eidx(pi)
                        g.wait_ge(sE, pe + 1)
                        g.dma_start(R_out[:, rcol(pd, pq, pu):rcol(pd, pq, pu) + TT],
                                    tmp[pe % 2][:, :]).then_inc(sSO[pe % 2], 16)
            ie = NQ * 16 - 1
            _, _, lu, ld = eidx(ie % 16)
            g.wait_ge(sE, ie + 1)
            g.dma_start(R_out[:, rcol(ld, NQ - 1, lu):rcol(ld, NQ - 1, lu) + TT],
                        tmp[ie % 2][:, :]).then_inc(sSO[ie % 2], 16)
            g.wait_ge(sSO[0], 16 * (NQ * 16 // 2))
            g.wait_ge(sSO[1], 16 * (NQ * 16 // 2))

        @block.tensor
        def _(t):
            for q in range(NQ):
                for u in range(2):
                    L.pe(t, q * 2 + u)
                for mg in range(8):
                    gg = gidx[("B", q, mg)]
                    for m in range(4):
                        for u in range(2):
                            ib = q * 64 + mg * 8 + m * 2 + u
                            if m == 0 and u == 0:
                                t.wait_ge(sW[gg % 2], Wdone[gg])
                            if mg == 0 and m == 0 and u == 0:
                                t.wait_ge(L.sXN, L.a0 + 2 * q + 2)
                                if q >= 1:
                                    t.wait_ge(sE, 16 * q)
                                    t.wait_ge(sCDd, 64 * q)
                            if ib >= 2:
                                t.wait_ge(sSil, ib - 1)
                                t.wait_ge(sEvB, ib - 1)
                            for k in range(8):
                                ins = t.matmul(psA[ib % 2][:, :],
                                               wbuf[gg % 2][:, k * 512 + m * 128:k * 512 + m * 128 + 128],
                                               h_b[:, bcol(k, u):bcol(k, u) + TT],
                                               start=(k == 0), stop=(k == 7))
                                if k == 7:
                                    ins.then_inc(sMA, 1)
                            for k in range(8):
                                ins = t.matmul(psB[ib % 2][:, :],
                                               wbuf[gg % 2][:, 4096 + k * 512 + m * 128:4096 + k * 512 + m * 128 + 128],
                                               h_b[:, bcol(k, u):bcol(k, u) + TT],
                                               start=(k == 0), stop=(k == 7))
                                if k == 7:
                                    ins.then_inc(sMB, 1)
                for mg2 in range(16):
                    gg = gidx[("C", q, mg2)]
                    for m in range(2):
                        for u in range(2):
                            ic = q * 64 + mg2 * 4 + m * 2 + u
                            if m == 0 and u == 0:
                                t.wait_ge(sW[gg % 2], Wdone[gg])
                            if mg2 == 0 and m == 0 and u == 0:
                                t.wait_ge(sEvB, 64 * (q + 1))
                                t.wait_ge(sSil, 64 * (q + 1))
                            if ic >= 2:
                                t.wait_ge(sCDc, ic - 1)
                            for k in range(32):
                                ins = t.matmul(psA[ic % 2][:, :],
                                               wbuf[gg % 2][:, k * 256 + m * 128:k * 256 + m * 128 + 128],
                                               h_a[:, bcol(k, u):bcol(k, u) + TT],
                                               start=(k == 0), stop=(k == 31))
                                if k == 31:
                                    ins.then_inc(sMC, 1)
                for mg2 in range(16):
                    gg = gidx[("D", q, mg2)]
                    for m in range(2):
                        for u in range(2):
                            idd = q * 64 + mg2 * 4 + m * 2 + u
                            if m == 0 and u == 0:
                                t.wait_ge(sW[gg % 2], Wdone[gg])
                            if mg2 == 0 and m == 0 and u == 0:
                                t.wait_ge(sCDc, 64 * (q + 1))
                            if idd >= 2:
                                t.wait_ge(sCDd, idd - 1)
                            for k in range(32):
                                ins = t.matmul(psA[idd % 2][:, :],
                                               wbuf[gg % 2][:, k * 256 + m * 128:k * 256 + m * 128 + 128],
                                               h_b[:, bcol(k, u):bcol(k, u) + TT],
                                               start=(k == 0), stop=(k == 31))
                                if k == 31:
                                    ins.then_inc(sMD, 1)
                for mgE in range(4):
                    gg = gidx[("E", q, mgE)]
                    for m in range(2):
                        for u in range(2):
                            ie = q * 16 + mgE * 4 + m * 2 + u
                            if m == 0 and u == 0:
                                t.wait_ge(sW[gg % 2], Wdone[gg])
                            if mgE == 0 and m == 0 and u == 0:
                                t.wait_ge(sCDd, 64 * (q + 1))
                            if ie >= 2:
                                t.wait_ge(sE, ie - 1)
                            for k in range(32):
                                ins = t.matmul(psB[ie % 2][:, :],
                                               wbuf[gg % 2][:, k * 256 + m * 128:k * 256 + m * 128 + 128],
                                               h_a[:, bcol(k, u):bcol(k, u) + TT],
                                               start=(k == 0), stop=(k == 31))
                                if k == 31:
                                    ins.then_inc(sME, 1)

        @block.scalar
        def _(s):
            for q in range(NQ):
                for u in range(2):
                    L.act(s, q * 2 + u)
                for mg in range(8):
                    for m in range(4):
                        for u in range(2):
                            ib = q * 64 + mg * 8 + m * 2 + u
                            mglob = mg * 4 + m
                            s.wait_ge(sMA, ib + 1)
                            if mg == 0 and m == 0 and u == 0 and q >= 1:
                                s.wait_ge(sSO[0], 128 * q)
                                s.wait_ge(sSO[1], 128 * q)
                            if ib >= 2:
                                s.wait_ge(sEvB, ib - 1)
                            s.activation(tmp[ib % 2][:, :], psA[ib % 2][:, :], AF.Silu,
                                         scale=sm(C, "l0f_s1", mglob)).then_inc(sSil, 1)
                for mg2 in range(16):
                    for m in range(2):
                        for u in range(2):
                            ic = q * 64 + mg2 * 4 + m * 2 + u
                            mglob = mg2 * 2 + m
                            s.wait_ge(sMC, ic + 1)
                            s.activation(h_b[:, bcol(mglob, u):bcol(mglob, u) + TT],
                                         psA[ic % 2][:, :], AF.Gelu,
                                         scale=sm(C, "l0f_sh1", mglob)).then_inc(sCDc, 1)
                for mg2 in range(16):
                    for m in range(2):
                        for u in range(2):
                            idd = q * 64 + mg2 * 4 + m * 2 + u
                            mglob = mg2 * 2 + m
                            s.wait_ge(sMD, idd + 1)
                            s.activation(h_a[:, bcol(mglob, u):bcol(mglob, u) + TT],
                                         psA[idd % 2][:, :], AF.Gelu,
                                         scale=sm(C, "l0f_sh2", mglob)).then_inc(sCDd, 1)

        @block.vector
        def _(v):
            def extraw(vv, a):
                if a % 2 == 0 and a >= 2:
                    vv.wait_ge(sMD, 64 * (a // 2))
            for q in range(NQ):
                for u in range(2):
                    L.dve(v, q * 2 + u, extra_wait=extraw)
                for mg in range(8):
                    for m in range(4):
                        for u in range(2):
                            ib = q * 64 + mg * 8 + m * 2 + u
                            mglob = mg * 4 + m
                            v.wait_ge(sMB, ib + 1)
                            v.wait_ge(sSil, ib + 1)
                            if mg == 0 and m == 0 and u == 0 and q >= 1:
                                v.wait_ge(sME, 16 * q)
                            v.scalar_tensor_tensor(h_a[:, bcol(mglob, u):bcol(mglob, u) + TT],
                                                   psB[ib % 2][:, :], sm(C, "l0f_s2", mglob),
                                                   tmp[ib % 2][:, :], ALU.mult,
                                                   ALU.mult).then_inc(sEvB, 1)
                for i in range(16):
                    mgE, m, u, dchunk = eidx(i)
                    ie = q * 16 + i
                    v.wait_ge(sME, ie + 1)
                    v.wait_ge(sRE[ie % 2], 16 * (ie // 2 + 1))
                    if ie >= 2:
                        v.wait_ge(sSO[ie % 2], 16 * (ie // 2))
                    v.scalar_tensor_tensor(tmp[ie % 2][:, :], psB[ie % 2][:, :],
                                           C.hs3[:, dchunk:dchunk + 1], rbufE[ie % 2][:, :],
                                           ALU.mult, ALU.add).then_inc(sE, 1)
    P.pop()

MOE_NE = 8
MOE_ROUTING = True
MOE_RT_LEVEL = 3

def emit_moe_dense(P, C, dr, tag):
    nc = P.nc
    P.push()
    R_in, R_out = dr["R_in"], dr["R_out"]
    w1d, w2d, w3d, wgd = dr["bw1"], dr["bw2"], dr["bw3"], dr["bwg"]
    identF, SEL = dr["identF"], dr["SEL"]

    MH = 16
    xn = P.sb(f"{tag}_xn", [128, NCH * QT], BF16)
    hb = P.sb(f"{tag}_hb", [128, MH * QT], BF16)
    wbuf = [P.sb(f"{tag}_w{i}", [128, 8192], BF16) for i in range(2)]
    ACC = P.sb(f"{tag}_acc", [128, NCH * QT], F32)
    wgb = P.sb(f"{tag}_wgb", [128, NCH * 8], BF16)
    idf = P.sb(f"{tag}_idf", [128, 128], F32)
    sel = P.sb(f"{tag}_sel", [8, 8 * 128], F32)
    lgbuf = P.sb(f"{tag}_lg", [128, 64], F32)
    rtmp = P.sb(f"{tag}_rt", [128, 24], F32)
    wtile = P.sb(f"{tag}_wt", [128, 64], F32)
    wsb = P.sb(f"{tag}_wsb", [8, 1024], F32)
    wbsb = P.sb(f"{tag}_wbs", [128, 2 * TT], F32)
    ytmp = [P.sb(f"{tag}_yt{i}", [128, TT], F32) for i in range(2)]
    tmp = [P.sb(f"{tag}_tmp{i}", [128, TT], F32) for i in range(2)]
    silu = [P.sb(f"{tag}_sl{i}", [128, TT], F32) for i in range(2)]
    rbufE = [P.sb(f"{tag}_re{i}", [128, TT], F32) for i in range(2)]

    psA = [P.ps(f"{tag}_psA{i}", [128, TT]) for i in range(2)]
    psB = [P.ps(f"{tag}_psB{i}", [128, TT]) for i in range(2)]

    def bcol(chunk, u):
        return chunk * QT + u * TT

    def rcol_a(a, c):
        return c * NTOK + (a // 2) * QT + (a % 2) * TT

    def rcol(chunk, q, u):
        return chunk * NTOK + q * QT + u * TT

    psLs = P.ps(f"{tag}_psLs", [1, TT])
    psLq = P.ps(f"{tag}_psLq", [1, TT])
    pbY = P.ps(f"{tag}_pbY", [128, 1024])
    pbY_lo, pbY_hi = pbY[:, 0:TT], pbY[:, TT:2 * TT]
    L = LNEmit(P, C, tag, R_in, 2 * NQ, rcol_a,
               lambda a, c: xn[:, bcol(c, a % 2):bcol(c, a % 2) + TT],
               ps3=(psLs, psLq, psA[0], psA[1]))

    sW = [P.sem(f"{tag}_W0"), P.sem(f"{tag}_W1")]
    sLG = P.sem(f"{tag}_LG"); sRT = P.sem(f"{tag}_RT"); sRD = P.sem(f"{tag}_RD")
    sSG = P.sem(f"{tag}_SG"); sWT = P.sem(f"{tag}_WT"); sMT = P.sem(f"{tag}_MT")
    sMLG = P.sem(f"{tag}_MLG")
    sWBM = P.sem(f"{tag}_WBM"); sWB = P.sem(f"{tag}_WB")
    sMA = P.sem(f"{tag}_MA"); sMB = P.sem(f"{tag}_MB"); sSil = P.sem(f"{tag}_Sil")
    sEvB = P.sem(f"{tag}_EvB")
    sMY = P.sem(f"{tag}_MY"); sYW = P.sem(f"{tag}_YW"); sYA = P.sem(f"{tag}_YA")
    sMS = P.sem(f"{tag}_MS"); sRU = P.sem(f"{tag}_RU")
    sRE = [P.sem(f"{tag}_RE0"), P.sem(f"{tag}_RE1")]
    sSO = [P.sem(f"{tag}_SO0"), P.sem(f"{tag}_SO1")]
    sCst = P.sem(f"{tag}_Cst")

    wgroups = []
    for q in range(NQ):
        for e in range(MOE_NE):
            for mg in range(4):
                dmas = [(w1d, 1024 * e + k * 128, mg * 512, 512, k * 512) for k in range(8)]
                dmas += [(w2d, 1024 * e + k * 128, mg * 512, 512, 4096 + k * 512) for k in range(8)]
                wgroups.append(("B", q, e, mg, dmas))
            for mgE in range(2):
                dmas = [(w3d, 2048 * e + k * 128, mgE * 512, 512, k * 512) for k in range(16)]
                wgroups.append(("E", q, e, mgE, dmas))
    Wdone, gidx = [], {}
    ptot = [0, 0]
    for gg, (kind, q, e, idx, dmas) in enumerate(wgroups):
        ptot[gg % 2] += 16 * len(dmas)
        Wdone.append(ptot[gg % 2])
        gidx[(kind, q, e, idx)] = gg

    def ib_of(q, e, mg, m, u):
        return (q * 8 + e) * 32 + mg * 8 + m * 2 + u

    def iy_of(q, e, j):
        return (q * 8 + e) * 16 + j

    with nc.Block() as block:
        def gdone(kind, q, e, idx):
            if kind == "B":
                return (sMB, ib_of(q, e, idx, 3, 1) + 1)
            return (sMY, iy_of(q, e, idx * 8 + 7) + 1)

        @block.sync
        def _(sync):
            for c in range(NCH):
                sync.dma_start(wgb[:, c * 8:(c + 1) * 8],
                               wgd[c * 128:(c + 1) * 128, 0:8]).then_inc(sCst, 16)
            sync.dma_start(idf[:, :], identF).then_inc(sCst, 16)
            sync.dma_start(sel[:, :], SEL).then_inc(sCst, 16)
            for gg, (kind, q, e, idx, dmas) in enumerate(wgroups):
                if gg >= 2:
                    pk, pq, pe_, pi, _ = wgroups[gg - 2]
                    dsem, dval = gdone(pk, pq, pe_, pi)
                    sync.wait_ge(dsem, dval)
                for (dram, r0, c0, cw, dc) in dmas:
                    sync.dma_start(wbuf[gg % 2][:, dc:dc + cw],
                                   dram[r0:r0 + 128, c0:c0 + cw]).then_inc(sW[gg % 2], 16)

        @block.gpsimd
        def _(g):
            for q in range(NQ):
                for u in range(2):
                    L.gpsimd(g, q * 2 + u)
                g.wait_ge(sRU, 16 * q)
                g.memset(ACC[:, :], 0.0).then_inc(sMS, 1)
                for i in range(16):
                    m, u = i // 2, i % 2
                    ir = q * 16 + i
                    if ir >= 2:
                        g.wait_ge(sRU, ir - 1)
                    g.dma_start(rbufE[ir % 2][:, :],
                                R_in[:, rcol(m, q, u):rcol(m, q, u) + TT]).then_inc(sRE[ir % 2], 16)
                    if ir >= 1:
                        pr = ir - 1
                        pqq, pii = pr // 16, pr % 16
                        pm, pu = pii // 2, pii % 2
                        g.wait_ge(sRU, pr + 1)
                        g.dma_start(R_out[:, rcol(pm, pqq, pu):rcol(pm, pqq, pu) + TT],
                                    tmp[pr % 2][:, :]).then_inc(sSO[pr % 2], 16)
            ir = NQ * 16 - 1
            pm, pu = (ir % 16) // 2, ir % 2
            g.wait_ge(sRU, ir + 1)
            g.dma_start(R_out[:, rcol(pm, NQ - 1, pu):rcol(pm, NQ - 1, pu) + TT],
                        tmp[ir % 2][:, :]).then_inc(sSO[ir % 2], 16)
            g.wait_ge(sSO[0], 16 * (NQ * 16 // 2))
            g.wait_ge(sSO[1], 16 * (NQ * 16 // 2))

        @block.tensor
        def _(t):
            t.wait_ge(sCst, 160)
            for q in range(NQ):
                def hook(tt_, a):
                    qq = a // 2
                    if qq >= 1 and MOE_NE == 8:
                        tt_.wait_ge(sSil, 256 * qq)
                        tt_.wait_ge(sLG, 8 * qq)
                for u in range(2):
                    L.pe(t, q * 2 + u, extra_wait=hook)
                if not MOE_ROUTING:
                    pass
                t.wait_ge(L.sXN, L.a0 + 2 * q + 2)
                if q >= 1 and MOE_NE == 8:
                    t.wait_ge(sSil, 256 * q)
                for tc in range(8 if MOE_ROUTING else 0):
                    if q * 8 + tc >= 1:
                        t.wait_ge(sLG, q * 8 + tc)
                    for c in range(NCH):
                        t.matmul(psA[0][:, 0:8], xn[:, c * QT + tc * 128:c * QT + tc * 128 + 128],
                                 wgb[:, c * 8:(c + 1) * 8],
                                 start=(c == 0), stop=(c == 7)).then_inc(sMLG, 1 if c == 7 else 0)
                for tc in range(8 if (MOE_ROUTING and MOE_RT_LEVEL >= 2.5) else 0):
                    i = q * 8 + tc
                    t.wait_ge(sRT, i + 1)
                    if q >= 1 and tc == 0 and MOE_NE == 8:
                        t.wait_ge(sEvB, 256 * q)
                    if i >= 1 and MOE_RT_LEVEL >= 3:
                        t.wait_ge(sWT, i)
                    t.matmul(psB[0][0:8, (tc % 4) * 128:(tc % 4) * 128 + 128],
                             wtile[:, tc * 8:(tc + 1) * 8], idf[:, :],
                             start=True, stop=True).then_inc(sMT, 1)
                for e in range(MOE_NE):
                    for u in range(2):
                        if e == 0 and u == 0:
                            t.wait_ge(L.sXN, L.a0 + 2 * q + 2)
                            if q >= 1:
                                t.wait_ge(sYA, 128 * q)
                                t.wait_ge(sWB, 16 * q)
                        t.wait_ge(sWT, q * 8 + 8)
                        t.wait_ge(sYA, q * 128 + e * 16)
                        t.wait_ge(sWB, 2 * (q * 8 + e))
                        t.matmul((pbY_lo if u == 0 else pbY_hi), sel[:, e * 128:(e + 1) * 128],
                                 wsb[:, u * TT:u * TT + TT],
                                 start=True, stop=True).then_inc(sWBM, 1)
                    for mg in range(4):
                        gg = gidx[("B", q, e, mg)]
                        for m in range(4):
                            for u in range(2):
                                ib = ib_of(q, e, mg, m, u)
                                if m == 0 and u == 0:
                                    t.wait_ge(sW[gg % 2], Wdone[gg])
                                if e == 0 and mg == 0 and m == 0 and u == 0:
                                    t.wait_ge(sLG, 8 * (q + 1))
                                    t.wait_ge(sWT, 8 * (q + 1))
                                if ib >= 2:
                                    t.wait_ge(sSil, ib - 1)
                                    t.wait_ge(sEvB, ib - 1)
                                for k in range(8):
                                    ins = t.matmul(psA[ib % 2][:, :],
                                                   wbuf[gg % 2][:, k * 512 + m * 128:k * 512 + m * 128 + 128],
                                                   xn[:, bcol(k, u):bcol(k, u) + TT],
                                                   start=(k == 0), stop=(k == 7))
                                    if k == 7:
                                        ins.then_inc(sMA, 1)
                                for k in range(8):
                                    ins = t.matmul(psB[ib % 2][:, :],
                                                   wbuf[gg % 2][:, 4096 + k * 512 + m * 128:4096 + k * 512 + m * 128 + 128],
                                                   xn[:, bcol(k, u):bcol(k, u) + TT],
                                                   start=(k == 0), stop=(k == 7))
                                    if k == 7:
                                        ins.then_inc(sMB, 1)
                    for mgE in range(2):
                        gg = gidx[("E", q, e, mgE)]
                        for m in range(4):
                            for u in range(2):
                                j = mgE * 8 + m * 2 + u
                                iy = iy_of(q, e, j)
                                if m == 0 and u == 0:
                                    t.wait_ge(sW[gg % 2], Wdone[gg])
                                if mgE == 0 and m == 0 and u == 0:
                                    t.wait_ge(sWB, 2 * (q * 8 + e) + 2)
                                    t.wait_ge(sEvB, ib_of(q, e, 3, 3, 1) + 1)
                                if iy >= 2:
                                    t.wait_ge(sYA, iy - 1)
                                for k in range(16):
                                    ins = t.matmul((pbY_lo if j % 2 == 0 else pbY_hi),
                                                   wbuf[gg % 2][:, k * 512 + m * 128:k * 512 + m * 128 + 128],
                                                   hb[:, bcol(k, u):bcol(k, u) + TT],
                                                   start=(k == 0), stop=(k == 15))
                                    if k == 15:
                                        ins.then_inc(sMY, 1)

        @block.scalar
        def _(s):
            for q in range(NQ):
                for u in range(2):
                    L.act(s, q * 2 + u)
                for tc in range(8 if (MOE_ROUTING and MOE_RT_LEVEL >= 2) else 0):
                    i = q * 8 + tc
                    s.wait_ge(sSG, i + 1)
                    s.activation(rtmp[:, 19:20], rtmp[:, 18:19], AF.Sigmoid).then_inc(sRD, 1)
                for e in range(MOE_NE):
                    for mg in range(4):
                        for m in range(4):
                            for u in range(2):
                                ib = ib_of(q, e, mg, m, u)
                                s.wait_ge(sMA, ib + 1)
                                if ib >= 2:
                                    s.wait_ge(sEvB, ib - 1)
                                s.activation(silu[ib % 2][:, :], psA[ib % 2][:, :],
                                             AF.Silu).then_inc(sSil, 1)

        @block.vector
        def _(v):
            rdc = [0]

            def rinc(ins):
                rdc[0] += 1
                ins.then_inc(sRD, 1)
                return rdc[0]

            for q in range(NQ):
                def extraw(vv, a):
                    qq = a // 2
                    if qq >= 1:
                        if MOE_NE == 8:
                            vv.wait_ge(sMB, 256 * qq)
                        if MOE_ROUTING:
                            vv.wait_ge(sMLG, 8 * qq)
                for u in range(2):
                    L.dve(v, q * 2 + u, extra_wait=extraw)
                for tc in range(8 if MOE_ROUTING else 0):
                    i = q * 8 + tc
                    base = 9 * i
                    lgs = lgbuf[:, tc * 8:(tc + 1) * 8]
                    v.wait_ge(sMLG, i + 1)
                    v.tensor_copy(lgs, psA[0][:, 0:8]).then_inc(sLG, 1)
                    if MOE_RT_LEVEL < 2:
                        continue
                    v.wait_ge(sLG, i + 1)
                    v.reduce_max(rtmp[:, 16:17], lgs, axis=AX.X).then_inc(sRD, 1)
                    v.wait_ge(sRD, base + 1)
                    v.tensor_scalar(rtmp[:, 0:8], lgs, rtmp[:, 16:17], None,
                                    ALU.is_equal).then_inc(sRD, 1)
                    v.wait_ge(sRD, base + 2)
                    v.scalar_tensor_tensor(rtmp[:, 8:16], rtmp[:, 0:8], C.cc[:, 8:9],
                                           lgs, ALU.mult, ALU.add).then_inc(sRD, 1)
                    v.wait_ge(sRD, base + 3)
                    v.reduce_max(rtmp[:, 17:18], rtmp[:, 8:16], axis=AX.X).then_inc(sRD, 1)
                    v.wait_ge(sRD, base + 4)
                    v.tensor_scalar(rtmp[:, 8:16], rtmp[:, 8:16], rtmp[:, 17:18], None,
                                    ALU.is_equal).then_inc(sRD, 1)
                    v.wait_ge(sRD, base + 5)
                    v.tensor_tensor(rtmp[:, 18:19], rtmp[:, 16:17], rtmp[:, 17:18],
                                    ALU.subtract).then_inc(sSG, 1)
                    v.wait_ge(sRD, base + 6)
                    v.tensor_scalar(rtmp[:, 20:21], rtmp[:, 19:20], C.cc[:, 3:4], None,
                                    ALU.mult).then_inc(sRD, 1)
                    v.wait_ge(sRD, base + 7)
                    v.tensor_scalar(rtmp[:, 21:22], rtmp[:, 19:20], C.cc[:, 2:3], C.cc[:, 3:4],
                                    ALU.mult, ALU.add).then_inc(sRD, 1)
                    v.wait_ge(sRD, base + 8)
                    v.tensor_scalar(rtmp[:, 0:8], rtmp[:, 0:8], rtmp[:, 20:21], None,
                                    ALU.mult).then_inc(sRD, 1)
                    v.wait_ge(sRD, base + 9)
                    if q >= 1 and MOE_RT_LEVEL >= 3:
                        v.wait_ge(sMT, (q - 1) * 8 + tc + 1)
                    v.scalar_tensor_tensor(wtile[:, tc * 8:(tc + 1) * 8], rtmp[:, 8:16],
                                           rtmp[:, 21:22],
                                           rtmp[:, 0:8], ALU.mult, ALU.add).then_inc(sRT, 1)
                for tc in range(8 if (MOE_ROUTING and MOE_RT_LEVEL >= 3) else 0):
                    i = q * 8 + tc
                    v.wait_ge(sMT, i + 1)
                    if q >= 1 and tc == 0 and MOE_NE == 8:
                        v.wait_ge(sWBM, 16 * q)
                    v.tensor_copy(wsb[:, tc * 128:(tc + 1) * 128],
                                  psB[0][0:8, (tc % 4) * 128:(tc % 4) * 128 + 128]).then_inc(sWT, 1)
                for e in range(MOE_NE):
                    for u in range(2):
                        v.wait_ge(sWBM, 2 * (q * 8 + e) + u + 1)
                        if (q, e) != (0, 0):
                            v.wait_ge(sYW, (q * 8 + e) * 16)
                        v.tensor_copy(wbsb[:, u * TT:u * TT + TT],
                                      (pbY_lo if u == 0 else pbY_hi)).then_inc(sWB, 1)
                    for mg in range(4):
                        for m in range(4):
                            for u in range(2):
                                ib = ib_of(q, e, mg, m, u)
                                mglob = mg * 4 + m
                                v.wait_ge(sMB, ib + 1)
                                v.wait_ge(sSil, ib + 1)
                                if mg == 0 and m == 0 and u == 0 and (q, e) != (0, 0):
                                    v.wait_ge(sMY, (q * 8 + e) * 16)
                                v.tensor_tensor(hb[:, bcol(mglob, u):bcol(mglob, u) + TT],
                                                psB[ib % 2][:, :], silu[ib % 2][:, :],
                                                ALU.mult).then_inc(sEvB, 1)
                    for j in range(16):
                        mgE, m, u = j // 8, (j % 8) // 2, j % 2
                        mo = mgE * 4 + m
                        iy = iy_of(q, e, j)
                        v.wait_ge(sMY, iy + 1)
                        v.tensor_tensor(ytmp[j % 2][:, :],
                                        (pbY_lo if j % 2 == 0 else pbY_hi),
                                        wbsb[:, u * TT:u * TT + TT], ALU.mult).then_inc(sYW, 1)
                        v.wait_ge(sYW, iy + 1)
                        if e >= 1:
                            v.wait_ge(sYA, iy_of(q, e - 1, j) + 1)
                        if e == 0 and j == 0:
                            v.wait_ge(sMS, q + 1)
                        v.tensor_tensor(ACC[:, bcol(mo, u):bcol(mo, u) + TT], ytmp[j % 2][:, :],
                                        ACC[:, bcol(mo, u):bcol(mo, u) + TT], ALU.add).then_inc(sYA, 1)
                for i in range(16):
                    m, u = i // 2, i % 2
                    ir = q * 16 + i
                    if i == 0:
                        if MOE_NE == 8:
                            v.wait_ge(sYA, 16 * MOE_NE * (q + 1))
                        v.wait_ge(sMS, q + 1)
                    v.wait_ge(sRE[ir % 2], 16 * (ir // 2 + 1))
                    if ir >= 2:
                        v.wait_ge(sSO[ir % 2], 16 * (ir // 2))
                    v.tensor_tensor(tmp[ir % 2][:, :], ACC[:, bcol(m, u):bcol(m, u) + TT],
                                    rbufE[ir % 2][:, :], ALU.add).then_inc(sRU, 1)
    P.pop()

def l0a_chunks(i):
    out = []
    for j in (i - 8, i - 4, i - 2):
        if j >= 0:
            out.append((j, 256))
    if i - 1 >= 0:
        out.append((i - 1, 128))
    out.append((i, 0))
    return out

L0A_CHUNKS = {i: l0a_chunks(i) for i in range(16)}
L0A_CUMI = {}
_c = 0
for _i in range(16):
    L0A_CUMI[_i] = _c
    _c += len(L0A_CHUNKS[_i])
L0A_NCH_HEAD = _c
L0A_NCH_HG = _c * 8


def sm_rows(C, name, chunk, row0, nrows=64):
    c0, n = SM_COLS[name]
    return C.smalls[row0:row0 + nrows, c0 + chunk:c0 + chunk + 1]


def emit_l0attn(P, C, dr, tag):
    nc = P.nc
    P.push()
    R_in, R_out = dr["R_in"], dr["R_out"]
    wsrc = [dr["bqw"], dr["bkw"], dr["bvw"], dr["bow"]]

    ctiles = dr.get("cast_tiles") or []
    if ctiles:
        cbf = [P.sb(f"{tag}_cf{i}", [128, 512], F32) for i in range(4)]
        cbb = [P.sb(f"{tag}_cb{i}", [128, 512], BF16) for i in range(4)]
        sCI = [P.sem(f"{tag}_CI{i}") for i in range(4)]
        sCC = P.sem(f"{tag}_CC")
        sCS = [P.sem(f"{tag}_CS{i}") for i in range(4)]
    chalf = (len(ctiles) + 1) // 2

    NT = T
    xn = P.sb(f"{tag}_xn", [128, NCH * NT], BF16)
    Qe = P.sb(f"{tag}_qe", [65, 8 * NT], BF16)
    Ke = P.sb(f"{tag}_ke", [65, 8 * NT], BF16)
    vt = P.sb(f"{tag}_vt", [128, 16 * 8 * 65], BF16)
    wqk = P.sb(f"{tag}_wqk", [128, 88 * 128], BF16)
    kmax = P.sb(f"{tag}_kmx", [1, 4], F32)
    qsq = P.sb(f"{tag}_qsq", [64, TT], BF16)
    pbf = [P.sb(f"{tag}_pb{i}", [128, 128], BF16) for i in range(2)]
    rcp = P.sb(f"{tag}_rcp", [1, 128], F32)
    sct = [P.sb(f"{tag}_sct{i}", [64, 128], F32) for i in range(2)]
    t1b = P.sb(f"{tag}_t1b", [64, 128], F32)
    tmpR = [P.sb(f"{tag}_tr{i}", [128, TT], F32) for i in range(2)]
    rbufE = [P.sb(f"{tag}_rE{i}", [128, TT], F32) for i in range(2)]

    pA = P.ps(f"{tag}_pA", [128, TT])
    pB = P.ps(f"{tag}_pB", [128, TT])
    pC = P.ps(f"{tag}_pC", [128, 1024])
    psS = [P.ps(f"{tag}_pS0", [128, TT]), P.ps(f"{tag}_pS1", [128, TT])]
    psO = [P.ps(f"{tag}_pO0", [128, TT]), P.ps(f"{tag}_pO1", [128, TT])]

    sWL = P.sem(f"{tag}_WL"); sKO = P.sem(f"{tag}_KO"); sVT = P.sem(f"{tag}_VT")
    sMP = P.sem(f"{tag}_MP"); sEvP = P.sem(f"{tag}_EvP")
    sNQ = P.sem(f"{tag}_NQ")
    sMS2 = P.sem(f"{tag}_MS2"); sEXP = P.sem(f"{tag}_EXP"); sPM = P.sem(f"{tag}_PM")
    sMV = P.sem(f"{tag}_MV"); sOD = P.sem(f"{tag}_OD"); sMR = P.sem(f"{tag}_MR")
    sEV = P.sem(f"{tag}_EVc"); sNR = P.sem(f"{tag}_NR"); sEB = P.sem(f"{tag}_EBx")
    sRE = [P.sem(f"{tag}_RE0"), P.sem(f"{tag}_RE1")]
    sSO = [P.sem(f"{tag}_SO0"), P.sem(f"{tag}_SO1")]
    sRU = P.sem(f"{tag}_RU")

    def wcol(p, j):
        return (p * 22 + j) * 128

    def proj_sched(hg):
        out = []
        for pi, kind in ((0, "q"), (1, "k")):
            for b in range(4 * hg, 4 * hg + 4):
                for tt in range(4):
                    out.append((kind, pi, b, tt))
        for b in range(4 * hg, 4 * hg + 4):
            for tc in range(16):
                out.append(("v", 2, b, tc))
        return out

    S_SEQ = 2 * L0A_NCH_HG
    IO_SEQ = 256
    EVP_HG = 96
    EVP_SEQ = 192
    NQ_HEAD = 34
    NQ_SEQ = NQ_HEAD * 16
    IP_SEQ = 224

    def nq_base(s, hg, hl):
        return s * NQ_SEQ + hg * NQ_HEAD * 8 + hl * NQ_HEAD

    def ia_base(s, hg, hl, i):
        return s * S_SEQ + hg * L0A_NCH_HG + hl * L0A_NCH_HEAD + L0A_CUMI[i]

    def io_of(s, hg, hl, i):
        return s * IO_SEQ + hg * 128 + hl * 16 + i

    def rot_wait(eng, ip):
        p = ip - 2
        if p < 0:
            return
        sq, loc = p // IP_SEQ, p % IP_SEQ
        if loc >= 192:
            eng.wait_ge(sRU, sq * 32 + (loc - 192) + 1)
        else:
            eng.wait_ge(sEvP, sq * 192 + loc + 1)

    for s in range(NSEQ):
        ts = s * NT
        ip0 = s * IP_SEQ
        evp0 = s * EVP_SEQ
        ru0 = s * 32

        P.push()
        L = LNEmit(P, C, f"{tag}{s}", R_in, 4,
                   lambda a, c: c * NTOK + ts + a * TT,
                   lambda a, c: xn[:, c * NT + a * TT: c * NT + a * TT + TT],
                   ps3=(pA, pB, pC))
        with nc.Block() as b1:
            @b1.gpsimd
            def _(g):
                for a in range(4):
                    L.gpsimd(g, a)

            @b1.tensor
            def _(t):
                for a in range(4):
                    L.pe(t, a)

            @b1.scalar
            def _(sc):
                for a in range(4):
                    L.act(sc, a)

            @b1.vector
            def _(v):
                for a in range(4):
                    L.dve(v, a)
        P.pop()

        P.push()
        obuf = P.sb(f"{tag}_obf{s}", [128, NCH * NT], BF16)

        with nc.Block() as b2:
            @b2.sync
            def _(sy):
                if s == 0:
                    for p in range(4):
                        for j in range(22):
                            sy.dma_start(wqk[:, wcol(p, j):wcol(p, j) + 128],
                                         wsrc[p][j * 128:(j + 1) * 128, :]).then_inc(sWL, 16)
                t0, t1 = s * chalf, min(len(ctiles), (s + 1) * chalf)
                for i in range(t0, t1 + 4):
                    if i < t1:
                        src, dst, r0, c0, cw = ctiles[i]
                        if i >= 4:
                            sy.wait_ge(sCC, i - 3)
                        sy.dma_start(cbf[i % 4][:, 0:cw],
                                     src[r0:r0 + 128, c0:c0 + cw]).then_inc(sCI[i % 4], 16)
                    j_ = i - 4
                    if t0 <= j_ < t1:
                        src, dst, r0, c0, cw = ctiles[j_]
                        sy.wait_ge(sCC, j_ + 1)
                        sy.dma_start(dst[r0:r0 + 128, c0:c0 + cw],
                                     cbb[j_ % 4][:, 0:cw]).then_inc(sCS[j_ % 4], 16)
                if s == 1 and ctiles:
                    n = len(ctiles)
                    for b in range(4):
                        nb_ = (n - b + 3) // 4
                        if nb_ > 0:
                            sy.wait_ge(sCS[b], 16 * nb_)

            @b2.gpsimd
            def _(g):
                cv_next = [s * chalf]
                cv_end = min(len(ctiles), (s + 1) * chalf)

                def cast_conv():
                    i = cv_next[0]
                    if i >= cv_end:
                        return
                    src, dst, r0, c0, cw = ctiles[i]
                    g.wait_ge(sCI[i % 4], 16 * (i // 4 + 1))
                    if i >= 4:
                        g.wait_ge(sCS[i % 4], 16 * (i // 4))
                    g.tensor_copy(cbb[i % 4][:, 0:cw], cbf[i % 4][:, 0:cw]).then_inc(sCC, 1)
                    cv_next[0] += 1

                nm = 0
                for hg in range(2):
                    for hl in range(8):
                        for i in range(16):
                            for n_, (j, mcol) in enumerate(L0A_CHUNKS[i]):
                                ia = ia_base(s, hg, hl, i) + n_
                                g.wait_ge(sEXP, ia + 1)
                                if ia >= 2:
                                    g.wait_ge(sMV, ia - 1)
                                g.tensor_tensor(pbf[ia % 2][:, :], pbf[ia % 2][:, :],
                                                C.masks[:, mcol:mcol + 128], ALU.mult).then_inc(sPM, 1)
                                nm += 1
                                if nm % 2 == 0:
                                    cast_conv()
                while cv_next[0] < cv_end:
                    cast_conv()
                for ir_ in range(32):
                    b, tt = ir_ // 4, ir_ % 4
                    ir = ru0 + ir_
                    if ir >= 2:
                        g.wait_ge(sRU, ir - 1)
                    g.dma_start(rbufE[ir % 2][:, :],
                                R_in[:, b * NTOK + ts + tt * TT: b * NTOK + ts + tt * TT + TT]
                                ).then_inc(sRE[ir % 2], 16)
                    if ir_ >= 1:
                        pr = ir - 1
                        pb_, pt_ = (pr - ru0) // 4, (pr - ru0) % 4
                        g.wait_ge(sRU, pr + 1)
                        g.dma_start(R_out[:, pb_ * NTOK + ts + pt_ * TT: pb_ * NTOK + ts + pt_ * TT + TT],
                                    tmpR[pr % 2][:, :]).then_inc(sSO[pr % 2], 16)
                ir = ru0 + 31
                g.wait_ge(sRU, ir + 1)
                g.dma_start(R_out[:, 7 * NTOK + ts + 3 * TT: 7 * NTOK + ts + 3 * TT + TT],
                            tmpR[ir % 2][:, :]).then_inc(sSO[ir % 2], 16)
                g.wait_ge(sSO[0], 16 * ((ru0 + 32 + 1) // 2))
                g.wait_ge(sSO[1], 16 * ((ru0 + 32) // 2))

            @b2.tensor
            def _(t):
                if s == 0:
                    t.wait_ge(sWL, 16 * 88)
                for hg in range(2):
                    for idx, (kind, pi, b, tx) in enumerate(proj_sched(hg)):
                        ip = ip0 + hg * 96 + idx
                        rot_wait(t, ip)
                        blocks = L0A_META[b]
                        dst = (pA if ip % 2 == 0 else pB)
                        if kind in ("q", "k"):
                            for n_, (ib, ro) in enumerate(blocks):
                                t.matmul(dst[:, :],
                                         wqk[:, wcol(pi, ro):wcol(pi, ro) + 128],
                                         xn[:, ib * NT + tx * TT: ib * NT + tx * TT + TT],
                                         start=(n_ == 0), stop=(n_ == len(blocks) - 1)
                                         ).then_inc(sMP, 1 if n_ == len(blocks) - 1 else 0)
                        else:
                            for n_, (ib, ro) in enumerate(blocks):
                                t.matmul(dst[:, 0:128],
                                         xn[:, ib * NT + tx * 128: ib * NT + tx * 128 + 128],
                                         wqk[:, wcol(2, ro):wcol(2, ro) + 128],
                                         start=(n_ == 0), stop=(n_ == len(blocks) - 1)
                                         ).then_inc(sMP, 1 if n_ == len(blocks) - 1 else 0)
                    for hl in range(8):
                        base = nq_base(s, hg, hl)
                        for tt in range(4):
                            t.wait_ge(sNQ, base + 4 * tt + 1)
                            t.matmul(pC[0:1, 0:TT], C.ones_col[0:64, 0:1], qsq[:, :],
                                     start=True, stop=True).then_inc(sNQ, 1)
                        for tt in range(4):
                            t.wait_ge(sNQ, base + 18 + 3 * tt + 1)
                            t.matmul(pC[0:1, 0:TT], C.ones_col[0:64, 0:1], qsq[:, :],
                                     start=True, stop=True).then_inc(sNQ, 1)
                    achunks = []
                    for hl in range(8):
                        for i in range(16):
                            io = io_of(s, hg, hl, i)
                            lst = L0A_CHUNKS[i]
                            for n_, (j, mcol) in enumerate(lst):
                                ia = ia_base(s, hg, hl, i) + n_
                                achunks.append(dict(hl=hl, i=i, j=j, ia=ia, io=io,
                                                    first=(i == 0 and n_ == 0),
                                                    start=(n_ == 0),
                                                    stop=(n_ == len(lst) - 1)))

                    def emit_S(ch):
                        hl, ia = ch["hl"], ch["ia"]
                        hcol = hl * NT
                        if ch["first"]:
                            t.wait_ge(sNQ, nq_base(s, hg, hl) + NQ_HEAD)
                            t.wait_ge(sKO, s + 1)
                        if ia >= 2:
                            t.wait_ge(sEXP, ia - 1)
                        t.matmul(psS[ia % 2][0:128, 0:128],
                                 Ke[0:65, hcol + ch["j"] * 128: hcol + ch["j"] * 128 + 128],
                                 Qe[0:65, hcol + ch["i"] * 128: hcol + ch["i"] * 128 + 128],
                                 start=True, stop=True).then_inc(sMS2, 1)

                    def emit_PV(ch):
                        hl, ia, io = ch["hl"], ch["ia"], ch["io"]
                        t.wait_ge(sPM, ia + 1)
                        if ch["start"] and io >= 2:
                            t.wait_ge(sEV, io - 1)
                            t.wait_ge(sNR, io - 1)
                        t.matmul(psO[io % 2][0:65, 0:128],
                                 vt[:, (ch["j"] * 8 + hl) * 65: (ch["j"] * 8 + hl) * 65 + 65],
                                 pbf[ia % 2][:, :],
                                 start=ch["start"], stop=ch["stop"]).then_inc(sMV, 1)

                    def emit_MR(io):
                        t.wait_ge(sNR, io + 1)
                        t.wait_ge(sEB, io)
                        t.matmul(pC[0:64, TT:TT + 128], C.ones_rowF[0:1, 0:64],
                                 rcp[:, :], start=True, stop=True).then_inc(sMR, 1)

                    aseq_ = achunks + [None, None]
                    for k_, ch in enumerate(aseq_):
                        if ch is not None:
                            emit_S(ch)
                        if k_ >= 1 and aseq_[k_ - 1] is not None:
                            emit_PV(aseq_[k_ - 1])
                        if k_ >= 2 and aseq_[k_ - 2] is not None and aseq_[k_ - 2]["stop"]:
                            emit_MR(aseq_[k_ - 2]["io"])
                for irx in range(32):
                    b, tt = irx // 4, irx % 4
                    ip = ip0 + 192 + irx
                    ir = ru0 + irx
                    if irx == 0:
                        t.wait_ge(sEvP, evp0 + EVP_SEQ)
                        t.wait_ge(sOD, 2 * (s + 1) * IO_SEQ)
                    rot_wait(t, ip)
                    dst = (pA if ip % 2 == 0 else pB)
                    blocks = L0A_META[b]
                    for n_, (ib, ro) in enumerate(blocks):
                        t.matmul(dst[:, :],
                                 wqk[:, wcol(3, ro):wcol(3, ro) + 128],
                                 obuf[:, ib * NT + tt * TT: ib * NT + tt * TT + TT],
                                 start=(n_ == 0), stop=(n_ == len(blocks) - 1)
                                 ).then_inc(sMP, 1 if n_ == len(blocks) - 1 else 0)

            @b2.scalar
            def _(sc):
                for hg in range(2):
                    for hl in range(8):
                        base = nq_base(s, hg, hl)
                        hcol = hl * NT
                        if hl == 0:
                            sc.wait_ge(sEvP, evp0 + EVP_HG * (hg + 1))
                        for tt in range(4):
                            sc.wait_ge(sNQ, base + 4 * tt)
                            sc.activation(qsq[:, :], Ke[0:64, hcol + tt * TT: hcol + tt * TT + TT],
                                          AF.Square).then_inc(sNQ, 1)
                        sc.wait_ge(sNQ, base + 16)
                        sc.activation(kmax[:, 1:2], kmax[:, 0:1], AF.Sqrt).then_inc(sNQ, 1)
                        for tt in range(4):
                            sc.wait_ge(sNQ, base + 18 + 3 * tt)
                            sc.activation(qsq[:, :], Qe[0:64, hcol + tt * TT: hcol + tt * TT + TT],
                                          AF.Square).then_inc(sNQ, 1)
                            sc.wait_ge(sNQ, base + 18 + 3 * tt + 2)
                            sc.activation(Qe[64:65, hcol + tt * TT: hcol + tt * TT + TT],
                                          pC[0:1, 0:TT], AF.Sqrt).then_inc(sNQ, 1)
                    aios = []
                    for hl in range(8):
                        for i in range(16):
                            aios.append((hl, i, io_of(s, hg, hl, i),
                                         ia_base(s, hg, hl, i), len(L0A_CHUNKS[i])))

                    def sc_exptile(pio_):
                        sc.wait_ge(sMR, pio_ + 1)
                        if pio_ >= 2:
                            sc.wait_ge(sOD, 2 * (pio_ - 2) + 1)
                        sc.activation(tmpR[pio_ % 2][0:64, 0:128], pC[0:64, TT:TT + 128],
                                      AF.Exp, scale=C.cc[0:64, 6:7]).then_inc(sEB, 1)

                    for k_, (hl, i, io, ia0, nch) in enumerate(aios):
                        for n_ in range(nch):
                            ia = ia0 + n_
                            sc.wait_ge(sMS2, ia + 1)
                            if ia >= 2:
                                sc.wait_ge(sMV, ia - 1)
                            sc.activation(pbf[ia % 2][:, :], psS[ia % 2][0:128, 0:128],
                                          AF.Exp, scale=C.cc[:, 5:6]).then_inc(sEXP, 1)
                        if k_ >= 1:
                            sc_exptile(aios[k_ - 1][2])
                        sc.wait_ge(sMV, ia0 + nch)
                        sc.wait_ge(sMR, io)
                        sc.activation(rcp[:, :], psO[io % 2][64:65, 0:128],
                                      AF.Ln).then_inc(sNR, 1)
                    sc_exptile(aios[-1][2])

            @b2.vector
            def _(v):
                if s == 0:
                    v.memset(Ke[64:65, :], 1.0).then_inc(sKO, 1)
                else:
                    v.wait_ge(sMS2, S_SEQ)
                    v.engine_nop().then_inc(sKO, 1)
                for hg in range(2):
                    for idx, (kind, pi, b, tx) in enumerate(proj_sched(hg)):
                        ip = ip0 + hg * 96 + idx
                        src = (pA if ip % 2 == 0 else pB)
                        v.wait_ge(sMP, ip + 1)
                        if idx == 0 and (s > 0 or hg > 0):
                            v.wait_ge(sMS2, s * S_SEQ + hg * L0A_NCH_HG)
                        bl = b - 4 * hg
                        if kind == "q":
                            v.tensor_scalar(Qe[0:64, (2 * bl) * NT + tx * TT:(2 * bl) * NT + tx * TT + TT],
                                            src[0:64, :], sm_rows(C, "l0a_qb", b, 0), None, ALU.add)
                            v.tensor_scalar(Qe[0:64, (2 * bl + 1) * NT + tx * TT:(2 * bl + 1) * NT + tx * TT + TT],
                                            src[64:128, :], sm_rows(C, "l0a_qb", b, 64), None,
                                            ALU.add).then_inc(sEvP, 1)
                        elif kind == "k":
                            v.tensor_scalar(Ke[0:64, (2 * bl) * NT + tx * TT:(2 * bl) * NT + tx * TT + TT],
                                            src[0:64, :], sm_rows(C, "l0a_kb", b, 0), None, ALU.add)
                            v.tensor_scalar(Ke[0:64, (2 * bl + 1) * NT + tx * TT:(2 * bl + 1) * NT + tx * TT + TT],
                                            src[64:128, :], sm_rows(C, "l0a_kb", b, 64), None,
                                            ALU.add).then_inc(sEvP, 1)
                        else:
                            tc = tx
                            if bl == 0 and tc == 0:
                                if s > 0 or hg > 0:
                                    v.wait_ge(sMV, s * S_SEQ + hg * L0A_NCH_HG)
                                v.memset(vt[:, :], 1.0).then_inc(sVT, 1)
                                v.wait_ge(sVT, s * 2 + hg + 1)
                            v.tensor_copy(vt[:, (tc * 8 + 2 * bl) * 65:(tc * 8 + 2 * bl) * 65 + 64],
                                          src[0:128, 0:64])
                            v.tensor_copy(vt[:, (tc * 8 + 2 * bl + 1) * 65:(tc * 8 + 2 * bl + 1) * 65 + 64],
                                          src[0:128, 64:128]).then_inc(sEvP, 1)
                    for hl in range(8):
                        base = nq_base(s, hg, hl)
                        hcol = hl * NT
                        for tt in range(4):
                            v.wait_ge(sNQ, base + 4 * tt + 2)
                            v.reduce_max(kmax[:, 3:4], pC[0:1, 0:TT], axis=AX.X).then_inc(sNQ, 1)
                            v.wait_ge(sNQ, base + 4 * tt + 3)
                            if tt == 0:
                                v.tensor_copy(kmax[:, 0:1], kmax[:, 3:4]).then_inc(sNQ, 1)
                            else:
                                v.tensor_tensor(kmax[:, 0:1], kmax[:, 0:1], kmax[:, 3:4],
                                                ALU.max).then_inc(sNQ, 1)
                        v.wait_ge(sNQ, base + 17)
                        v.tensor_scalar(kmax[:, 2:3], kmax[:, 1:2], C.cc[0:1, 6:7], None,
                                        ALU.mult).then_inc(sNQ, 1)
                        for tt in range(4):
                            v.wait_ge(sNQ, base + 30 + tt)
                            v.tensor_scalar(Qe[64:65, hcol + tt * TT: hcol + tt * TT + TT],
                                            Qe[64:65, hcol + tt * TT: hcol + tt * TT + TT],
                                            kmax[0:1, 2:3], None, ALU.mult).then_inc(sNQ, 1)
                    avios = []
                    for hl in range(8):
                        for i in range(16):
                            avios.append((hg * 8 + hl, i, io_of(s, hg, hl, i),
                                          ia_base(s, hg, hl, i) + len(L0A_CHUNKS[i])))

                    def a_evac(el):
                        io = el[2]
                        v.wait_ge(sMV, el[3])
                        v.tensor_copy(sct[io % 2][:, :],
                                      psO[io % 2][0:64, 0:128]).then_inc(sEV, 1)

                    def a_mult(el):
                        h, i, io = el[0], el[1], el[2]
                        v.wait_ge(sEB, io + 1)
                        v.wait_ge(sEV, io + 1)
                        if io >= 1:
                            v.wait_ge(sOD, 2 * io)
                        v.tensor_tensor(t1b[:, :], sct[io % 2][:, :], tmpR[io % 2][0:64, 0:128],
                                        ALU.mult).then_inc(sOD, 1)
                        v.wait_ge(sOD, 2 * io + 1)
                        v.tensor_scalar(obuf[(h % 2) * 64:(h % 2) * 64 + 64,
                                             (h // 2) * NT + i * 128:(h // 2) * NT + i * 128 + 128],
                                        t1b[:, :], sm_rows(C, "l0a_vb", h // 2, (h % 2) * 64),
                                        None, ALU.add).then_inc(sOD, 1)

                    for k_, el in enumerate(avios):
                        a_evac(el)
                        if k_ >= 1:
                            a_mult(avios[k_ - 1])
                    a_mult(avios[-1])
                for irx in range(32):
                    b, tt = irx // 4, irx % 4
                    ip = ip0 + 192 + irx
                    ir = ru0 + irx
                    src = (pA if ip % 2 == 0 else pB)
                    v.wait_ge(sMP, ip + 1)
                    v.wait_ge(sRE[ir % 2], 16 * (ir // 2 + 1))
                    if ir >= 2:
                        v.wait_ge(sSO[ir % 2], 16 * (ir // 2))
                    v.scalar_tensor_tensor(tmpR[ir % 2][:, :], src[:, :],
                                           sm(C, "l0a_ob", b), rbufE[ir % 2][:, :],
                                           ALU.add, ALU.add).then_inc(sRU, 1)
        P.pop()
    P.pop()


def emit_l1attn(P, C, dr, tag):
    nc = P.nc
    P.push()
    R_in, R_out = dr["R_in"], dr["R_out"]
    caw, cpw_d, cvw_d, gw1_d, gw2_d = dr["bcaw"], dr["bcpw"], dr["bcvw"], dr["bgw1"], dr["bgw2"]

    NT = T
    NHG, HPG = 4, 4
    xn = P.sb(f"{tag}_xn", [128, NCH * NT], BF16)
    Qe = P.sb(f"{tag}_qe", [65, HPG * NT], BF16)
    Ke = P.sb(f"{tag}_ke", [65, HPG * NT], BF16)
    vf = P.sb(f"{tag}_vf", [64, HPG * NT], BF16)
    vt65 = P.sb(f"{tag}_vt", [128, 16 * HPG * 65], BF16)
    kce = P.sb(f"{tag}_kce", [65, HPG * 127], BF16)
    vcp65 = P.sb(f"{tag}_vcp", [127, HPG * 65], BF16)
    cvwb = P.sb(f"{tag}_cvw", [64, 32 * 64], BF16)
    gw2b = P.sb(f"{tag}_gw2", [128, 16], BF16)
    wbuf = [P.sb(f"{tag}_wb{i}", [128, 2048], BF16) for i in range(2)]
    kmax = P.sb(f"{tag}_kmx", [1, 4], F32)
    qsq = P.sb(f"{tag}_qsq", [64, TT], BF16)
    sqc = P.sb(f"{tag}_sqc", [64, 127], BF16)
    co = P.sb(f"{tag}_co", [64, 127], BF16)
    xhh = P.sb(f"{tag}_xhh", [64, 2 * 127], F32)
    vtmp = P.sb(f"{tag}_vtp", [64, 127], BF16)
    gsc = P.sb(f"{tag}_gsc", [1, 64], F32)
    lnsc = P.sb(f"{tag}_lns", [1, 4 * TT], F32)
    sql = P.sb(f"{tag}_sql", [128, 2 * TT], BF16)
    lnx = P.sb(f"{tag}_lnx", [128, 2 * TT], F32)
    pexp = [P.sb(f"{tag}_px{i}", [128, TT], BF16) for i in range(2)]
    rcp = P.sb(f"{tag}_rcp", [1, TT], F32)
    sct = [P.sb(f"{tag}_sct{i}", [64, TT], F32) for i in range(2)]
    gy = P.sb(f"{tag}_gy", [128, 8], F32)
    gyb = P.sb(f"{tag}_gyb", [128, 8], BF16)
    gy2 = P.sb(f"{tag}_gy2", [128, 8], BF16)
    gcol = P.sb(f"{tag}_gcl", [128, 4], F32)
    tmpR = [P.sb(f"{tag}_tr{i}", [128, TT], F32) for i in range(2)]
    rbufE = [P.sb(f"{tag}_rB{i}", [128, TT], F32) for i in range(2)]

    pA = P.ps(f"{tag}_pA", [128, TT])
    pB = P.ps(f"{tag}_pB", [128, TT])
    pC = P.ps(f"{tag}_pC", [128, 1024])
    psS = [P.ps(f"{tag}_pS0", [128, TT]), P.ps(f"{tag}_pS1", [128, TT])]
    psO = [P.ps(f"{tag}_pO0", [128, TT]), P.ps(f"{tag}_pO1", [128, TT])]
    pH = psS[0]

    sWL = P.sem(f"{tag}_WL"); sWG = [P.sem(f"{tag}_WG0"), P.sem(f"{tag}_WG1")]
    sMP = P.sem(f"{tag}_MP"); sEvP = P.sem(f"{tag}_EvP")
    sTR = P.sem(f"{tag}_TR"); sCV = P.sem(f"{tag}_CV"); sNQ = P.sem(f"{tag}_NQ")
    sMS2 = P.sem(f"{tag}_MS2"); sEXP = P.sem(f"{tag}_EXP"); sPM = P.sem(f"{tag}_PM")
    sMV = P.sem(f"{tag}_MV"); sOD = P.sem(f"{tag}_OD"); sMR = P.sem(f"{tag}_MR")
    sEV = P.sem(f"{tag}_EVc"); sNR = P.sem(f"{tag}_NR"); sEB = P.sem(f"{tag}_EBx")
    sLL = P.sem(f"{tag}_LL"); sGA = P.sem(f"{tag}_GA"); sCB = P.sem(f"{tag}_CB"); sCB2 = P.sem(f"{tag}_CB2")
    sVT = P.sem(f"{tag}_VT"); sKO = P.sem(f"{tag}_KO")
    sRE = [P.sem(f"{tag}_RE0"), P.sem(f"{tag}_RE1")]
    sSO = [P.sem(f"{tag}_SO0"), P.sem(f"{tag}_SO1")]
    sRU = P.sem(f"{tag}_RU")

    IP_SEQ = 136
    EVP_SEQ = 104
    NQ_HEAD = 38
    NQ_SEQ = NQ_HEAD * 16
    S_HG = HPG * 44
    S_SEQ = NHG * S_HG
    IO_HG = HPG * 8
    IO_SEQ = NHG * IO_HG
    CV_K, CV_V = 17, 19
    CV_HEAD = CV_K + CV_V
    CV_HG = HPG * CV_HEAD
    CV_SEQ = NHG * CV_HG
    TR_HG = 16 * HPG * 2
    TR_SEQ = NHG * TR_HG
    LL_TILE = 50
    GA_STEPS = 31
    LL0 = {0: 0, 1: 4 * 50}
    GA0 = {0: 0, 1: 31}
    MASKED = {}
    _cum = 0
    for _s in range(2):
        for _hg in range(NHG):
            for _hl in range(HPG):
                for _qb in range(4):
                    for _j in range(4 * _qb + 4):
                        _ia = _s * S_SEQ + _hg * S_HG + _hl * 44 + [0, 4, 12, 24][_qb] + _j
                        if _j >= 4 * _qb:
                            _cum += 1
                            MASKED[_ia] = _cum
                for _qb in range(4):
                    _ia = _s * S_SEQ + _hg * S_HG + _hl * 44 + 40 + _qb
                    if _qb == 0:
                        _cum += 1
                        MASKED[_ia] = _cum

    def wg_list(s):
        gl = []
        for hg in range(NHG):
            for ki in range(3):
                for sub in range(2):
                    oc = ki * 8 + hg * 2 + sub
                    done = s * IP_SEQ + hg * 24 + ki * 8 + sub * 4 + 4
                    gl.append(("qkv", hg, ki, sub, oc, done))
        for m in range(8):
            gl.append(("gw1", m, 0, 0, m, s * IP_SEQ + 96 + m + 1))
        for m in range(8):
            gl.append(("cpj", m, 0, 0, m, s * IP_SEQ + 104 + m * 4 + 4))
        return gl

    NWG = 40
    WGV = {}
    _pt = [0, 0]
    for _s in range(2):
        for _g, _w in enumerate(wg_list(_s)):
            _gg = _s * NWG + _g
            _nd = 16 if _w[0] == "gw1" else 8
            _pt[_gg % 2] += 16 * _nd
            WGV[_gg] = _pt[_gg % 2]

    def rot_guard(eng, ip):
        p = ip - 2
        if p < 0:
            return
        sq, loc = p // IP_SEQ, p % IP_SEQ
        if loc < EVP_SEQ:
            eng.wait_ge(sEvP, sq * EVP_SEQ + loc + 1)
        else:
            eng.wait_ge(sRU, sq * 32 + (loc - EVP_SEQ) + 1)

    def cv_base(s, hg, hl, kv):
        return s * CV_SEQ + hg * CV_HG + hl * CV_HEAD + (0 if kv == 0 else CV_K)

    def nq_base(s, hg, hl):
        return s * NQ_SEQ + (hg * HPG + hl) * NQ_HEAD

    def ia_base(s, hg, hl, qb, comp):
        b = s * S_SEQ + hg * S_HG + hl * 44
        if not comp:
            return b + [0, 4, 12, 24][qb]
        return b + 40 + qb

    def io_of(s, hg, hl, qb, comp):
        return s * IO_SEQ + hg * IO_HG + hl * 8 + (0 if not comp else 4) + qb

    for s in range(NSEQ):
        ts = s * NT
        ip0 = s * IP_SEQ
        evp0 = s * EVP_SEQ
        ru0 = s * 32

        P.push()
        L = LNEmit(P, C, f"{tag}{s}", R_in, 4,
                   lambda a, c: c * NTOK + ts + a * TT,
                   lambda a, c: xn[:, c * NT + a * TT: c * NT + a * TT + TT],
                   ps3=(pA, pB, pC))
        with nc.Block() as b1:
            @b1.gpsimd
            def _(g):
                for a in range(4):
                    L.gpsimd(g, a)

            @b1.tensor
            def _(t):
                for a in range(4):
                    L.pe(t, a)

            @b1.scalar
            def _(sc):
                for a in range(4):
                    L.act(sc, a)

            @b1.vector
            def _(v):
                for a in range(4):
                    L.dve(v, a)
        P.pop()

        P.push()
        lbuf = P.sb(f"{tag}_lb{s}", [128, NCH * NT], BF16)
        cbuf = P.sb(f"{tag}_cb{s}", [128, NCH * NT], BF16)
        wgl = wg_list(s)

        with nc.Block() as b2:
            @b2.sync
            def _(sy):
                if s == 0:
                    for tap in range(32):
                        sy.dma_start(cvwb[:, tap * 64:(tap + 1) * 64],
                                     cvw_d[tap * 64:tap * 64 + 64, :]).then_inc(sWL, 16)
                    for c in range(8):
                        sy.dma_start(gw2b[:, c * 2:(c + 1) * 2],
                                     gw2_d[c * 128:(c + 1) * 128, :]).then_inc(sWL, 16)
                for gg, (kind, aa, ki, sub, oc, done) in enumerate(wgl):
                    gg_g = s * NWG + gg
                    if gg_g >= 2:
                        pg = gg_g - 2
                        psq, ploc = pg // NWG, pg % NWG
                        sy.wait_ge(sMP, wg_list(psq)[ploc][5])
                    nk = 8 if kind in ("qkv", "cpj") else 16
                    for k in range(nk):
                        if kind == "qkv":
                            src = caw[k * 128:(k + 1) * 128, oc * 128:(oc + 1) * 128]
                        elif kind == "gw1":
                            src = gw1_d[k * 128:(k + 1) * 128, aa * 128:(aa + 1) * 128]
                        else:
                            src = cpw_d[k * 128:(k + 1) * 128, aa * 128:(aa + 1) * 128]
                        sy.dma_start(wbuf[gg_g % 2][:, k * 128:(k + 1) * 128],
                                     src).then_inc(sWG[gg_g % 2], 16)

            @b2.gpsimd
            def _(g):
                for hg in range(NHG):
                    for hl in range(HPG):
                        for qb in range(4):
                            for n_ in range(4 * qb + 4):
                                ia = ia_base(s, hg, hl, qb, False) + n_
                                if ia in MASKED:
                                    g.wait_ge(sEXP, ia + 1)
                                    r = n_ - 4 * qb
                                    g.tensor_tensor(pexp[ia % 2][:, :], pexp[ia % 2][:, :],
                                                    C.masks[:, 512 + r * 512: 512 + r * 512 + TT],
                                                    ALU.mult).then_inc(sPM, 1)
                        for qb in range(4):
                            ia = ia_base(s, hg, hl, qb, True)
                            if ia in MASKED:
                                g.wait_ge(sEXP, ia + 1)
                                g.tensor_tensor(pexp[ia % 2][0:127, 0:TT], pexp[ia % 2][0:127, 0:TT],
                                                C.masks[0:127, 512:512 + TT], ALU.mult).then_inc(sPM, 1)
                for ir_ in range(32):
                    b, tt = ir_ // 4, ir_ % 4
                    ir = ru0 + ir_
                    if ir >= 2:
                        g.wait_ge(sRU, ir - 1)
                    g.dma_start(rbufE[ir % 2][:, :],
                                R_in[:, b * NTOK + ts + tt * TT: b * NTOK + ts + tt * TT + TT]
                                ).then_inc(sRE[ir % 2], 16)
                    if ir_ >= 1:
                        pr = ir - 1
                        pb_, pt_ = (pr - ru0) // 4, (pr - ru0) % 4
                        g.wait_ge(sRU, pr + 1)
                        g.dma_start(R_out[:, pb_ * NTOK + ts + pt_ * TT: pb_ * NTOK + ts + pt_ * TT + TT],
                                    tmpR[pr % 2][:, :]).then_inc(sSO[pr % 2], 16)
                ir = ru0 + 31
                g.wait_ge(sRU, ir + 1)
                g.dma_start(R_out[:, 7 * NTOK + ts + 3 * TT: 7 * NTOK + ts + 3 * TT + TT],
                            tmpR[ir % 2][:, :]).then_inc(sSO[ir % 2], 16)
                g.wait_ge(sSO[0], 16 * ((ru0 + 32 + 1) // 2))
                g.wait_ge(sSO[1], 16 * ((ru0 + 32) // 2))

            @b2.tensor
            def _(t):
                if s == 0:
                    t.wait_ge(sWL, 16 * 40)
                for hg in range(NHG):
                    for ki in range(3):
                        for sub in range(2):
                            gg_g = s * NWG + hg * 6 + ki * 2 + sub
                            for tt in range(4):
                                ip = ip0 + hg * 24 + ki * 8 + sub * 4 + tt
                                rot_guard(t, ip)
                                if tt == 0:
                                    t.wait_ge(sWG[gg_g % 2], WGV[gg_g])
                                dst = (pA if ip % 2 == 0 else pB)
                                for k in range(NCH):
                                    t.matmul(dst[:, :],
                                             wbuf[gg_g % 2][:, k * 128:(k + 1) * 128],
                                             xn[:, k * NT + tt * TT: k * NT + tt * TT + TT],
                                             start=(k == 0), stop=(k == 7)
                                             ).then_inc(sMP, 1 if k == 7 else 0)
                    for hl in range(HPG):
                        for tc in range(16):
                            trs = s * TR_SEQ + hg * TR_HG + (hl * 16 + tc) * 2
                            t.wait_ge(sTR, trs)
                            if hl == 0 and tc == 0:
                                t.wait_ge(sEvP, evp0 + hg * 24 + 24)
                                if hg >= 1 or s >= 1:
                                    pio = s * IO_SEQ + hg * IO_HG if hg >= 1 else (s - 1) * IO_SEQ + NHG * IO_HG
                                    t.wait_ge(sEB, pio)
                            t.matmul(pC[0:128, TT:TT + 64],
                                     vf[:, hl * NT + tc * 128: hl * NT + tc * 128 + 128],
                                     C.ident[0:64, 0:64], start=True, stop=True).then_inc(sTR, 1)
                    for hl in range(HPG):
                        for kv in range(2):
                            base = cv_base(s, hg, hl, kv)
                            t.wait_ge(sCV, base)
                            if hl == 0 and kv == 0:
                                t.wait_ge(sNQ, (s * 16 + hg * HPG) * NQ_HEAD)
                                t.wait_ge(sTR, s * TR_SEQ + hg * TR_HG + TR_HG)
                                t.wait_ge(sEXP, s * S_SEQ + hg * S_HG)
                                t.wait_ge(sPM, MASKED.get(s * S_SEQ + hg * S_HG - 1, 0))
                            srcT = Ke if kv == 0 else vf
                            r0 = 0 if kv == 0 else 0
                            for tap in range(32):
                                t.matmul(pH[0:64, 0:127],
                                         cvwb[:, tap * 64:(tap + 1) * 64],
                                         srcT[0:64, hl * NT + tap: hl * NT + tap + 16 * 126 + 1:16],
                                         start=(tap == 0), stop=(tap == 31)
                                         ).then_inc(sCV, 1 if tap == 31 else 0)
                            t.wait_ge(sCV, base + 3)
                            t.matmul(pH[0:1, 128:255], C.ones_col[0:64, 0:1], co[:, :],
                                     start=True, stop=True).then_inc(sCV, 1)
                            t.wait_ge(sCV, base + 5)
                            t.matmul(pH[0:1, 128:255], C.ones_col[0:64, 0:1], sqc[:, :],
                                     start=True, stop=True).then_inc(sCV, 1)
                            t.wait_ge(sCV, base + 12)
                            t.matmul(pH[0:64, 256:383], C.ones_rowF[0:1, 0:64],
                                     lnsc[0:1, 0:127], start=True, stop=True).then_inc(sCV, 1)
                            t.wait_ge(sCV, base + 13)
                            t.matmul(pH[0:64, 384:511], C.ones_rowF[0:1, 0:64],
                                     lnsc[0:1, 127:254], start=True, stop=True).then_inc(sCV, 1)
                            if kv == 1:
                                t.wait_ge(sCV, base + 17)
                                t.matmul(pC[0:127, 0:64], vtmp[:, :], C.ident[0:64, 0:64],
                                         start=True, stop=True).then_inc(sCV, 1)
                    for hl in range(HPG):
                        base = nq_base(s, hg, hl)
                        hcol = hl * NT
                        for tt in range(4):
                            t.wait_ge(sNQ, base + 4 * tt + 1)
                            t.matmul(pC[0:1, 0:TT], C.ones_col[0:64, 0:1], qsq[:, :],
                                     start=True, stop=True).then_inc(sNQ, 1)
                        t.wait_ge(sNQ, base + 17)
                        t.matmul(pC[0:1, 0:127], C.ones_col[0:64, 0:1], sqc[:, :],
                                 start=True, stop=True).then_inc(sNQ, 1)
                        for tt in range(4):
                            t.wait_ge(sNQ, base + 22 + 3 * tt + 1)
                            t.matmul(pC[0:1, 0:TT], C.ones_col[0:64, 0:1], qsq[:, :],
                                     start=True, stop=True).then_inc(sNQ, 1)
                    chunks = []
                    for hl in range(HPG):
                        for qb in range(4):
                            io = io_of(s, hg, hl, qb, False)
                            for n_ in range(4 * qb + 4):
                                ia = ia_base(s, hg, hl, qb, False) + n_
                                chunks.append(dict(hl=hl, qb=qb, comp=False, n_=n_,
                                                   ia=ia, io=io,
                                                   first=(qb == 0 and n_ == 0),
                                                   start=(n_ == 0), stop=(n_ == 4 * qb + 3)))
                        for qb in range(4):
                            io = io_of(s, hg, hl, qb, True)
                            ia = ia_base(s, hg, hl, qb, True)
                            chunks.append(dict(hl=hl, qb=qb, comp=True, n_=0,
                                               ia=ia, io=io, first=False,
                                               start=True, stop=True))

                    def emit_S(ch):
                        hl, ia = ch["hl"], ch["ia"]
                        if ch["first"]:
                            t.wait_ge(sNQ, nq_base(s, hg, hl) + NQ_HEAD)
                            t.wait_ge(sKO, s * NHG + hg + 1)
                        if ia >= 2:
                            t.wait_ge(sEXP, ia - 1)
                        if not ch["comp"]:
                            j = ch["n_"]
                            t.matmul(psS[ia % 2][0:128, 0:TT],
                                     Ke[0:65, hl * NT + j * 128: hl * NT + j * 128 + 128],
                                     Qe[0:65, hl * NT + ch["qb"] * TT: hl * NT + ch["qb"] * TT + TT],
                                     start=True, stop=True).then_inc(sMS2, 1)
                        else:
                            t.matmul(psS[ia % 2][0:127, 0:TT],
                                     kce[0:65, hl * 127: hl * 127 + 127],
                                     Qe[0:65, hl * NT + ch["qb"] * TT: hl * NT + ch["qb"] * TT + TT],
                                     start=True, stop=True).then_inc(sMS2, 1)

                    def emit_PV(ch):
                        hl, ia, io = ch["hl"], ch["ia"], ch["io"]
                        if ia in MASKED:
                            t.wait_ge(sPM, MASKED[ia])
                        else:
                            t.wait_ge(sEXP, ia + 1)
                        if ch["start"] and io >= 2:
                            t.wait_ge(sEV, io - 1)
                            t.wait_ge(sNR, io - 1)
                        if not ch["comp"]:
                            j = ch["n_"]
                            t.matmul(psO[io % 2][0:65, 0:TT],
                                     vt65[:, (j * HPG + hl) * 65: (j * HPG + hl) * 65 + 65],
                                     pexp[ia % 2][:, :],
                                     start=ch["start"], stop=ch["stop"]).then_inc(sMV, 1)
                        else:
                            t.matmul(psO[io % 2][0:65, 0:TT],
                                     vcp65[0:127, hl * 65: hl * 65 + 65],
                                     pexp[ia % 2][0:127, 0:TT],
                                     start=True, stop=True).then_inc(sMV, 1)

                    def emit_MR(io):
                        t.wait_ge(sNR, io + 1)
                        t.wait_ge(sEB, io)
                        t.matmul(pC[0:64, TT:2 * TT], C.ones_rowF[0:1, 0:64],
                                 rcp[:, :], start=True, stop=True).then_inc(sMR, 1)

                    seq_ = chunks + [None, None]
                    for k_, ch in enumerate(seq_):
                        if ch is not None:
                            emit_S(ch)
                        if k_ >= 1 and seq_[k_ - 1] is not None:
                            emit_PV(seq_[k_ - 1])
                        if k_ >= 2 and seq_[k_ - 2] is not None and seq_[k_ - 2]["stop"]:
                            emit_MR(seq_[k_ - 2]["io"])
                for tt in range(4):
                    llb = LL0[s] + tt * 50
                    if tt == 0:
                        t.wait_ge(sOD, (s + 1) * IO_SEQ)
                    t.wait_ge(sLL, llb + 0)
                    for c in range(NCH):
                        t.matmul(pC[0:1, 0:TT], C.ones_col[:, 0:1],
                                 lbuf[:, c * NT + tt * TT: c * NT + tt * TT + TT],
                                 start=(c == 0), stop=(c == 7)).then_inc(sLL, 1 if c == 7 else 0)
                    for c in range(NCH):
                        st = 3 if c == 0 else min(2 * c + 3, 16)
                        t.wait_ge(sLL, llb + st)
                        t.matmul(pC[0:1, TT:2 * TT], C.ones_col[:, 0:1],
                                 sql[:, (c % 2) * TT:(c % 2) * TT + TT],
                                 start=(c == 0), stop=(c == 7)).then_inc(sLL, 1)
                    t.wait_ge(sLL, llb + 24)
                    t.matmul(pC[0:128, 0:TT], C.ones_rowF[:, :],
                             lnsc[0:1, 2 * TT:3 * TT], start=True, stop=True).then_inc(sLL, 1)
                    t.wait_ge(sLL, llb + 25)
                    t.matmul(pC[0:128, TT:2 * TT], C.ones_rowF[:, :],
                             lnsc[0:1, 3 * TT:4 * TT], start=True, stop=True).then_inc(sLL, 1)
                gb = GA0[s]
                for m in range(8):
                    ip = ip0 + 96 + m
                    rot_guard(t, ip)
                    gg_g = s * NWG + 24 + m
                    if m == 0:
                        t.wait_ge(sLL, LL0[s] + 4 * 50)
                        t.wait_ge(sOD, (s + 1) * IO_SEQ)
                    t.wait_ge(sWG[gg_g % 2], WGV[gg_g])
                    dst = (pA if ip % 2 == 0 else pB)
                    for k in range(16):
                        src = lbuf[:, k * NT + NT - 1: k * NT + NT] if k < 8 else \
                              cbuf[:, (k - 8) * NT + NT - 1: (k - 8) * NT + NT]
                        t.matmul(dst[:, 0:1], wbuf[gg_g % 2][:, k * 128:(k + 1) * 128],
                                 src, start=(k == 0), stop=(k == 15)
                                 ).then_inc(sMP, 1 if k == 15 else 0)
                t.wait_ge(sGA, gb + 3)
                t.matmul(pC[0:1, 0:8], C.ones_col[:, 0:1], gyb[:, :],
                         start=True, stop=True).then_inc(sGA, 1)
                t.matmul(pC[0:1, 8:16], C.ones_col[:, 0:1], gy2[:, :],
                         start=True, stop=True).then_inc(sGA, 1)
                t.wait_ge(sGA, gb + 14)
                t.matmul(pC[0:128, 16:17], C.ones_rowF[:, :], gsc[0:1, 4:5],
                         start=True, stop=True).then_inc(sGA, 1)
                t.matmul(pC[0:128, 17:18], C.ones_rowF[:, :], gsc[0:1, 5:6],
                         start=True, stop=True).then_inc(sGA, 1)
                t.wait_ge(sGA, gb + 21)
                for e in range(2):
                    for c in range(8):
                        t.matmul(pC[0:1, 18 + e:19 + e], gw2b[:, c * 2 + e:c * 2 + e + 1],
                                 gyb[:, c:c + 1], start=(c == 0), stop=(c == 7)
                                 ).then_inc(sGA, 1 if c == 7 else 0)
                t.wait_ge(sGA, gb + 28)
                t.matmul(pC[0:128, 20:21], C.ones_rowF[:, :], gsc[0:1, 8:9],
                         start=True, stop=True).then_inc(sGA, 1)
                t.matmul(pC[0:128, 21:22], C.ones_rowF[:, :], gsc[0:1, 9:10],
                         start=True, stop=True).then_inc(sGA, 1)
                for irx in range(32):
                    b, tt = irx // 4, irx % 4
                    ip = ip0 + 104 + irx
                    ir = ru0 + irx
                    gg_g = s * NWG + 32 + b
                    rot_guard(t, ip)
                    if irx == 0:
                        t.wait_ge(sCB, 32 * (s + 1))
                    if tt == 0:
                        t.wait_ge(sWG[gg_g % 2], WGV[gg_g])
                    dst = (pA if ip % 2 == 0 else pB)
                    for k in range(NCH):
                        t.matmul(dst[:, :], wbuf[gg_g % 2][:, k * 128:(k + 1) * 128],
                                 lbuf[:, k * NT + tt * TT: k * NT + tt * TT + TT],
                                 start=(k == 0), stop=(k == 7)
                                 ).then_inc(sMP, 1 if k == 7 else 0)

            @b2.scalar
            def _(sc):
                for hg in range(NHG):
                    for hl in range(HPG):
                        for kv in range(2):
                            base = cv_base(s, hg, hl, kv)
                            sc.wait_ge(sCV, base + 2)
                            sc.activation(sqc[:, :], co[:, :], AF.Square).then_inc(sCV, 1)
                            sc.wait_ge(sCV, base + 9)
                            sc.activation(lnsc[0:1, 381:508], lnsc[0:1, 381:508],
                                          AF.Ln, bias=C.cc[0:1, 1:2]).then_inc(sCV, 1)
                            sc.wait_ge(sCV, base + 10)
                            sc.activation(lnsc[0:1, 0:127], lnsc[0:1, 381:508],
                                          AF.Exp, scale=C.cc[0:1, 2:3]).then_inc(sCV, 1)
                            sc.wait_ge(sCV, base + 16)
                            gdst = kce[0:64, hl * 127: hl * 127 + 127] if kv == 0 else vtmp[:, :]
                            sc.activation(gdst, xhh[:, 127:254], AF.Gelu,
                                          scale=sm_rows(C, "l1a_clw", 0, 0),
                                          bias=sm_rows(C, "l1a_clb", 0, 0)).then_inc(sCV, 1)
                    for hl in range(HPG):
                        base = nq_base(s, hg, hl)
                        hcol = hl * NT
                        if hl == 0:
                            sc.wait_ge(sCV, s * CV_SEQ + (hg + 1) * CV_HG)
                            sc.wait_ge(sEvP, evp0 + hg * 24 + 24)
                        for tt in range(4):
                            sc.wait_ge(sNQ, base + 4 * tt)
                            sc.activation(qsq[:, :], Ke[0:64, hcol + tt * TT: hcol + tt * TT + TT],
                                          AF.Square).then_inc(sNQ, 1)
                        sc.wait_ge(sNQ, base + 16)
                        sc.activation(sqc[:, :], kce[0:64, hl * 127: hl * 127 + 127],
                                      AF.Square).then_inc(sNQ, 1)
                        sc.wait_ge(sNQ, base + 20)
                        sc.activation(kmax[:, 1:2], kmax[:, 0:1], AF.Sqrt).then_inc(sNQ, 1)
                        for tt in range(4):
                            sc.wait_ge(sNQ, base + 22 + 3 * tt)
                            sc.activation(qsq[:, :], Qe[0:64, hcol + tt * TT: hcol + tt * TT + TT],
                                          AF.Square).then_inc(sNQ, 1)
                            sc.wait_ge(sNQ, base + 22 + 3 * tt + 2)
                            sc.activation(Qe[64:65, hcol + tt * TT: hcol + tt * TT + TT],
                                          pC[0:1, 0:TT], AF.Sqrt).then_inc(sNQ, 1)
                    sios = []
                    for hl in range(HPG):
                        for cp_ in (False, True):
                            for qb in range(4):
                                sios.append((hl, qb, cp_, io_of(s, hg, hl, qb, cp_),
                                             ia_base(s, hg, hl, qb, cp_),
                                             1 if cp_ else 4 * qb + 4))

                    def sc_exptile(pio_):
                        sc.wait_ge(sMR, pio_ + 1)
                        if pio_ >= 2:
                            sc.wait_ge(sOD, pio_ - 1)
                        sc.activation(tmpR[pio_ % 2][0:64, 0:TT], pC[0:64, TT:2 * TT],
                                      AF.Exp, scale=C.cc[0:64, 6:7]).then_inc(sEB, 1)

                    for k_, (hl, qb, cp_, io, ia0, nch) in enumerate(sios):
                        for n_ in range(nch):
                            ia = ia0 + n_
                            sc.wait_ge(sMS2, ia + 1)
                            if ia >= 2:
                                sc.wait_ge(sMV, ia - 1)
                            if not cp_:
                                sc.activation(pexp[ia % 2][:, :], psS[ia % 2][0:128, 0:TT],
                                              AF.Exp, scale=C.cc[:, 5:6]).then_inc(sEXP, 1)
                            else:
                                sc.activation(pexp[ia % 2][0:127, 0:TT], psS[ia % 2][0:127, 0:TT],
                                              AF.Exp, scale=C.cc[0:127, 5:6]).then_inc(sEXP, 1)
                        if k_ >= 1:
                            sc_exptile(sios[k_ - 1][3])
                        sc.wait_ge(sMV, ia0 + nch)
                        sc.wait_ge(sMR, io)
                        sc.activation(rcp[:, :], psO[io % 2][64:65, 0:TT],
                                      AF.Ln).then_inc(sNR, 1)
                    sc_exptile(sios[-1][3])
                for tt in range(4):
                    llb = LL0[s] + tt * 50
                    for c in range(NCH):
                        st = (1 + c) if c < 2 else 2 * c
                        sc.wait_ge(sLL, llb + st)
                        sc.activation(sql[:, (c % 2) * TT:(c % 2) * TT + TT],
                                      lbuf[:, c * NT + tt * TT: c * NT + tt * TT + TT],
                                      AF.Square).then_inc(sLL, 1)
                    sc.wait_ge(sLL, llb + 21)
                    sc.activation(lnsc[0:1, TT:2 * TT], lnsc[0:1, TT:2 * TT],
                                  AF.Ln, bias=C.cc[0:1, 1:2]).then_inc(sLL, 1)
                    sc.wait_ge(sLL, llb + 22)
                    sc.activation(lnsc[0:1, 2 * TT:3 * TT], lnsc[0:1, TT:2 * TT],
                                  AF.Exp, scale=C.cc[0:1, 2:3]).then_inc(sLL, 1)
                gb = GA0[s]
                sc.wait_ge(sGA, gb + 2)
                sc.activation(gy2[:, :], gyb[:, :], AF.Square).then_inc(sGA, 1)
                sc.wait_ge(sGA, gb + 11)
                sc.activation(gsc[0:1, 1:2], gsc[0:1, 1:2], AF.Ln,
                              bias=C.cc[0:1, 1:2]).then_inc(sGA, 1)
                sc.wait_ge(sGA, gb + 12)
                sc.activation(gsc[0:1, 4:5], gsc[0:1, 1:2], AF.Exp,
                              scale=C.cc[0:1, 2:3]).then_inc(sGA, 1)
                sc.wait_ge(sGA, gb + 20)
                sc.activation(gyb[:, :], gy[:, :], AF.Relu).then_inc(sGA, 1)
                sc.wait_ge(sGA, gb + 26)
                sc.activation(gsc[0:1, 8:9], gsc[0:1, 10:11], AF.Sigmoid).then_inc(sGA, 1)

            @b2.vector
            def _(v):
                for hg in range(NHG):
                    for ki in range(3):
                        for sub in range(2):
                            for tt in range(4):
                                ip = ip0 + hg * 24 + ki * 8 + sub * 4 + tt
                                v.wait_ge(sMP, ip + 1)
                                src = (pA if ip % 2 == 0 else pB)
                                oc2 = hg * 2 + sub
                                if ki == 0:
                                    if (sub, tt) == (0, 0) and (s > 0 or hg > 0):
                                        v.wait_ge(sMS2, s * S_SEQ + hg * S_HG)
                                    for half in range(2):
                                        hd = sub * 2 + half
                                        ins = v.tensor_scalar(
                                            Qe[0:64, hd * NT + tt * TT: hd * NT + tt * TT + TT],
                                            src[half * 64:half * 64 + 64, :],
                                            sm_rows(C, "l1a_cab", oc2, half * 64), None, ALU.add)
                                    ins.then_inc(sEvP, 1)
                                elif ki == 1:
                                    for half in range(2):
                                        hd = sub * 2 + half
                                        ins = v.tensor_scalar(
                                            Ke[0:64, hd * NT + tt * TT: hd * NT + tt * TT + TT],
                                            src[half * 64:half * 64 + 64, :],
                                            sm_rows(C, "l1a_cab", 8 + oc2, half * 64), None, ALU.add)
                                    ins.then_inc(sEvP, 1)
                                else:
                                    for half in range(2):
                                        hd = sub * 2 + half
                                        ins = v.tensor_scalar(
                                            vf[0:64, hd * NT + tt * TT: hd * NT + tt * TT + TT],
                                            src[half * 64:half * 64 + 64, :],
                                            sm_rows(C, "l1a_cab", 16 + oc2, half * 64), None, ALU.add)
                                    ins.then_inc(sEvP, 1)
                    v.wait_ge(sMV, s * S_SEQ + hg * S_HG)
                    v.memset(vt65[:, :], 1.0).then_inc(sVT, 1)
                    v.memset(Ke[64:65, :], 1.0)
                    v.memset(kce[64:65, :], 1.0)
                    v.memset(vcp65[:, :], 1.0).then_inc(sKO, 1)
                    v.wait_ge(sVT, s * NHG + hg + 1)
                    for hl in range(HPG):
                        for tc in range(16):
                            trs = s * TR_SEQ + hg * TR_HG + (hl * 16 + tc) * 2
                            v.wait_ge(sTR, trs + 1)
                            v.tensor_copy(vt65[:, (tc * HPG + hl) * 65: (tc * HPG + hl) * 65 + 64],
                                          pC[0:128, TT:TT + 64]).then_inc(sTR, 1)
                    for hl in range(HPG):
                        for kv in range(2):
                            base = cv_base(s, hg, hl, kv)
                            v.wait_ge(sCV, base + 1)
                            v.tensor_scalar(co[:, :], pH[0:64, 0:127],
                                            sm_rows(C, "l1a_cvb", 0, 0), None, ALU.add).then_inc(sCV, 1)
                            v.wait_ge(sCV, base + 4)
                            v.tensor_scalar(lnsc[0:1, 254:381], pH[0:1, 128:255],
                                            C.cc[0:1, 4:5], None, ALU.mult).then_inc(sCV, 1)
                            v.wait_ge(sCV, base + 6)
                            v.tensor_scalar(lnsc[0:1, 381:508], pH[0:1, 128:255],
                                            C.cc[0:1, 4:5], None, ALU.mult).then_inc(sCV, 1)
                            v.wait_ge(sCV, base + 7)
                            v.tensor_tensor(lnsc[0:1, 508:635], lnsc[0:1, 254:381],
                                            lnsc[0:1, 254:381], ALU.mult).then_inc(sCV, 1)
                            v.wait_ge(sCV, base + 8)
                            v.tensor_tensor(lnsc[0:1, 381:508], lnsc[0:1, 381:508],
                                            lnsc[0:1, 508:635], ALU.subtract).then_inc(sCV, 1)
                            v.wait_ge(sCV, base + 11)
                            v.tensor_tensor(lnsc[0:1, 127:254], lnsc[0:1, 254:381],
                                            lnsc[0:1, 0:127], ALU.mult).then_inc(sCV, 1)
                            v.wait_ge(sCV, base + 14)
                            v.tensor_tensor(xhh[:, 0:127], co[:, :], pH[0:64, 256:383],
                                            ALU.mult).then_inc(sCV, 1)
                            v.wait_ge(sCV, base + 15)
                            v.tensor_tensor(xhh[:, 127:254], xhh[:, 0:127], pH[0:64, 384:511],
                                            ALU.subtract).then_inc(sCV, 1)
                            if kv == 1:
                                v.wait_ge(sCV, base + 18)
                                v.tensor_copy(vcp65[0:127, hl * 65: hl * 65 + 64],
                                              pC[0:127, 0:64]).then_inc(sCV, 1)
                    for hl in range(HPG):
                        base = nq_base(s, hg, hl)
                        hcol = hl * NT
                        for tt in range(4):
                            v.wait_ge(sNQ, base + 4 * tt + 2)
                            v.reduce_max(kmax[:, 3:4], pC[0:1, 0:TT], axis=AX.X).then_inc(sNQ, 1)
                            v.wait_ge(sNQ, base + 4 * tt + 3)
                            if tt == 0:
                                v.tensor_copy(kmax[:, 0:1], kmax[:, 3:4]).then_inc(sNQ, 1)
                            else:
                                v.tensor_tensor(kmax[:, 0:1], kmax[:, 0:1], kmax[:, 3:4],
                                                ALU.max).then_inc(sNQ, 1)
                        v.wait_ge(sNQ, base + 18)
                        v.reduce_max(kmax[:, 3:4], pC[0:1, 0:127], axis=AX.X).then_inc(sNQ, 1)
                        v.wait_ge(sNQ, base + 19)
                        v.tensor_tensor(kmax[:, 0:1], kmax[:, 0:1], kmax[:, 3:4],
                                        ALU.max).then_inc(sNQ, 1)
                        v.wait_ge(sNQ, base + 21)
                        v.tensor_scalar(kmax[:, 2:3], kmax[:, 1:2], C.cc[0:1, 6:7], None,
                                        ALU.mult).then_inc(sNQ, 1)
                        for tt in range(4):
                            v.wait_ge(sNQ, base + 34 + tt)
                            v.tensor_scalar(Qe[64:65, hcol + tt * TT: hcol + tt * TT + TT],
                                            Qe[64:65, hcol + tt * TT: hcol + tt * TT + TT],
                                            kmax[0:1, 2:3], None, ALU.mult).then_inc(sNQ, 1)
                    vios = []
                    for hl in range(HPG):
                        h = hg * HPG + hl
                        for cp_ in (False, True):
                            for qb in range(4):
                                vios.append(dict(
                                    h=h, qb=qb, comp=cp_,
                                    io=io_of(s, hg, hl, qb, cp_),
                                    cd=ia_base(s, hg, hl, qb, cp_) + (1 if cp_ else 4 * qb + 4)))

                    def v_evac(el):
                        io = el["io"]
                        v.wait_ge(sMV, el["cd"])
                        v.tensor_copy(sct[io % 2][:, :],
                                      psO[io % 2][0:64, 0:TT]).then_inc(sEV, 1)

                    def v_mult(el):
                        h, qb, io = el["h"], el["qb"], el["io"]
                        dstb = cbuf if el["comp"] else lbuf
                        v.wait_ge(sEB, io + 1)
                        v.wait_ge(sEV, io + 1)
                        v.tensor_tensor(dstb[(h % 2) * 64:(h % 2) * 64 + 64,
                                             (h // 2) * NT + qb * TT:(h // 2) * NT + qb * TT + TT],
                                        sct[io % 2][:, :], tmpR[io % 2][0:64, 0:TT],
                                        ALU.mult).then_inc(sOD, 1)

                    for k_, el in enumerate(vios):
                        v_evac(el)
                        if k_ >= 1:
                            v_mult(vios[k_ - 1])
                    v_mult(vios[-1])
                for tt in range(4):
                    llb = LL0[s] + tt * 50
                    v.wait_ge(sLL, llb + 17)
                    v.tensor_scalar(lnsc[0:1, 0:TT], pC[0:1, 0:TT], C.cc[0:1, 0:1],
                                    None, ALU.mult).then_inc(sLL, 1)
                    v.wait_ge(sLL, llb + 18)
                    v.tensor_scalar(lnsc[0:1, TT:2 * TT], pC[0:1, TT:2 * TT], C.cc[0:1, 0:1],
                                    None, ALU.mult).then_inc(sLL, 1)
                    v.wait_ge(sLL, llb + 19)
                    v.tensor_tensor(lnsc[0:1, 3 * TT:4 * TT], lnsc[0:1, 0:TT],
                                    lnsc[0:1, 0:TT], ALU.mult).then_inc(sLL, 1)
                    v.wait_ge(sLL, llb + 20)
                    v.tensor_tensor(lnsc[0:1, TT:2 * TT], lnsc[0:1, TT:2 * TT],
                                    lnsc[0:1, 3 * TT:4 * TT], ALU.subtract).then_inc(sLL, 1)
                    v.wait_ge(sLL, llb + 23)
                    v.tensor_tensor(lnsc[0:1, 3 * TT:4 * TT], lnsc[0:1, 0:TT],
                                    lnsc[0:1, 2 * TT:3 * TT], ALU.mult).then_inc(sLL, 1)
                    for c in range(NCH):
                        v.wait_ge(sLL, llb + 26 + 3 * c)
                        v.tensor_tensor(lnx[:, 0:TT], lbuf[:, c * NT + tt * TT: c * NT + tt * TT + TT],
                                        pC[0:128, 0:TT], ALU.mult).then_inc(sLL, 1)
                        v.wait_ge(sLL, llb + 27 + 3 * c)
                        v.tensor_tensor(lnx[:, TT:2 * TT], lnx[:, 0:TT],
                                        pC[0:128, TT:2 * TT], ALU.subtract).then_inc(sLL, 1)
                        v.wait_ge(sLL, llb + 28 + 3 * c)
                        v.tensor_scalar(lbuf[:, c * NT + tt * TT: c * NT + tt * TT + TT],
                                        lnx[:, TT:2 * TT], sm(C, "l1a_lnw", c), sm(C, "l1a_lnb", c),
                                        ALU.mult, ALU.add).then_inc(sLL, 1)
                gb = GA0[s]
                for m in range(8):
                    ip = ip0 + 96 + m
                    v.wait_ge(sMP, ip + 1)
                    src = (pA if ip % 2 == 0 else pB)
                    v.tensor_copy(gy[:, m:m + 1], src[:, 0:1]).then_inc(sEvP, 1)
                v.wait_ge(sEvP, evp0 + EVP_SEQ)
                v.tensor_tensor(gy[:, :], gy[:, :], sm(C, "l1a_gb1", 0, 8), ALU.add).then_inc(sGA, 1)
                v.wait_ge(sGA, gb + 1)
                v.tensor_copy(gyb[:, :], gy[:, :]).then_inc(sGA, 1)
                v.wait_ge(sGA, gb + 5)
                v.reduce_sum(gsc[0:1, 16:17], pC[0:1, 0:8], axis=AX.X).then_inc(sGA, 1)
                v.wait_ge(sGA, gb + 6)
                v.tensor_scalar(gsc[0:1, 0:1], gsc[0:1, 16:17], C.cc[0:1, 0:1], None,
                                ALU.mult).then_inc(sGA, 1)
                v.wait_ge(sGA, gb + 7)
                v.reduce_sum(gsc[0:1, 17:18], pC[0:1, 8:16], axis=AX.X).then_inc(sGA, 1)
                v.wait_ge(sGA, gb + 8)
                v.tensor_scalar(gsc[0:1, 1:2], gsc[0:1, 17:18], C.cc[0:1, 0:1], None,
                                ALU.mult).then_inc(sGA, 1)
                v.wait_ge(sGA, gb + 9)
                v.tensor_tensor(gsc[0:1, 2:3], gsc[0:1, 0:1], gsc[0:1, 0:1],
                                ALU.mult).then_inc(sGA, 1)
                v.wait_ge(sGA, gb + 10)
                v.tensor_tensor(gsc[0:1, 1:2], gsc[0:1, 1:2], gsc[0:1, 2:3],
                                ALU.subtract).then_inc(sGA, 1)
                v.wait_ge(sGA, gb + 13)
                v.tensor_tensor(gsc[0:1, 5:6], gsc[0:1, 0:1], gsc[0:1, 4:5],
                                ALU.mult).then_inc(sGA, 1)
                v.wait_ge(sGA, gb + 16)
                v.tensor_copy(gcol[:, 0:2], pC[0:128, 16:18]).then_inc(sGA, 1)
                v.wait_ge(sGA, gb + 17)
                v.tensor_scalar(gy[:, :], gy[:, :], gcol[:, 0:1], gcol[:, 1:2],
                                ALU.mult, ALU.subtract).then_inc(sGA, 1)
                v.wait_ge(sGA, gb + 18)
                v.tensor_tensor(gy[:, :], gy[:, :], sm(C, "l1a_glw", 0, 8), ALU.mult).then_inc(sGA, 1)
                v.wait_ge(sGA, gb + 19)
                v.tensor_tensor(gy[:, :], gy[:, :], sm(C, "l1a_glb", 0, 8), ALU.add).then_inc(sGA, 1)
                v.wait_ge(sGA, gb + 23)
                v.tensor_copy(gsc[0:1, 8:10], pC[0:1, 18:20]).then_inc(sGA, 1)
                v.wait_ge(sGA, gb + 24)
                v.tensor_tensor(gsc[0:1, 8:10], gsc[0:1, 8:10],
                                C.smalls[0:1, SM_COLS["l1a_gb2"][0]:SM_COLS["l1a_gb2"][0] + 2],
                                ALU.add).then_inc(sGA, 1)
                v.wait_ge(sGA, gb + 25)
                v.tensor_tensor(gsc[0:1, 10:11], gsc[0:1, 8:9], gsc[0:1, 9:10],
                                ALU.subtract).then_inc(sGA, 1)
                v.wait_ge(sGA, gb + 27)
                v.tensor_scalar(gsc[0:1, 9:10], gsc[0:1, 8:9], C.cc[0:1, 6:7], C.cc[0:1, 7:8],
                                ALU.mult, ALU.add).then_inc(sGA, 1)
                v.wait_ge(sGA, gb + 30)
                v.tensor_copy(gcol[:, 2:4], pC[0:128, 20:22]).then_inc(sGA, 1)
                for ctile in range(32):
                    c, tt = ctile // 4, ctile % 4
                    cb = s * 32 + ctile
                    if ctile == 0:
                        v.wait_ge(sGA, gb + 31)
                    if cb >= 2:
                        v.wait_ge(sCB, cb - 1)
                    v.tensor_scalar(lnx[:, (ctile % 2) * TT:(ctile % 2) * TT + TT],
                                    lbuf[:, c * NT + tt * TT: c * NT + tt * TT + TT],
                                    gcol[:, 2:3], None, ALU.mult).then_inc(sCB2, 1)
                    v.wait_ge(sCB2, cb + 1)
                    v.scalar_tensor_tensor(lbuf[:, c * NT + tt * TT: c * NT + tt * TT + TT],
                                           cbuf[:, c * NT + tt * TT: c * NT + tt * TT + TT],
                                           gcol[:, 3:4],
                                           lnx[:, (ctile % 2) * TT:(ctile % 2) * TT + TT],
                                           ALU.mult, ALU.add).then_inc(sCB, 1)
                for irx in range(32):
                    b, tt = irx // 4, irx % 4
                    ip = ip0 + 104 + irx
                    ir = ru0 + irx
                    src = (pA if ip % 2 == 0 else pB)
                    v.wait_ge(sMP, ip + 1)
                    v.wait_ge(sRE[ir % 2], 16 * (ir // 2 + 1))
                    if ir >= 2:
                        v.wait_ge(sSO[ir % 2], 16 * (ir // 2))
                    v.scalar_tensor_tensor(tmpR[ir % 2][:, :], src[:, :],
                                           sm(C, "l1a_cpb", b), rbufE[ir % 2][:, :],
                                           ALU.add, ALU.add).then_inc(sRU, 1)
        P.pop()
    P.pop()


WNAMES = ("l0f_w1T", "l0f_w2T", "l0f_h1T", "l0f_h2T", "l0f_w3T",
          "l0a_qw", "l0a_kw", "l0a_vw", "l0a_ow",
          "l1a_caw", "l1a_cpw", "l1a_cvw", "l1a_gw1", "l1a_gw2",
          "l1m_w1", "l1m_w2", "l1m_w3", "l1m_wg")


def build_nc(smalls_shape, wshapes, masks_shape):
    nc = bass.Bass("TRN2", target_bir_lowering=False, debug=False)
    dr = {}
    dr["smalls"] = nc.dram_tensor("smalls", list(smalls_shape), F32, kind="ExternalInput").ap()
    dr["ident"] = nc.dram_tensor("ident", [128, 128], BF16, kind="ExternalInput").ap()
    dr["masks"] = nc.dram_tensor("masks", list(masks_shape), BF16, kind="ExternalInput").ap()
    dr["identF"] = nc.dram_tensor("identF", [128, 128], F32, kind="ExternalInput").ap()
    dr["SEL"] = nc.dram_tensor("SEL", [8, 1024], F32, kind="ExternalInput").ap()
    dr["x_in"] = nc.dram_tensor("x_in", [128, NCH * NTOK], F32, kind="ExternalInput").ap()
    dr["y_out"] = nc.dram_tensor("y_out", [128, NCH * NTOK], F32, kind="ExternalOutput").ap()
    dr["R"] = nc.dram_tensor("R", [128, NCH * NTOK], F32, kind="Internal").ap()
    for n in WNAMES:
        dr[n] = nc.dram_tensor(n, list(wshapes[n]), F32, kind="ExternalInput").ap()
        dr["b_" + n] = nc.dram_tensor("b_" + n, list(wshapes[n]), BF16, kind="Internal").ap()

    with ExitStack() as ctx:
        P = Prog(nc, ctx)
        C = emit_setup(P, dr, smalls_shape[1])
        emit_cast(P, [(dr[n], dr["b_" + n]) for n in WNAMES if n.startswith("l0")])
        ctiles = []
        for n in WNAMES:
            if n.startswith("l0"):
                continue
            src, dst = dr[n], dr["b_" + n]
            rows, cols = src.tensor.shape
            for r0 in range(0, rows, 128):
                for c0 in range(0, cols, 512):
                    ctiles.append((src, dst, r0, c0, min(512, cols - c0)))
        emit_l0ffn(P, C, {
            "R_in": dr["x_in"], "R_out": dr["R"],
            "bw1T": dr["b_l0f_w1T"], "bw2T": dr["b_l0f_w2T"],
            "bh1T": dr["b_l0f_h1T"], "bh2T": dr["b_l0f_h2T"], "bw3T": dr["b_l0f_w3T"],
        }, tag="f0")
        emit_l0attn(P, C, {
            "R_in": dr["R"], "R_out": dr["R"],
            "bqw": dr["b_l0a_qw"], "bkw": dr["b_l0a_kw"],
            "bvw": dr["b_l0a_vw"], "bow": dr["b_l0a_ow"],
            "cast_tiles": ctiles,
        }, tag="a0")
        emit_l0ffn(P, C, {
            "R_in": dr["R"], "R_out": dr["R"],
            "bw1T": dr["b_l0f_w1T"], "bw2T": dr["b_l0f_w2T"],
            "bh1T": dr["b_l0f_h1T"], "bh2T": dr["b_l0f_h2T"], "bw3T": dr["b_l0f_w3T"],
        }, tag="f1")
        emit_moe_dense(P, C, {
            "R_in": dr["R"], "R_out": dr["R"],
            "bw1": dr["b_l1m_w1"], "bw2": dr["b_l1m_w2"], "bw3": dr["b_l1m_w3"],
            "bwg": dr["b_l1m_wg"], "identF": dr["identF"], "SEL": dr["SEL"],
        }, tag="m0")
        emit_l1attn(P, C, {
            "R_in": dr["R"], "R_out": dr["R"],
            "bcaw": dr["b_l1a_caw"], "bcpw": dr["b_l1a_cpw"], "bcvw": dr["b_l1a_cvw"],
            "bgw1": dr["b_l1a_gw1"], "bgw2": dr["b_l1a_gw2"],
        }, tag="x1")
        emit_moe_dense(P, C, {
            "R_in": dr["R"], "R_out": dr["y_out"],
            "bw1": dr["b_l1m_w1"], "bw2": dr["b_l1m_w2"], "bw3": dr["b_l1m_w3"],
            "bwg": dr["b_l1m_wg"], "identF": dr["identF"], "SEL": dr["SEL"],
        }, tag="m1")
    return nc


def kernel(**inputs):
    smalls = pack_smalls(inputs)
    weights = prep_weights(inputs)
    ident, masks = make_consts()
    identF = np.eye(128, dtype=np.float32)
    SEL = np.zeros((8, 8 * 128), np.float32)
    for e in range(8):
        SEL[e, e * 128:(e + 1) * 128] = 1.0
    x = np.asarray(inputs["L0_attn_x"], np.float32)

    wshapes = {n: weights[n].shape for n in WNAMES}
    nc = build_nc(smalls.shape, wshapes, masks.shape)

    in_maps = []
    for core in range(NCORE):
        im = {"smalls": smalls, "ident": ident, "masks": masks,
              "identF": identF, "SEL": SEL,
              "x_in": x_to_fm(x[2 * core:2 * core + 2])}
        for n in WNAMES:
            im[n] = weights[n]
        in_maps.append(im)

    res = run_bass_kernel_spmd(nc, in_maps, core_ids=list(range(NCORE)))
    out = np.concatenate([fm_to_x(np.asarray(res.results[i]["y_out"]))
                          for i in range(NCORE)], axis=0)
    return out
